# version 04 (final)
# baseline (speedup 1.0000x reference)
import numpy as np
from contextlib import ExitStack
import concourse.bass as bass
import concourse.mybir as mybir
from concourse.bass import IndirectOffsetOnAxis
from concourse.bass_utils import run_bass_kernel_spmd
from concourse.aot_env import allow_ici_all_to_all

F32 = mybir.dt.float32
I32 = mybir.dt.int32
AF = mybir.ActivationFunctionType
ALU = mybir.AluOpType
AX = mybir.AxisListType

B, T, D = 2, 4096, 2048
N_CORES = 8
NTOK = 1024
DT_ = 16
NCH = 512
PROJ = 5248
LN_EPS = 1e-5
L0_EPS = 1e-5
MOD_EPS = 1e-6
TCB = 256
TCC = 256
NMID = 44
KPAD = 1536
NS = 32
ROPE_C1 = float(np.float32(2 * np.pi))
ROPE_C2 = float(np.float64(2 * np.pi) - np.float32(2 * np.pi))

G8 = [list(range(8))]


def fm(a2d):
    return np.ascontiguousarray(np.asarray(a2d).T.astype(np.float32, copy=False))


def colv(vec, nt=None):
    v = np.asarray(vec).astype(np.float32, copy=False)
    n = v.shape[0] // 128 if nt is None else nt
    return np.ascontiguousarray(v.reshape(n, 128).T)


class SP:

    def __init__(self, nc):
        self.nc = nc
        self.ops = []

    def ld(self, out, in_):
        self.ops.append(("sync", (lambda o=out, i=in_: self.nc.sync.dma_start(out=o, in_=i)), 16))

    def st(self, out, in_):
        self.ops.append(("gpsimd", (lambda o=out, i=in_: self.nc.gpsimd.dma_start(out=o, in_=i)), 16))

    def v(self, fn):
        self.ops.append(("vector", fn, 1))

    def s(self, fn):
        self.ops.append(("scalar", fn, 1))

    def t(self, fn):
        self.ops.append(("tensor", fn, 1))

    def g(self, fn, inc=1):
        self.ops.append(("gpsimd", fn, inc))

    def build(self, sem):
        pre = []
        tot = 0
        for (_, _, inc) in self.ops:
            pre.append(tot)
            tot += inc
        nc = self.nc
        ops = self.ops
        with nc.Block() as block:
            def mk(name):
                def body(eng):
                    for i, (e, f, inc) in enumerate(ops):
                        if e != name:
                            continue
                        if pre[i] > 0:
                            eng.wait_ge(sem, pre[i])
                        f().then_inc(sem, inc)
                    if name == "gpsimd":
                        eng.wait_ge(sem, tot)
                return body
            block.sync(mk("sync"))
            block.vector(mk("vector"))
            block.scalar(mk("scalar"))
            block.tensor(mk("tensor"))
            block.gpsimd(mk("gpsimd"))


class Arena:
    def __init__(self, ap, lo, hi):
        self.ap = ap
        self.lo = lo
        self.hi = hi
        self.off = lo

    def al(self, w):
        a = self.ap[:, self.off:self.off + w]
        self.off += w
        assert self.off <= self.hi, (self.off, self.hi)
        return a

    def reset(self):
        self.off = self.lo


def band_ld(sp, nc, wtile, W, m, nk, parts=2):
    step = nk // parts
    for p0 in range(0, nk, step):
        sp.ld(wtile[:, p0 * 128:(p0 + step) * 128],
              W[p0 * 128:(p0 + step) * 128, m * 128:(m + 1) * 128]
              .rearrange("(k p) j -> p k j", p=128))


def build_fused():
    nc = bass.Bass("TRN2", target_bir_lowering=False, debug=False)
    Xin = nc.dram_tensor("x", [D, NTOK], F32, kind="ExternalInput").ap()
    Win = nc.dram_tensor("w_in", [D, PROJ], F32, kind="ExternalInput").ap()
    Wdt = nc.dram_tensor("w_dt", [128, D], F32, kind="ExternalInput").ap()
    Bdt = nc.dram_tensor("b_dt", [128, DT_], F32, kind="ExternalInput").ap()
    G1v = nc.dram_tensor("g1", [128, DT_], F32, kind="ExternalInput").ap()
    WR = nc.dram_tensor("wr", [128, DT_], F32, kind="ExternalInput").ap()
    Ial = nc.dram_tensor("alog_sl", [128, 64], F32, kind="ExternalInput").ap()
    Idp = nc.dram_tensor("dp_sl", [128, 2], F32, kind="ExternalInput").ap()
    Wo0 = nc.dram_tensor("w_out", [D, D], F32, kind="ExternalInput").ap()
    Wg0 = nc.dram_tensor("w_gate", [D, 5632], F32, kind="ExternalInput").ap()
    Wu0 = nc.dram_tensor("w_up", [D, 5632], F32, kind="ExternalInput").ap()
    Wd0 = nc.dram_tensor("w_down", [5632, D], F32, kind="ExternalInput").ap()
    G2v = nc.dram_tensor("g2", [128, DT_], F32, kind="ExternalInput").ap()
    Ipostm = nc.dram_tensor("postm", [128, 64], F32, kind="ExternalInput").ap()
    Igrowtm = nc.dram_tensor("growtm", [128, 64], F32, kind="ExternalInput").ap()
    Iiota = nc.dram_tensor("iotar", [1, NTOK], F32, kind="ExternalInput").ap()
    Icofs = nc.dram_tensor("cofs", [128, 1], F32, kind="ExternalInput").ap()
    Ioneh = nc.dram_tensor("onehot", [1, 8], F32, kind="ExternalInput").ap()
    Imyslot = nc.dram_tensor("myslot", [128, 1], I32, kind="ExternalInput").ap()
    Imyrows = nc.dram_tensor("myrows", [8, 1], I32, kind="ExternalInput").ap()
    WqE = nc.dram_tensor("wq", [D, D], F32, kind="ExternalInput").ap()
    WkE = nc.dram_tensor("wk", [D, 1024], F32, kind="ExternalInput").ap()
    WvE = nc.dram_tensor("wv", [D, 1024], F32, kind="ExternalInput").ap()
    WoE = nc.dram_tensor("wo", [D, D], F32, kind="ExternalInput").ap()
    WgE = nc.dram_tensor("wg", [D, 8192], F32, kind="ExternalInput").ap()
    WuE = nc.dram_tensor("wu", [D, 8192], F32, kind="ExternalInput").ap()
    WdE = nc.dram_tensor("wd", [8192, D], F32, kind="ExternalInput").ap()
    Vln1 = nc.dram_tensor("ln1", [128, DT_], F32, kind="ExternalInput").ap()
    Vln2 = nc.dram_tensor("ln2", [128, DT_], F32, kind="ExternalInput").ap()
    VbqE = nc.dram_tensor("bqE", [128, DT_], F32, kind="ExternalInput").ap()
    VbkE = nc.dram_tensor("bkE", [128, 8], F32, kind="ExternalInput").ap()
    VbvE = nc.dram_tensor("bvE", [128, 8], F32, kind="ExternalInput").ap()
    Icm = nc.dram_tensor("cmask", [128, 8 * 128], F32, kind="ExternalInput").ap()
    Wq1 = nc.dram_tensor("wqt", [D, D], F32, kind="ExternalInput").ap()
    Wk1 = nc.dram_tensor("wkt", [D, D], F32, kind="ExternalInput").ap()
    Wv1 = nc.dram_tensor("wvt", [D, D], F32, kind="ExternalInput").ap()
    Wo1 = nc.dram_tensor("wot", [D, D], F32, kind="ExternalInput").ap()
    Vbq1 = nc.dram_tensor("bq1", [128, DT_], F32, kind="ExternalInput").ap()
    Vbk1 = nc.dram_tensor("bk1", [128, DT_], F32, kind="ExternalInput").ap()
    Vbv1 = nc.dram_tensor("bv1", [128, DT_], F32, kind="ExternalInput").ap()
    Vbo1 = nc.dram_tensor("bo1", [128, DT_], F32, kind="ExternalInput").ap()
    Iqp = nc.dram_tensor("qpos", [128, 8], F32, kind="ExternalInput").ap()
    Ikp = nc.dram_tensor("kpos", [128, 12], F32, kind="ExternalInput").ap()
    Ikm = nc.dram_tensor("kmask", [128, 12], F32, kind="ExternalInput").ap()
    Ikmr = nc.dram_tensor("kmaskr", [1, KPAD], F32, kind="ExternalInput").ap()
    Ismask = nc.dram_tensor("smask", [128, 1], F32, kind="ExternalInput").ap()
    Wqg = nc.dram_tensor("wqg", [D, 5632], I32, kind="ExternalInput").ap()
    Wqu = nc.dram_tensor("wqu", [D, 5632], I32, kind="ExternalInput").ap()
    Wqd = nc.dram_tensor("wqd", [5632, D], I32, kind="ExternalInput").ap()
    Sg = nc.dram_tensor("sg", [1, 5632], F32, kind="ExternalInput").ap()
    Su = nc.dram_tensor("su", [1, 5632], F32, kind="ExternalInput").ap()
    Sd = nc.dram_tensor("sd", [1, D], F32, kind="ExternalInput").ap()
    Oout = nc.dram_tensor("out", [D, NTOK], F32, kind="ExternalOutput").ap()
    HBxn0 = nc.dram_tensor("hb_xn0", [D, NTOK], F32).ap()
    HBgb = nc.dram_tensor("hb_gb", [D, NTOK], F32).ap()
    HBz = nc.dram_tensor("hb_z", [D, NTOK], F32).ap()
    XN0TM = nc.dram_tensor("xn0_tm", [1025, D], F32).ap()
    A2AI = nc.dram_tensor("a2a_in", [8 * 640, NTOK], F32).ap()
    A2AO = nc.dram_tensor("a2a_out", [8 * 640, NTOK], F32).ap()
    SCI = nc.dram_tensor("sc_in", [1, NTOK], F32).ap()
    SCO = nc.dram_tensor("sc_out", [8, NTOK], F32).ap()
    YI = nc.dram_tensor("y_in", [8 * 256, NTOK], F32).ap()
    YO = nc.dram_tensor("y_out", [8 * 256, NTOK], F32).ap()
    CANS = nc.dram_tensor("canon_slot", [2, T], F32).ap()
    CANSC = nc.dram_tensor("canon_sc", [2, T], F32).ap()
    BYSLOT = nc.dram_tensor("byslot", [4, NTOK], F32).ap()
    SELP = nc.dram_tensor("selpart", [NTOK, D], F32).ap()
    SELTM = nc.dram_tensor("sel_tm", [NTOK, D], F32).ap()
    GATI = nc.dram_tensor("gat_in", [128, D], F32).ap()
    GATO = nc.dram_tensor("gat_out", [NTOK, D], F32).ap()
    HBs0 = nc.dram_tensor("hb_s0", [D, NTOK], F32).ap()
    KEYI = nc.dram_tensor("keys_in", [D, 384], F32).ap()
    KEYO = nc.dram_tensor("keys_out", [8 * D, 384], F32).ap()
    HBx1 = nc.dram_tensor("hb_x1", [D, NTOK], F32).ap()
    HBxn1 = nc.dram_tensor("hb_xn1", [D, NTOK], F32).ap()
    HBs1 = nc.dram_tensor("hb_s1", [D, NTOK], F32).ap()
    Kscr = nc.dram_tensor("kscr", [D, KPAD], F32).ap()
    Vscr = nc.dram_tensor("vscr", [KPAD, D], F32).ap()

    sp = SP(nc)
    with ExitStack() as es:
        E = es.enter_context
        AR_t = E(nc.sbuf_tensor("AR", [128, 49152], F32))
        VR = E(nc.sbuf_tensor("VR", [1, 1352], F32))
        MR = E(nc.sbuf_tensor("MR", [8, 1152], F32))
        IR = E(nc.sbuf_tensor("IR", [128, 1152], I32))
        P0 = E(nc.psum_tensor("P0", [128, 512], F32))
        P1 = E(nc.psum_tensor("P1", [128, 512], F32))
        P2 = E(nc.psum_tensor("P2", [128, 512], F32))
        P3 = E(nc.psum_tensor("P3", [128, 512], F32))
        P4 = E(nc.psum_tensor("P4", [128, 512], F32))
        PR = E(nc.psum_tensor("PR", [1, 512], F32))
        PR2 = E(nc.psum_tensor("PR2", [1, 512], F32))
        sem = E(nc.semaphore("gs"))

        AR = Arena(AR_t, 0, 48600)
        PERS = Arena(AR_t, 48600, 49152)
        idf = PERS.al(128)
        ones128 = PERS.al(1)
        csb = PERS.al(8)
        posk8 = PERS.al(8)
        grow8 = PERS.al(8)
        sco8 = PERS.al(8)
        slotf = PERS.al(8)
        posq = PERS.al(1)
        t8a = PERS.al(8)
        t8b = PERS.al(8)
        t8c = PERS.al(8)
        ones1 = VR[:, 1096:1224]
        scqrow = VR[:, 1224:1352]
        wti = IR[:, 0:1024]
        pii = IR[:, 1024:1025]

        fiiI = IR[:, 0:128]
        sp.g(lambda: nc.gpsimd.iota(pii[:], pattern=[[0, 1]], base=0, channel_multiplier=1))
        sp.g(lambda: nc.gpsimd.iota(fiiI[:], pattern=[[1, 128]], base=0, channel_multiplier=0))
        tmpf = AR.al(128)
        sp.v(lambda: nc.vector.tensor_copy(tmpf[:, 0:1], pii[:]))
        sp.v(lambda: nc.vector.tensor_copy(idf[:], fiiI[:]))
        sp.v(lambda: nc.vector.tensor_scalar(idf[:], idf[:], tmpf[:, 0:1], None, ALU.is_equal))
        sp.v(lambda: nc.vector.memset(ones128[:], 1.0))
        sp.v(lambda: nc.vector.memset(ones1[:], 1.0))
        sp.v(lambda: nc.vector.memset(csb[:, 0:1], LN_EPS))
        sp.v(lambda: nc.vector.memset(csb[:, 1:2], 1.0))
        sp.v(lambda: nc.vector.memset(csb[:, 2:3], -1.0))
        sp.v(lambda: nc.vector.memset(csb[:, 3:4], MOD_EPS))
        sp.v(lambda: nc.vector.memset(csb[:, 4:5], -np.log(10000.0) / 64.0))
        AR.reset()

        selidxI = IR[:, 0:8]
        gidxI = IR[:, 8:16]
        myslotI = IR[:, 16:17]
        myrowsI = IR[0:8, 17:18]
        nintI = IR[:, 18:82]
        ifriI = IR[0:1, 82:146]

        def ln_stats(src_sb, ntok, nt, s1, s2, sq, base=0):
            nch = max(1, ntok // NCH)
            w = min(ntok, NCH)
            for ch in range(nch):
                for k in range(nt):
                    c0 = base + k * ntok + ch * w
                    sp.t(lambda a=PR, c=src_sb, c0=c0, st=(k == 0), sp_=(k == nt - 1), w=w:
                         nc.tensor.matmul(a[:, 0:w], ones128[:, 0:1], c[:, c0:c0 + w], start=st, stop=sp_))
                sp.v(lambda a=s1, ch=ch, w=w: nc.vector.tensor_copy(a[:, ch * w:(ch + 1) * w], PR[0:1, 0:w]))
                for k in range(nt):
                    c0 = base + k * ntok + ch * w
                    sp.s(lambda a=sq, b=src_sb, c0=c0, w=w: nc.scalar.activation(out=a[:, 0:w], in_=b[:, c0:c0 + w], func=AF.Square))
                    sp.t(lambda a=PR, b=sq, st=(k == 0), sp_=(k == nt - 1), w=w:
                         nc.tensor.matmul(a[:, 0:w], ones128[:, 0:1], b[:, 0:w], start=st, stop=sp_))
                sp.v(lambda a=s2, ch=ch, w=w: nc.vector.tensor_copy(a[:, ch * w:(ch + 1) * w], PR[0:1, 0:w]))

        def ln_stats2(src_sb, ntok, nt, s1, s2, sq, base=0, stride=None):
            if stride is None:
                stride = ntok
            for k in range(nt):
                c0 = base + k * stride
                sp.t(lambda c0=c0, st=(k == 0), sp_=(k == nt - 1):
                     nc.tensor.matmul(PR[0:1, 0:ntok], ones128[:, 0:1], src_sb[:, c0:c0 + ntok], start=st, stop=sp_))
            sp.v(lambda: nc.vector.tensor_copy(s1[:, 0:ntok], PR[0:1, 0:ntok]))
            for k in range(nt):
                c0 = base + k * stride
                sp.s(lambda c0=c0: nc.scalar.activation(out=sq[:, 0:ntok], in_=src_sb[:, c0:c0 + ntok], func=AF.Square))
                sp.t(lambda st=(k == 0), sp_=(k == nt - 1):
                     nc.tensor.matmul(PR[0:1, 0:ntok], ones128[:, 0:1], sq[:, 0:ntok], start=st, stop=sp_))
            sp.v(lambda: nc.vector.tensor_copy(s2[:, 0:ntok], PR[0:1, 0:ntok]))

        def rstd_from_s2(s2, rstd, dinv, eps_ap, ntok):
            sp.v(lambda: nc.vector.tensor_scalar_mul(s2[:, 0:ntok], s2[:, 0:ntok], dinv))
            sp.s(lambda: nc.scalar.activation(out=rstd[:, 0:ntok], in_=s2[:, 0:ntok], func=AF.Sqrt, bias=eps_ap))
            sp.v(lambda: nc.vector.reciprocal(rstd[:, 0:ntok], rstd[:, 0:ntok]))

        def bcast_row(ps, vec, c0, n):
            sp.t(lambda ps=ps, vec=vec, c0=c0, n=n: nc.tensor.matmul(
                ps[:, 0:n], ones1[:, 0:128], vec[:, c0:c0 + n], start=True, stop=True))

        xsb = AR.al(DT_ * NTOK)
        xnsb = AR.al(DT_ * NTOK)
        r1sb = xsb
        drsb = AR.al(NTOK)
        dtwsb = AR.al(D)
        wtileA = AR.al(DT_ * 128)
        tmb = AR.al(D)
        sq_ = AR.al(NCH)
        outsb = AR.al(NCH)
        t1_ = AR.al(NCH)
        t2_ = AR.al(NCH)
        smallv = AR.al(3 * DT_)
        vrowA = AR.al(6 * NTOK)
        S1 = lambda: vrowA[0:1, 0:NTOK]
        S2 = lambda: vrowA[0:1, NTOK:2 * NTOK]
        MU = lambda: vrowA[0:1, 2 * NTOK:3 * NTOK]
        RSTD = lambda: vrowA[0:1, 3 * NTOK:4 * NTOK]
        NMR = lambda: vrowA[0:1, 4 * NTOK:5 * NTOK]
        SCr = lambda: vrowA[0:1, 5 * NTOK:6 * NTOK]

        for k in range(DT_):
            sp.ld(xsb[:, k * NTOK:(k + 1) * NTOK], Xin[k * 128:(k + 1) * 128, :])
        sp.ld(dtwsb[:], Wdt[:])
        sp.ld(smallv[:, 0:DT_], Bdt[:])
        sp.ld(smallv[:, DT_:2 * DT_], G1v[:])
        sp.ld(smallv[:, 2 * DT_:3 * DT_], WR[:])

        ln_stats(xsb, NTOK, DT_, S1(), S2(), sq_)
        dinv = 1.0 / D
        sp.v(lambda: nc.vector.tensor_scalar_mul(MU(), S1(), dinv))
        sp.v(lambda: nc.vector.tensor_mul(S1(), MU(), MU()))
        sp.v(lambda: nc.vector.tensor_scalar_mul(S2(), S2(), dinv))
        sp.v(lambda: nc.vector.tensor_sub(S2(), S2(), S1()))
        sp.s(lambda: nc.scalar.activation(out=RSTD(), in_=S2(), func=AF.Sqrt, bias=csb[0:1, 0:1]))
        sp.v(lambda: nc.vector.reciprocal(RSTD(), RSTD()))
        sp.v(lambda: nc.vector.tensor_mul(NMR(), MU(), RSTD()))
        sp.v(lambda: nc.vector.tensor_scalar_mul(NMR(), NMR(), -1.0))
        for ch in range(NTOK // NCH):
            bcast_row(P1, RSTD(), ch * NCH, NCH)
            bcast_row(P2, NMR(), ch * NCH, NCH)
            for k in range(DT_):
                c0 = k * NTOK + ch * NCH
                sp.v(lambda c0=c0: nc.vector.tensor_mul(t1_[:, 0:NCH], xsb[:, c0:c0 + NCH], P1[:, 0:NCH]))
                sp.v(lambda c0=c0: nc.vector.tensor_add(xnsb[:, c0:c0 + NCH], t1_[:, 0:NCH], P2[:, 0:NCH]))
        ln_stats(xnsb, NTOK, DT_, S1(), S2(), sq_)
        rstd_from_s2(S2(), RSTD(), dinv, csb[0:1, 0:1], NTOK)
        for ch in range(NTOK // NCH):
            bcast_row(P1, RSTD(), ch * NCH, NCH)
            for k in range(DT_):
                c0 = k * NTOK + ch * NCH
                sp.v(lambda c0=c0, k=k: nc.vector.scalar_tensor_tensor(
                    r1sb[:, c0:c0 + NCH], xnsb[:, c0:c0 + NCH], smallv[:, DT_ + k:DT_ + k + 1],
                    P1[:, 0:NCH], ALU.mult, ALU.mult))
        for ch in range(NTOK // NCH):
            for k in range(DT_):
                c0 = k * NTOK + ch * NCH
                sp.t(lambda c0=c0, k=k, st=(k == 0), sp_=(k == DT_ - 1):
                     nc.tensor.matmul(PR[0:1, 0:NCH], smallv[:, 2 * DT_ + k:2 * DT_ + k + 1],
                                      xnsb[:, c0:c0 + NCH], start=st, stop=sp_))
            sp.v(lambda ch=ch: nc.vector.tensor_copy(SCr()[:, ch * NCH:(ch + 1) * NCH], PR[0:1, 0:NCH]))
        sp.st(SCI[:], SCr())
        for k in range(DT_):
            sp.st(HBxn0[k * 128:(k + 1) * 128, :], xnsb[:, k * NTOK:(k + 1) * NTOK])
        for k in range(DT_):
            for ch in range(NTOK // NCH):
                c0 = k * NTOK + ch * NCH
                sp.s(lambda c0=c0: nc.scalar.activation(out=outsb[:, 0:NCH], in_=xnsb[:, c0:c0 + NCH], func=AF.Silu))
                sp.st(HBgb[k * 128:(k + 1) * 128, ch * NCH:(ch + 1) * NCH], outsb[:, 0:NCH])
        for lk in range(8):
            for ft in range(DT_):
                sp.t(lambda lk=lk, ft=ft: nc.tensor.transpose(
                    P4[:, 0:128], xnsb[:, ft * NTOK + lk * 128:ft * NTOK + lk * 128 + 128], idf[:]))
                sp.v(lambda ft=ft: nc.vector.tensor_copy(tmb[:, ft * 128:(ft + 1) * 128], P4[:, 0:128]))
            sp.st(XN0TM[lk * 128:(lk + 1) * 128, :], tmb[:])
        sp.v(lambda: nc.vector.memset(tmb[0:1, :], 0.0))
        sp.st(XN0TM[1024:1025, :], tmb[0:1, 0:D])
        for m in range(PROJ // 128):
            for k in range(DT_):
                sp.ld(wtileA[:, k * 128:(k + 1) * 128], Win[k * 128:(k + 1) * 128, m * 128:(m + 1) * 128])
            for ch in range(NTOK // NCH):
                for k in range(DT_):
                    c0 = k * NTOK + ch * NCH
                    sp.t(lambda c0=c0, k=k, st=(k == 0), sp_=(k == DT_ - 1):
                         nc.tensor.matmul(P0[:, 0:NCH], wtileA[:, k * 128:(k + 1) * 128],
                                          r1sb[:, c0:c0 + NCH], start=st, stop=sp_))
                if m == 40:
                    sp.v(lambda ch=ch: nc.vector.tensor_copy(drsb[:, ch * NCH:(ch + 1) * NCH], P0[:, 0:NCH]))
                else:
                    sp.v(lambda: nc.vector.tensor_copy(outsb[:, 0:NCH], P0[:, 0:NCH]))
                    cs = slice(ch * NCH, (ch + 1) * NCH)
                    if m < 16:
                        sp.st(HBz[m * 128:(m + 1) * 128, cs], outsb[:, 0:NCH])
                    elif m < 20:
                        e0 = 2 * (m - 16)
                        sp.st(A2AI[e0 * 640 + 512:e0 * 640 + 576, cs], outsb[0:64, 0:NCH])
                        sp.st(A2AI[(e0 + 1) * 640 + 512:(e0 + 1) * 640 + 576, cs], outsb[64:128, 0:NCH])
                    elif m < 24:
                        e0 = 2 * (m - 20)
                        sp.st(A2AI[e0 * 640 + 576:e0 * 640 + 640, cs], outsb[0:64, 0:NCH])
                        sp.st(A2AI[(e0 + 1) * 640 + 576:(e0 + 1) * 640 + 640, cs], outsb[64:128, 0:NCH])
                    else:
                        e0 = (m - 24) // 2
                        ro = 256 + 128 * ((m - 24) % 2)
                        sp.st(A2AI[e0 * 640 + ro:e0 * 640 + ro + 128, cs], outsb[:, 0:NCH])
        for m2 in range(DT_):
            e0 = m2 // 2
            ro = 128 * (m2 % 2)
            for ch in range(NTOK // NCH):
                sp.t(lambda m2=m2, ch=ch: nc.tensor.matmul(
                    P0[:, 0:NCH], dtwsb[:, m2 * 128:(m2 + 1) * 128],
                    drsb[:, ch * NCH:(ch + 1) * NCH], start=True, stop=True))
                sp.v(lambda m2=m2: nc.vector.tensor_scalar_add(outsb[:, 0:NCH], P0[:, 0:NCH], smallv[:, m2:m2 + 1]))
                sp.s(lambda: nc.scalar.activation(out=t1_[:, 0:NCH], in_=outsb[:, 0:NCH], func=AF.Abs))
                sp.s(lambda: nc.scalar.activation(out=t1_[:, 0:NCH], in_=t1_[:, 0:NCH], func=AF.Exp, scale=csb[:, 2:3]))
                sp.s(lambda: nc.scalar.activation(out=t1_[:, 0:NCH], in_=t1_[:, 0:NCH], func=AF.Ln, bias=csb[:, 1:2]))
                sp.s(lambda: nc.scalar.activation(out=t2_[:, 0:NCH], in_=outsb[:, 0:NCH], func=AF.Relu))
                sp.v(lambda: nc.vector.tensor_add(t2_[:, 0:NCH], t2_[:, 0:NCH], t1_[:, 0:NCH]))
                sp.st(A2AI[e0 * 640 + ro:e0 * 640 + ro + 128, ch * NCH:(ch + 1) * NCH], t2_[:, 0:NCH])

        sp.g(lambda: nc.gpsimd.collective_compute(
            "AllGather", ALU.bypass, replica_groups=G8, ins=[SCI], outs=[SCO]))
        def _a2a1():
            with allow_ici_all_to_all():
                return nc.gpsimd.collective_compute(
                    "AllToAll", ALU.bypass, replica_groups=G8, ins=[A2AI], outs=[A2AO])
        sp.g(_a2a1)

        AR.reset()
        mda = AR.al(NS * TCB)
        ub = AR.al(NS * TCB)
        hbt = AR.al(NS * TCB)
        cr = AR.al(NS * TCB)
        dtc = AR.al(TCB)
        xrc = AR.al(TCB)
        usb2 = AR.al(TCB)
        ysb2 = AR.al(TCB)
        asb = AR.al(64)
        dpsb = AR.al(2)
        stt = AR.al(NS)
        sp.ld(asb[:], Ial[:])
        sp.ld(dpsb[:], Idp[:])
        sp.s(lambda: nc.scalar.activation(out=asb[:], in_=asb[:], func=AF.Exp))
        sp.v(lambda: nc.vector.tensor_scalar_mul(asb[:], asb[:], -1.0))
        for b_ in range(2):
            for h in range(2):
                A_t = asb[:, h * NS:(h + 1) * NS]
                sp.v(lambda: nc.vector.memset(stt[:], 0.0))
                for cch in range(T // TCB):
                    rs_ = []
                    for q2 in range(2):
                        gg = 2 * cch + q2
                        r = 4 * b_ + gg % 4
                        n = gg // 4
                        rs_.append((r, n))
                        sp.ld(dtc[:, q2 * 128:(q2 + 1) * 128],
                              A2AO[r * 640 + 128 * h:r * 640 + 128 * h + 128, 128 * n:128 * n + 128])
                        sp.ld(xrc[:, q2 * 128:(q2 + 1) * 128],
                              A2AO[r * 640 + 512 + 32 * h:r * 640 + 512 + 32 * h + 32, 128 * n:128 * n + 128]
                              .unsqueeze(0).broadcast_to([4, 32, 128]))
                    sp.v(lambda: nc.vector.tensor_mul(usb2[:], dtc[:], xrc[:]))
                    sp.v(lambda A_t=A_t: nc.vector.tensor_mul(
                        mda[:].rearrange("p (n t) -> p n t", n=NS),
                        dtc[:].unsqueeze(1).broadcast_to([128, NS, TCB]),
                        A_t.unsqueeze(2).broadcast_to([128, NS, TCB])))
                    sp.s(lambda: nc.scalar.activation(out=mda[:], in_=mda[:], func=AF.Exp))
                    for q2, (r, n) in enumerate(rs_):
                        sp.ld(ub[:].rearrange("p (n t) -> p n t", n=NS)[:, :, q2 * 128:(q2 + 1) * 128],
                              A2AO[r * 640 + 576 + 32 * h:r * 640 + 576 + 32 * h + 32, 128 * n:128 * n + 128]
                              .unsqueeze(0).broadcast_to([128, 32, 128]))
                    sp.v(lambda: nc.vector.tensor_mul(
                        ub[:].rearrange("p (n t) -> p n t", n=NS),
                        usb2[:].unsqueeze(1).broadcast_to([128, NS, TCB]),
                        ub[:].rearrange("p (n t) -> p n t", n=NS)))
                    for n_ in range(NS):
                        sp.v(lambda n_=n_: nc.vector.tensor_tensor_scan(
                            hbt[:, n_ * TCB:(n_ + 1) * TCB], mda[:, n_ * TCB:(n_ + 1) * TCB],
                            ub[:, n_ * TCB:(n_ + 1) * TCB], stt[:, n_:n_ + 1], ALU.mult, ALU.add))
                    sp.v(lambda: nc.vector.tensor_copy(stt[:], hbt[:, TCB - 1:NS * TCB:TCB]))
                    for gq in range(4):
                        for q2, (r, n) in enumerate(rs_):
                            sp.ld(cr[gq * 32:(gq + 1) * 32, 0:NS * TCB]
                                  .rearrange("p (n t) -> p n t", n=NS)[:, :, q2 * 128:(q2 + 1) * 128],
                                  A2AO[r * 640 + 256 + 32 * (4 * h + gq):r * 640 + 256 + 32 * (4 * h + gq) + 32,
                                       128 * n:128 * n + 128]
                                  .unsqueeze(0).broadcast_to([32, 32, 128]))
                    sp.v(lambda: nc.vector.tensor_mul(cr[:], hbt[:], cr[:]))
                    sp.v(lambda: nc.vector.reduce_sum(
                        ysb2[:, 0:TCB], cr[:].rearrange("p (n t) -> p t n", n=NS), axis=AX.X))
                    sp.v(lambda h=h: nc.vector.scalar_tensor_tensor(
                        ysb2[:], xrc[:], dpsb[:, h:h + 1], ysb2[:], ALU.mult, ALU.add))
                    for q2, (r, n) in enumerate(rs_):
                        sp.st(YI[r * 256 + 128 * h:r * 256 + 128 * h + 128, 128 * n:128 * n + 128],
                              ysb2[:, q2 * 128:(q2 + 1) * 128])
        def _a2a2():
            with allow_ici_all_to_all():
                return nc.gpsimd.collective_compute(
                    "AllToAll", ALU.bypass, replica_groups=G8, ins=[YI], outs=[YO])
        sp.g(_a2a2)

        AR.reset()
        ybuf = AR.al(DT_ * TCC)
        zbuf = AR.al(DT_ * TCC)
        h1buf = AR.al(DT_ * TCC)
        midbuf = AR.al(NMID * TCC)
        wtileC = AR.al(NMID * 128)
        xnsc = AR.al(TCC)
        gscC = AR.al(TCC)
        oscC = AR.al(TCC)
        sqC = AR.al(TCC)
        g2sb = AR.al(DT_)
        vrowC = AR.al(3 * TCC)
        S1c = lambda: vrowC[0:1, 0:TCC]
        S2c = lambda: vrowC[0:1, TCC:2 * TCC]
        RSTDc = lambda: vrowC[0:1, 2 * TCC:3 * TCC]
        sp.ld(g2sb[:], G2v[:])
        for ch in range(NTOK // TCC):
            t0 = ch * TCC
            for k in range(DT_):
                sp.ld(ybuf[:, k * TCC:(k + 1) * TCC],
                      YO[(k // 2) * 256 + 128 * (k % 2):(k // 2) * 256 + 128 * (k % 2) + 128, t0:t0 + TCC])
                sp.ld(zbuf[:, k * TCC:(k + 1) * TCC], HBz[k * 128:(k + 1) * 128, t0:t0 + TCC])
            for k in range(DT_):
                c0 = k * TCC
                sp.s(lambda c0=c0: nc.scalar.activation(out=zbuf[:, c0:c0 + TCC], in_=zbuf[:, c0:c0 + TCC], func=AF.Silu))
                sp.v(lambda c0=c0: nc.vector.tensor_mul(ybuf[:, c0:c0 + TCC], ybuf[:, c0:c0 + TCC], zbuf[:, c0:c0 + TCC]))
            for m in range(DT_):
                band_ld(sp, nc, wtileC, Wo0, m, DT_, parts=2)
                for k in range(DT_):
                    sp.t(lambda k=k, st=(k == 0), sp_=(k == DT_ - 1):
                         nc.tensor.matmul(P0[:, 0:TCC], wtileC[:, k * 128:(k + 1) * 128],
                                          ybuf[:, k * TCC:(k + 1) * TCC], start=st, stop=sp_))
                sp.ld(xnsc[:], HBxn0[m * 128:(m + 1) * 128, t0:t0 + TCC])
                sp.v(lambda m=m: nc.vector.tensor_add(h1buf[:, m * TCC:(m + 1) * TCC], P0[:, 0:TCC], xnsc[:]))
            ln_stats2(h1buf, TCC, DT_, S1c(), S2c(), sqC)
            rstd_from_s2(S2c(), RSTDc(), 1.0 / D, csb[0:1, 0:1], TCC)
            sp.t(lambda: nc.tensor.matmul(P1[:, 0:TCC], ones1[:, 0:128], RSTDc()[:, 0:TCC], start=True, stop=True))
            for k in range(DT_):
                c0 = k * TCC
                sp.v(lambda c0=c0, k=k: nc.vector.scalar_tensor_tensor(
                    zbuf[:, c0:c0 + TCC], h1buf[:, c0:c0 + TCC], g2sb[:, k:k + 1],
                    P1[:, 0:TCC], ALU.mult, ALU.mult))
            for m in range(NMID):
                band_ld(sp, nc, wtileC, Wg0, m, DT_, parts=2)
                for k in range(DT_):
                    sp.t(lambda k=k, st=(k == 0), sp_=(k == DT_ - 1):
                         nc.tensor.matmul(P0[:, 0:TCC], wtileC[:, k * 128:(k + 1) * 128],
                                          zbuf[:, k * TCC:(k + 1) * TCC], start=st, stop=sp_))
                sp.s(lambda: nc.scalar.activation(out=gscC[:], in_=P0[:, 0:TCC], func=AF.Silu))
                band_ld(sp, nc, wtileC, Wu0, m, DT_, parts=2)
                for k in range(DT_):
                    sp.t(lambda k=k, st=(k == 0), sp_=(k == DT_ - 1):
                         nc.tensor.matmul(P0[:, 0:TCC], wtileC[:, k * 128:(k + 1) * 128],
                                          zbuf[:, k * TCC:(k + 1) * TCC], start=st, stop=sp_))
                sp.v(lambda m=m: nc.vector.tensor_mul(midbuf[:, m * TCC:(m + 1) * TCC], gscC[:], P0[:, 0:TCC]))
            for mo in range(DT_):
                band_ld(sp, nc, wtileC, Wd0, mo, NMID, parts=4)
                for k in range(NMID):
                    sp.t(lambda k=k, st=(k == 0), sp_=(k == NMID - 1):
                         nc.tensor.matmul(P0[:, 0:TCC], wtileC[:, k * 128:(k + 1) * 128],
                                          midbuf[:, k * TCC:(k + 1) * TCC], start=st, stop=sp_))
                sp.v(lambda mo=mo: nc.vector.tensor_add(oscC[:], P0[:, 0:TCC], h1buf[:, mo * TCC:(mo + 1) * TCC]))
                sp.st(HBs0[mo * 128:(mo + 1) * 128, t0:t0 + TCC], oscC[:])

        AR.reset()
        sD = AR.al(64)
        geD = AR.al(64)
        sc1D = AR.al(64)
        sc2D = AR.al(64)
        rsD = AR.al(2)
        ones64f = AR.al(64)
        ltri = AR.al(128)
        vvD = AR.al(32)
        BIG = 1e30
        LOd = vvD[0:1, 0:2]; HId = vvD[0:1, 2:4]; MIDd = vvD[0:1, 4:6]; CNTd = vvD[0:1, 6:8]
        OKd = vvD[0:1, 8:10]; OKCd = vvD[0:1, 10:12]; TMPd = vvD[0:1, 12:14]; VVd = vvD[0:1, 14:16]
        NEEDd = vvD[0:1, 16:18]; C512d = vvD[0:1, 18:20]
        tp = MR[0:2, 0:128]
        for n_ in range(8):
            for b_ in range(2):
                for a_ in range(4):
                    sp.ld(sD[16 * n_ + 4 * a_:16 * n_ + 4 * a_ + 4, 32 * b_:32 * b_ + 32],
                          SCO[4 * b_ + a_:4 * b_ + a_ + 1, 128 * n_:128 * n_ + 128]
                          .rearrange("x (q s) -> (x q) s", q=4))
        sp.v(lambda: nc.vector.memset(ones64f[:], 1.0))
        sp.v(lambda: nc.vector.memset(vvD[:], 0.0))
        sp.v(lambda: nc.vector.memset(C512d, 512.0))
        sp.v(lambda: nc.vector.tensor_copy(ltri[:], fiiI[:]))
        sp.v(lambda: nc.vector.tensor_copy(geD[:, 0:1], pii[:]))
        sp.v(lambda: nc.vector.tensor_scalar(ltri[:], ltri[:], geD[:, 0:1], None, ALU.is_gt))
        s3 = sD[:].rearrange("p (b j) -> p b j", b=2)
        ge3 = geD[:].rearrange("p (b j) -> p b j", b=2)

        def cross_min(dst, src3, op):
            sp.v(lambda src3=src3, op=op: nc.vector.tensor_reduce(rsD[:], src3, axis=AX.X, op=op))
            sp.t(lambda: nc.tensor.transpose(P2[0:2, 0:128], rsD[:], idf[:]))
            sp.v(lambda: nc.vector.tensor_copy(tp[:], P2[0:2, 0:128]))
            sp.v(lambda op=op: nc.vector.tensor_reduce(tp[0:2, 0:1], tp[0:2, :], axis=AX.X, op=op))
            sp.t(lambda: nc.tensor.transpose(P2[0:1, 0:2], tp[0:2, 0:1], idf[0:2, 0:2]))
            sp.v(lambda d=dst: nc.vector.tensor_copy(d, P2[0:1, 0:2]))

        def count_ge(dst_cnt, thr):
            sp.t(lambda thr=thr: nc.tensor.matmul(P0[:, 0:2], ones1[:, 0:128], thr, start=True, stop=True))
            sp.v(lambda: nc.vector.tensor_tensor(ge3, s3, P0[:, 0:2].unsqueeze(2).broadcast_to([128, 2, 32]), ALU.is_ge))
            sp.v(lambda: nc.vector.reduce_sum(rsD[:], ge3, axis=AX.X))
            sp.t(lambda: nc.tensor.matmul(PR[0:1, 0:2], ones128[:, 0:1], rsD[:], start=True, stop=True))
            sp.v(lambda d=dst_cnt: nc.vector.tensor_copy(d, PR[0:1, 0:2]))

        cross_min(LOd, s3, ALU.min)
        cross_min(HId, s3, ALU.max)
        sp.v(lambda: nc.vector.tensor_scalar_add(LOd, LOd, -1.0))
        sp.v(lambda: nc.vector.tensor_scalar_add(HId, HId, 1.0))
        for it in range(44):
            sp.v(lambda: nc.vector.tensor_add(MIDd, LOd, HId))
            sp.v(lambda: nc.vector.tensor_scalar_mul(MIDd, MIDd, 0.5))
            count_ge(CNTd, MIDd)
            sp.v(lambda: nc.vector.tensor_tensor(OKd, CNTd, C512d, ALU.is_ge))
            sp.v(lambda: nc.vector.tensor_scalar(OKCd, OKd, -1.0, 1.0, ALU.mult, ALU.add))
            sp.v(lambda: nc.vector.tensor_sub(TMPd, MIDd, LOd))
            sp.v(lambda: nc.vector.tensor_mul(TMPd, TMPd, OKd))
            sp.v(lambda: nc.vector.tensor_add(LOd, LOd, TMPd))
            sp.v(lambda: nc.vector.tensor_sub(TMPd, MIDd, HId))
            sp.v(lambda: nc.vector.tensor_mul(TMPd, TMPd, OKCd))
            sp.v(lambda: nc.vector.tensor_add(HId, HId, TMPd))
        count_ge(CNTd, LOd)
        sp.v(lambda: nc.vector.tensor_mul(sc1D[:], sD[:], geD[:]))
        sp.v(lambda: nc.vector.tensor_scalar(sc2D[:], geD[:], -BIG, BIG, ALU.mult, ALU.add))
        sp.v(lambda: nc.vector.tensor_add(sc1D[:], sc1D[:], sc2D[:]))
        cross_min(VVd, sc1D[:].rearrange("p (b j) -> p b j", b=2), ALU.min)
        sp.t(lambda: nc.tensor.matmul(P0[:, 0:2], ones1[:, 0:128], VVd, start=True, stop=True))
        sp.v(lambda: nc.vector.tensor_tensor(sc1D[:].rearrange("p (b j) -> p b j", b=2), s3,
                                             P0[:, 0:2].unsqueeze(2).broadcast_to([128, 2, 32]), ALU.is_gt))
        sp.v(lambda: nc.vector.tensor_tensor(sc2D[:].rearrange("p (b j) -> p b j", b=2), s3,
                                             P0[:, 0:2].unsqueeze(2).broadcast_to([128, 2, 32]), ALU.is_equal))
        sp.v(lambda: nc.vector.reduce_sum(rsD[:], sc1D[:].rearrange("p (b j) -> p b j", b=2), axis=AX.X))
        sp.t(lambda: nc.tensor.matmul(PR[0:1, 0:2], ones128[:, 0:1], rsD[:], start=True, stop=True))
        sp.v(lambda: nc.vector.tensor_copy(NEEDd, PR[0:1, 0:2]))
        sp.v(lambda: nc.vector.tensor_sub(NEEDd, C512d, NEEDd))

        def pos_cumsum(src, dst):
            for b_ in range(2):
                sp.v(lambda b_=b_: nc.vector.tensor_tensor_scan(
                    dst[:, b_ * 32:(b_ + 1) * 32], ones64f[:, 0:32], src[:, b_ * 32:(b_ + 1) * 32],
                    0.0, ALU.mult, ALU.add))
            sp.v(lambda dst=dst: nc.vector.tensor_copy(rsD[:], dst[:, 31:64:32]))
            sp.t(lambda: nc.tensor.matmul(P3[:, 0:2], ltri[:], rsD[:], start=True, stop=True))
            sp.v(lambda dst=dst: nc.vector.tensor_tensor(dst[:].rearrange("p (b j) -> p b j", b=2),
                                                         dst[:].rearrange("p (b j) -> p b j", b=2),
                                                         P3[:, 0:2].unsqueeze(2).broadcast_to([128, 2, 32]), ALU.add))

        pos_cumsum(sc2D, geD)
        sp.t(lambda: nc.tensor.matmul(P0[:, 0:2], ones1[:, 0:128], NEEDd, start=True, stop=True))
        sp.v(lambda: nc.vector.tensor_tensor(ge3, ge3, P0[:, 0:2].unsqueeze(2).broadcast_to([128, 2, 32]), ALU.is_le))
        sp.v(lambda: nc.vector.tensor_mul(geD[:], geD[:], sc2D[:]))
        sp.v(lambda: nc.vector.tensor_add(sc1D[:], sc1D[:], geD[:]))
        pos_cumsum(sc1D, sc2D)
        sp.v(lambda: nc.vector.tensor_scalar_add(sc2D[:, 0:32], sc2D[:, 0:32], -1.0))
        sp.v(lambda: nc.vector.tensor_scalar_add(sc2D[:, 32:64], sc2D[:, 32:64], 511.0))
        sp.v(lambda: nc.vector.tensor_scalar_add(geD[:], sc2D[:], 1.0))
        sp.v(lambda: nc.vector.tensor_mul(geD[:], geD[:], sc1D[:]))
        sp.v(lambda: nc.vector.tensor_scalar_add(geD[:], geD[:], -1.0))
        for b_ in range(2):
            sp.st(CANS[b_:b_ + 1, :].rearrange("x (p s) -> (x p) s", p=128), geD[:, 32 * b_:32 * b_ + 32])
            sp.st(CANSC[b_:b_ + 1, :].rearrange("x (p s) -> (x p) s", p=128), sD[:, 32 * b_:32 * b_ + 32])

        AR.reset()
        iotasb = AR.al(NTOK)
        Mt = AR.al(NTOK)
        sc2tm = AR.al(64)
        sctm = AR.al(64)
        postm = AR.al(64)
        growtm = AR.al(64)
        vt = AR.al(4)
        tpad = AR.al(128)
        vr1 = VR[:, 0:NTOK]
        sp.ld(vr1, Iiota[:])
        for sh in range(2):
            sp.t(lambda sh=sh: nc.tensor.matmul(P0[:, 0:512], ones1[:, 0:128], vr1[:, sh * 512:(sh + 1) * 512], start=True, stop=True))
            sp.v(lambda sh=sh: nc.vector.tensor_copy(iotasb[:, sh * 512:(sh + 1) * 512], P0[:, 0:512]))
        nat64 = AR.al(128)
        sp.ld(nat64[0:64, 0:128], CANS[:, :].rearrange("b (r k) -> (b r) k", k=128))
        sp.t(lambda: nc.tensor.transpose(P2[:, 0:64], nat64[0:64, 0:128], idf[0:64, 0:64]))
        sp.v(lambda: nc.vector.tensor_copy(sc2tm[:], P2[:, 0:64]))
        sp.ld(nat64[0:64, 0:128], CANSC[:, :].rearrange("b (r k) -> (b r) k", k=128))
        sp.t(lambda: nc.tensor.transpose(P2[:, 0:64], nat64[0:64, 0:128], idf[0:64, 0:64]))
        sp.v(lambda: nc.vector.tensor_copy(sctm[:], P2[:, 0:64]))
        sp.ld(postm[:], Ipostm[:])
        sp.ld(growtm[:], Igrowtm[:])
        sp.v(lambda: nc.vector.memset(vt[:, 3:4], 0.0))
        for c64 in range(64):
            sp.v(lambda c64=c64: nc.vector.tensor_scalar(Mt[:], iotasb[:], sc2tm[:, c64:c64 + 1], None, ALU.is_equal))
            sp.v(lambda c64=c64: nc.vector.tensor_copy(vt[:, 0:1], postm[:, c64:c64 + 1]))
            sp.v(lambda c64=c64: nc.vector.tensor_copy(vt[:, 1:2], growtm[:, c64:c64 + 1]))
            sp.v(lambda c64=c64: nc.vector.tensor_copy(vt[:, 2:3], sctm[:, c64:c64 + 1]))
            sp.t(lambda c64=c64: nc.tensor.matmul(P3[0:4, 0:512], vt[:, 0:4], Mt[:, 0:512],
                                                  start=(c64 == 0), stop=(c64 == 63)))
            sp.t(lambda c64=c64: nc.tensor.matmul(P4[0:4, 0:512], vt[:, 0:4], Mt[:, 512:1024],
                                                  start=(c64 == 0), stop=(c64 == 63)))
        sp.v(lambda: nc.vector.tensor_copy(MR[0:4, 0:512], P3[0:4, 0:512]))
        sp.v(lambda: nc.vector.tensor_copy(MR[0:4, 512:1024], P4[0:4, 0:512]))
        sp.st(BYSLOT[0:4, :], MR[0:4, 0:NTOK])
        for (dst8, row) in ((posk8, 0), (grow8, 1), (sco8, 2)):
            sp.ld(nat64[0:8, 0:128], BYSLOT[row:row + 1, :].rearrange("x (k p) -> (x k) p", p=128))
            sp.t(lambda: nc.tensor.transpose(P2[:, 0:8], nat64[0:8, 0:128], idf[0:8, 0:8]))
            sp.v(lambda dst8=dst8: nc.vector.tensor_copy(dst8[:], P2[:, 0:8]))
        vr8 = VR[:, 1088:1096]
        sp.ld(vr8, Ioneh[:])
        sp.t(lambda: nc.tensor.matmul(P2[:, 0:8], ones1[:, 0:128], vr8, start=True, stop=True))
        sp.v(lambda: nc.vector.tensor_mul(t8a[:], posk8[:], P2[:, 0:8]))
        sp.v(lambda: nc.vector.reduce_sum(posq[:], t8a[:], axis=AX.X))
        sp.v(lambda: nc.vector.tensor_mul(t8a[:], sco8[:], P2[:, 0:8]))
        sp.v(lambda: nc.vector.reduce_sum(t8b[:, 0:1], t8a[:], axis=AX.X))
        sp.t(lambda: nc.tensor.transpose(P2[0:1, 0:128], t8b[:, 0:1], idf[:]))
        sp.v(lambda: nc.vector.tensor_copy(scqrow[:], P2[0:1, 0:128]))
        cofs_sb = PERS.al(1)
        sp.ld(cofs_sb[:], Icofs[:])
        sp.v(lambda: nc.vector.tensor_scalar(t8a[:], grow8[:], cofs_sb[:, 0:1], None, ALU.add))
        sp.v(lambda: nc.vector.tensor_scalar(t8b[:], t8a[:], 0.0, None, ALU.is_ge))
        sp.v(lambda: nc.vector.tensor_scalar(t8c[:], t8a[:], 1024.0, None, ALU.is_lt))
        sp.v(lambda: nc.vector.tensor_mul(t8b[:], t8b[:], t8c[:]))
        sp.v(lambda: nc.vector.tensor_scalar_add(t8a[:], t8a[:], -1024.0))
        sp.v(lambda: nc.vector.tensor_mul(t8a[:], t8a[:], t8b[:]))
        sp.v(lambda: nc.vector.tensor_scalar_add(t8a[:], t8a[:], 1024.0))
        sp.v(lambda: nc.vector.tensor_copy(selidxI[:], t8a[:]))
        sp.ld(myslotI[:], Imyslot[:])
        sp.ld(myrowsI[:], Imyrows[:])
        slotrow = MR[0:8, 1024:1152]
        mselrow = MR[0:1, 0:1024]
        sp.g(lambda: nc.gpsimd.indirect_dma_start(
            out=slotrow, out_offset=None,
            in_=CANS[:, :].rearrange("b (r k) -> (b r) k", k=128),
            in_offset=IndirectOffsetOnAxis(ap=myrowsI, axis=0)), inc=16)
        sp.v(lambda: nc.vector.memset(tpad[:], 0.0))
        sp.v(lambda: nc.vector.tensor_copy(tpad[0:8, 0:128], slotrow))
        sp.t(lambda: nc.tensor.transpose(P2[:, 0:128], tpad[:], idf[:]))
        sp.v(lambda: nc.vector.tensor_copy(slotf[:], P2[:, 0:8]))
        sp.v(lambda: nc.vector.tensor_scalar(t8a[:], slotf[:], 0.0, None, ALU.is_ge))
        for lk in range(8):
            sp.t(lambda lk=lk: nc.tensor.transpose(P2[0:1, 0:128], t8a[:, lk:lk + 1], idf[:]))
            sp.v(lambda lk=lk: nc.vector.tensor_copy(mselrow[:, lk * 128:(lk + 1) * 128], P2[0:1, 0:128]))
        sp.s(lambda: nc.scalar.activation(out=slotf[:], in_=slotf[:], func=AF.Relu))
        sp.v(lambda: nc.vector.tensor_copy(gidxI[:], slotf[:]))
        g128 = AR.al(D)
        for k8 in range(8):
            sp.g(lambda k8=k8: nc.gpsimd.indirect_dma_start(
                out=g128[:], out_offset=None, in_=XN0TM[0:1, :],
                in_offset=IndirectOffsetOnAxis(ap=selidxI[:, k8:k8 + 1], axis=0)), inc=16)
            sp.st(SELP[k8 * 128:(k8 + 1) * 128, :], g128[:])
        sp.g(lambda: nc.gpsimd.collective_compute(
            "AllReduce", ALU.add, replica_groups=G8, ins=[SELP], outs=[SELTM]))

        AR.reset()
        S = 1024
        NH, NKV = 16, 8
        bufA = AR.al(DT_ * S)
        bufH = AR.al(DT_ * S)
        qsb = AR.al(DT_ * 128)
        selmy = AR.al(DT_ * 128)
        x1sbE = AR.al(DT_ * 128)
        avfm = AR.al(DT_ * 128)
        h2sb = qsb
        wtileE = AR.al(16 * 128)
        ttile = x1sbE
        gat_tm = avfm
        cmsb = bufH[:, 8192:9216]
        trig = AR.al(2 * 128)
        tmq = AR.al(128)
        tmq2 = AR.al(128)
        gscE = AR.al(128)
        expb = AR.al(128)
        sqE = AR.al(512)
        angs = AR.al(3 * 64)
        smv = AR.al(5 * DT_)
        vecsE = AR.al(3 * S + 384)
        RSTDe = lambda: vecsE[0:1, 0:S]
        S1e = lambda: vecsE[0:1, S:2 * S]
        S2e = lambda: vecsE[0:1, 2 * S:3 * S]
        AUXe = lambda: vecsE[0:1, 3 * S:3 * S + 384]
        ifr = VR[:, 1024:1088]

        for sk in range(8):
            sp.ld(ttile[:], SELTM[sk * 128:(sk + 1) * 128, :])
            for ft in range(DT_):
                sp.t(lambda ft=ft: nc.tensor.transpose(P4[:, 0:128], ttile[:, ft * 128:(ft + 1) * 128], idf[:]))
                sp.v(lambda ft=ft, sk=sk: nc.vector.tensor_copy(
                    bufA[:, ft * S + sk * 128:ft * S + sk * 128 + 128], P4[:, 0:128]))
        sp.g(lambda: nc.gpsimd.indirect_dma_start(
            out=ttile[:], out_offset=None, in_=SELTM[0:1, :],
            in_offset=IndirectOffsetOnAxis(ap=myslotI, axis=0)), inc=16)
        for ft in range(DT_):
            sp.t(lambda ft=ft: nc.tensor.transpose(P4[:, 0:128], ttile[:, ft * 128:(ft + 1) * 128], idf[:]))
            sp.v(lambda ft=ft: nc.vector.tensor_copy(selmy[:, ft * 128:(ft + 1) * 128], P4[:, 0:128]))
        sp.ld(smv[:, 0:DT_], Vln1[:])
        sp.ld(smv[:, DT_:2 * DT_], Vln2[:])
        sp.ld(smv[:, 2 * DT_:3 * DT_], VbqE[:])
        sp.ld(smv[:, 3 * DT_:3 * DT_ + 8], VbkE[:])
        sp.ld(smv[:, 4 * DT_:4 * DT_ + 8], VbvE[:])
        sp.g(lambda: nc.gpsimd.iota(ifriI[:], pattern=[[1, 64]], base=0, channel_multiplier=0))
        sp.v(lambda: nc.vector.tensor_copy(ifr[:], ifriI[:]))
        sp.s(lambda: nc.scalar.activation(out=ifr[:], in_=ifr[:], func=AF.Exp, scale=csb[0:1, 4:5]))

        def rope_trig(pos_col, dstc):
            sp.t(lambda: nc.tensor.matmul(P3[:, 0:64], ones1[:, 0:128], ifr[:], start=True, stop=True))
            A_ = angs[:, 0:64]; R_ = angs[:, 64:128]; W_ = angs[:, 128:192]
            sp.v(lambda pos_col=pos_col: nc.vector.tensor_scalar(A_, P3[:, 0:64], pos_col, None, ALU.mult))
            for (dst, phase) in ((dstc + 64, 0.0), (dstc, np.pi / 2)):
                sp.v(lambda phase=phase: nc.vector.tensor_scalar(W_, A_, 1.0 / ROPE_C1, phase / ROPE_C1, ALU.mult, ALU.add))
                sp.v(lambda: nc.vector.tensor_copy(nintI[:], W_))
                sp.v(lambda: nc.vector.tensor_copy(W_, nintI[:]))
                sp.v(lambda phase=phase: nc.vector.tensor_scalar(R_, A_, 1.0, phase, ALU.mult, ALU.add))
                sp.v(lambda: nc.vector.scalar_tensor_tensor(R_, W_, -ROPE_C1, R_, ALU.mult, ALU.add))
                sp.v(lambda: nc.vector.scalar_tensor_tensor(R_, W_, -ROPE_C2, R_, ALU.mult, ALU.add))
                sp.s(lambda dst=dst: nc.scalar.activation(out=trig[:, dst:dst + 64], in_=R_, func=AF.Sin))

        def rope_tile(fm_ap, trigc):
            sp.t(lambda fm_ap=fm_ap: nc.tensor.transpose(P4[:, 0:128], fm_ap, idf[:]))
            sp.v(lambda: nc.vector.tensor_copy(tmq[:], P4[:, 0:128]))
            C_ = trig[:, trigc:trigc + 64]; Sn = trig[:, trigc + 64:trigc + 128]
            sp.v(lambda: nc.vector.tensor_mul(tmq2[:, 0:64], tmq[:, 0:64], C_))
            sp.v(lambda: nc.vector.tensor_mul(gscE[:, 0:64], tmq[:, 64:128], Sn))
            sp.v(lambda: nc.vector.tensor_sub(tmq2[:, 0:64], tmq2[:, 0:64], gscE[:, 0:64]))
            sp.v(lambda: nc.vector.tensor_mul(tmq2[:, 64:128], tmq[:, 64:128], C_))
            sp.v(lambda: nc.vector.tensor_mul(gscE[:, 0:64], tmq[:, 0:64], Sn))
            sp.v(lambda: nc.vector.tensor_add(tmq2[:, 64:128], tmq2[:, 64:128], gscE[:, 0:64]))
            sp.t(lambda: nc.tensor.transpose(P4[:, 0:128], tmq2[:], idf[:]))
            sp.v(lambda fm_ap=fm_ap: nc.vector.tensor_copy(fm_ap, P4[:, 0:128]))

        for half in range(2):
            ln_stats2(bufA, 512, DT_, S1e()[:, half * 512:(half + 1) * 512],
                      S2e()[:, half * 512:(half + 1) * 512], sqE, base=half * 512, stride=S)
        sp.v(lambda: nc.vector.tensor_scalar_mul(S2e(), S2e(), 1.0 / D))
        sp.s(lambda: nc.scalar.activation(out=RSTDe(), in_=S2e(), func=AF.Sqrt, bias=csb[0:1, 3:4]))
        sp.v(lambda: nc.vector.reciprocal(RSTDe(), RSTDe()))
        for half in range(2):
            sp.t(lambda half=half: nc.tensor.matmul(P0[:, 0:512], ones1[:, 0:128],
                                                    RSTDe()[:, half * 512:(half + 1) * 512], start=True, stop=True))
            for k in range(DT_):
                c0 = k * S + half * 512
                sp.v(lambda c0=c0, k=k: nc.vector.scalar_tensor_tensor(
                    bufH[:, c0:c0 + 512], bufA[:, c0:c0 + 512], smv[:, k:k + 1], P0[:, 0:512], ALU.mult, ALU.mult))
        ln_stats2(selmy, 128, DT_, S1e()[:, 0:128], S2e()[:, 0:128], sqE)
        sp.v(lambda: nc.vector.tensor_scalar_mul(S2e()[:, 0:128], S2e()[:, 0:128], 1.0 / D))
        sp.s(lambda: nc.scalar.activation(out=AUXe()[:, 0:128], in_=S2e()[:, 0:128], func=AF.Sqrt, bias=csb[0:1, 3:4]))
        sp.v(lambda: nc.vector.reciprocal(AUXe()[:, 0:128], AUXe()[:, 0:128]))
        sp.t(lambda: nc.tensor.matmul(P3[:, 0:128], ones1[:, 0:128], AUXe()[:, 0:128], start=True, stop=True))
        for k in range(DT_):
            sp.v(lambda k=k: nc.vector.scalar_tensor_tensor(
                x1sbE[:, k * 128:(k + 1) * 128], selmy[:, k * 128:(k + 1) * 128], smv[:, k:k + 1],
                P3[:, 0:128], ALU.mult, ALU.mult))
        for m in range(DT_):
            band_ld(sp, nc, wtileE, WqE, m, DT_, parts=2)
            for k in range(DT_):
                sp.t(lambda k=k, st=(k == 0), sp_=(k == DT_ - 1):
                     nc.tensor.matmul(P1[:, 0:128], wtileE[:, k * 128:(k + 1) * 128],
                                      x1sbE[:, k * 128:(k + 1) * 128], start=st, stop=sp_))
            sp.v(lambda m=m: nc.vector.tensor_scalar_add(qsb[:, m * 128:(m + 1) * 128], P1[:, 0:128], smv[:, 2 * DT_ + m:2 * DT_ + m + 1]))
        for (W_, bcol, off) in ((WkE, 3 * DT_, 0), (WvE, 4 * DT_, 8 * S)):
            for m in range(8):
                band_ld(sp, nc, wtileE, W_, m, DT_, parts=2)
                for half in range(2):
                    for k in range(DT_):
                        sp.t(lambda k=k, half=half, st=(k == 0), sp_=(k == DT_ - 1):
                             nc.tensor.matmul(P0[:, 0:512], wtileE[:, k * 128:(k + 1) * 128],
                                              bufH[:, k * S + half * 512:k * S + half * 512 + 512], start=st, stop=sp_))
                    sp.v(lambda m=m, half=half, bcol=bcol, off=off: nc.vector.tensor_scalar_add(
                        bufA[:, off + m * S + half * 512:off + m * S + half * 512 + 512], P0[:, 0:512],
                        smv[:, bcol + m:bcol + m + 1]))
        rope_trig(posq[:, 0:1], 128)
        for h in range(NH):
            rope_tile(qsb[:, h * 128:(h + 1) * 128], 128)
        for st_ in range(8):
            rope_trig(posk8[:, st_:st_ + 1], 0)
            for kh in range(NKV):
                rope_tile(bufA[:, kh * S + st_ * 128:kh * S + st_ * 128 + 128], 0)
        for kh in range(NKV):
            for st_ in range(8):
                sp.t(lambda kh=kh, st_=st_: nc.tensor.transpose(
                    P4[:, 0:128], bufA[:, 8 * S + kh * S + st_ * 128:8 * S + kh * S + st_ * 128 + 128], idf[:]))
                sp.v(lambda kh=kh, st_=st_: nc.vector.tensor_copy(
                    bufH[:, (kh * 8 + st_) * 128:(kh * 8 + st_ + 1) * 128], P4[:, 0:128]))
        sp.ld(cmsb[:], Icm[:])
        for kh in range(NKV):
            for st_ in range(8):
                sp.s(lambda kh=kh, st_=st_: nc.scalar.activation(
                    out=sqE[:, 0:128], in_=bufA[:, kh * S + st_ * 128:kh * S + st_ * 128 + 128], func=AF.Square))
                sp.t(lambda: nc.tensor.matmul(PR2[0:1, 0:128], ones128[:, 0:1], sqE[:, 0:128], start=True, stop=True))
                sp.v(lambda st_=st_: nc.vector.tensor_copy(S1e()[:, st_ * 128:(st_ + 1) * 128], PR2[0:1, 0:128]))
            sp.v(lambda kh=kh: nc.vector.reduce_max(AUXe()[:, 256 + kh:256 + kh + 1], S1e()[:, 0:S], axis=AX.X))
        for h in range(NH):
            kh = h // 2
            sp.s(lambda h=h: nc.scalar.activation(out=sqE[:, 0:128], in_=qsb[:, h * 128:(h + 1) * 128], func=AF.Square))
            sp.t(lambda: nc.tensor.matmul(PR2[0:1, 0:128], ones128[:, 0:1], sqE[:, 0:128], start=True, stop=True))
            sp.s(lambda: nc.scalar.activation(out=S1e()[:, 0:128], in_=PR2[0:1, 0:128], func=AF.Sqrt))
            sp.s(lambda kh=kh: nc.scalar.activation(out=AUXe()[:, 300:301], in_=AUXe()[:, 256 + kh:257 + kh], func=AF.Sqrt))
            sp.v(lambda: nc.vector.tensor_scalar(S1e()[:, 0:128], S1e()[:, 0:128], AUXe()[0:1, 300:301], None, ALU.mult))
            sp.v(lambda: nc.vector.tensor_scalar_mul(S1e()[:, 0:128], S1e()[:, 0:128], -1.0 / np.sqrt(128.0)))
            sp.t(lambda: nc.tensor.matmul(P3[:, 0:128], ones1[:, 0:128], S1e()[:, 0:128], start=True, stop=True))
            sp.v(lambda: nc.vector.tensor_copy(tmq[:], P3[:, 0:128]))
            for kt in range(8):
                sp.t(lambda kh=kh, kt=kt, h=h: nc.tensor.matmul(
                    P1[:, 0:128], bufA[:, kh * S + kt * 128:kh * S + kt * 128 + 128],
                    qsb[:, h * 128:(h + 1) * 128], start=True, stop=True))
                sp.v(lambda: nc.vector.scalar_tensor_tensor(expb[:], P1[:, 0:128], 1.0 / np.sqrt(128.0), tmq[:], ALU.mult, ALU.add))
                sp.s(lambda: nc.scalar.activation(out=expb[:], in_=expb[:], func=AF.Exp))
                sp.v(lambda kt=kt: nc.vector.tensor_mul(expb[:], expb[:], cmsb[:, kt * 128:(kt + 1) * 128]))
                sp.t(lambda kt=kt: nc.tensor.matmul(PR2[0:1, 0:128], ones128[:, 0:1], expb[:], start=(kt == 0), stop=(kt == 7)))
                sp.t(lambda kh=kh, kt=kt: nc.tensor.matmul(
                    P2[:, 0:128], bufH[:, (kh * 8 + kt) * 128:(kh * 8 + kt + 1) * 128], expb[:], start=(kt == 0), stop=(kt == 7)))
            sp.v(lambda: nc.vector.tensor_copy(S1e()[:, 0:128], PR2[0:1, 0:128]))
            sp.v(lambda: nc.vector.reciprocal(S1e()[:, 0:128], S1e()[:, 0:128]))
            sp.t(lambda: nc.tensor.matmul(P3[:, 0:128], ones1[:, 0:128], S1e()[:, 0:128], start=True, stop=True))
            sp.v(lambda: nc.vector.tensor_copy(tmq[:], P3[:, 0:128]))
            sp.v(lambda h=h: nc.vector.tensor_mul(avfm[:, h * 128:(h + 1) * 128], P2[:, 0:128], tmq[:]))
        for m in range(DT_):
            band_ld(sp, nc, wtileE, WoE, m, DT_, parts=2)
            for k in range(DT_):
                sp.t(lambda k=k, st=(k == 0), sp_=(k == DT_ - 1):
                     nc.tensor.matmul(P1[:, 0:128], wtileE[:, k * 128:(k + 1) * 128],
                                      avfm[:, k * 128:(k + 1) * 128], start=st, stop=sp_))
            sp.v(lambda m=m: nc.vector.tensor_add(x1sbE[:, m * 128:(m + 1) * 128], P1[:, 0:128], selmy[:, m * 128:(m + 1) * 128]))
        ln_stats2(x1sbE, 128, DT_, S1e()[:, 0:128], S2e()[:, 0:128], sqE)
        sp.v(lambda: nc.vector.tensor_scalar_mul(S2e()[:, 0:128], S2e()[:, 0:128], 1.0 / D))
        sp.s(lambda: nc.scalar.activation(out=AUXe()[:, 0:128], in_=S2e()[:, 0:128], func=AF.Sqrt, bias=csb[0:1, 3:4]))
        sp.v(lambda: nc.vector.reciprocal(AUXe()[:, 0:128], AUXe()[:, 0:128]))
        sp.t(lambda: nc.tensor.matmul(P3[:, 0:128], ones1[:, 0:128], AUXe()[:, 0:128], start=True, stop=True))
        for k in range(DT_):
            sp.v(lambda k=k: nc.vector.scalar_tensor_tensor(
                h2sb[:, k * 128:(k + 1) * 128], x1sbE[:, k * 128:(k + 1) * 128], smv[:, DT_ + k:DT_ + k + 1],
                P3[:, 0:128], ALU.mult, ALU.mult))
        MIDOFF = 64 * 128
        for m in range(64):
            band_ld(sp, nc, wtileE, WgE, m, DT_, parts=2)
            for k in range(DT_):
                sp.t(lambda k=k, st=(k == 0), sp_=(k == DT_ - 1):
                     nc.tensor.matmul(P1[:, 0:128], wtileE[:, k * 128:(k + 1) * 128],
                                      h2sb[:, k * 128:(k + 1) * 128], start=st, stop=sp_))
            sp.s(lambda: nc.scalar.activation(out=gscE[:], in_=P1[:, 0:128], func=AF.Silu))
            band_ld(sp, nc, wtileE, WuE, m, DT_, parts=2)
            for k in range(DT_):
                sp.t(lambda k=k, st=(k == 0), sp_=(k == DT_ - 1):
                     nc.tensor.matmul(P1[:, 0:128], wtileE[:, k * 128:(k + 1) * 128],
                                      h2sb[:, k * 128:(k + 1) * 128], start=st, stop=sp_))
            sp.v(lambda m=m: nc.vector.tensor_mul(bufH[:, MIDOFF + m * 128:MIDOFF + (m + 1) * 128], gscE[:], P1[:, 0:128]))
        sp.s(lambda: nc.scalar.activation(out=S1e()[:, 0:128], in_=scqrow[:], func=AF.Sigmoid))
        sp.t(lambda: nc.tensor.matmul(P3[:, 0:128], ones1[:, 0:128], S1e()[:, 0:128], start=True, stop=True))
        for mo in range(DT_):
            for hf in range(4):
                for p0 in range(0, 16, 16):
                    sp.ld(wtileE[:, 0:16 * 128],
                          WdE[hf * 16 * 128:(hf + 1) * 16 * 128, mo * 128:(mo + 1) * 128]
                          .rearrange("(k p) j -> p k j", p=128))
                for k in range(16):
                    kk = hf * 16 + k
                    sp.t(lambda k=k, kk=kk, st=(kk == 0), sp_=(kk == 63):
                         nc.tensor.matmul(P1[:, 0:128], wtileE[:, k * 128:(k + 1) * 128],
                                          bufH[:, MIDOFF + kk * 128:MIDOFF + (kk + 1) * 128], start=st, stop=sp_))
            sp.v(lambda mo=mo: nc.vector.tensor_add(gscE[:], P1[:, 0:128], x1sbE[:, mo * 128:(mo + 1) * 128]))
            sp.v(lambda mo=mo: nc.vector.tensor_sub(gscE[:], gscE[:], selmy[:, mo * 128:(mo + 1) * 128]))
            sp.v(lambda: nc.vector.tensor_mul(gscE[:], gscE[:], P3[:, 0:128]))
            sp.v(lambda mo=mo: nc.vector.tensor_add(gscE[:], gscE[:], selmy[:, mo * 128:(mo + 1) * 128]))
            sp.s(lambda: nc.scalar.activation(out=gscE[:], in_=gscE[:], func=AF.Silu))
            sp.t(lambda: nc.tensor.transpose(P4[:, 0:128], gscE[:], idf[:]))
            sp.v(lambda mo=mo: nc.vector.tensor_copy(gat_tm[:, mo * 128:(mo + 1) * 128], P4[:, 0:128]))
        sp.st(GATI[:, :], gat_tm[:])
        sp.g(lambda: nc.gpsimd.collective_compute(
            "AllGather", ALU.bypass, replica_groups=G8, ins=[GATI], outs=[GATO]))

        AR.reset()
        x1sbF = AR.al(DT_ * NTOK)
        gself = AR.al(DT_ * NTOK)
        gtm = AR.al(D)
        tsbF = AR.al(NTOK)
        sqF = AR.al(NCH)
        t1F = AR.al(NCH)
        vrowF = AR.al(5 * NTOK)
        S1f = lambda: vrowF[0:1, 0:NTOK]
        S2f = lambda: vrowF[0:1, NTOK:2 * NTOK]
        MUf = lambda: vrowF[0:1, 2 * NTOK:3 * NTOK]
        RSTDf = lambda: vrowF[0:1, 3 * NTOK:4 * NTOK]
        NMRf = lambda: vrowF[0:1, 4 * NTOK:5 * NTOK]
        for lk in range(8):
            sp.g(lambda lk=lk: nc.gpsimd.indirect_dma_start(
                out=gtm[:], out_offset=None, in_=GATO[0:1, :],
                in_offset=IndirectOffsetOnAxis(ap=gidxI[:, lk:lk + 1], axis=0)), inc=16)
            for ft in range(DT_):
                sp.t(lambda ft=ft: nc.tensor.transpose(P4[:, 0:128], gtm[:, ft * 128:(ft + 1) * 128], idf[:]))
                sp.v(lambda ft=ft, lk=lk: nc.vector.tensor_copy(
                    gself[:, ft * NTOK + lk * 128:ft * NTOK + lk * 128 + 128], P4[:, 0:128]))
        mselrow = MR[0:1, 0:1024]
        for lk in range(8):
            bcast_row(P3, mselrow, lk * 128, 128)
            for k in range(DT_):
                c0 = k * NTOK + lk * 128
                sp.ld(tsbF[:, 0:128], HBgb[k * 128:(k + 1) * 128, lk * 128:lk * 128 + 128])
                sp.v(lambda c0=c0: nc.vector.tensor_sub(t1F[:, 0:128], gself[:, c0:c0 + 128], tsbF[:, 0:128]))
                sp.v(lambda: nc.vector.tensor_mul(t1F[:, 0:128], t1F[:, 0:128], P3[:, 0:128]))
                sp.v(lambda: nc.vector.tensor_add(t1F[:, 0:128], t1F[:, 0:128], tsbF[:, 0:128]))
                sp.ld(tsbF[:, 128:256], HBs0[k * 128:(k + 1) * 128, lk * 128:lk * 128 + 128])
                sp.v(lambda: nc.vector.tensor_mul(t1F[:, 0:128], t1F[:, 0:128], tsbF[:, 128:256]))
                sp.ld(tsbF[:, 256:384], Xin[k * 128:(k + 1) * 128, lk * 128:lk * 128 + 128])
                sp.v(lambda c0=c0: nc.vector.tensor_add(x1sbF[:, c0:c0 + 128], t1F[:, 0:128], tsbF[:, 256:384]))
        for k in range(DT_):
            sp.st(HBx1[k * 128:(k + 1) * 128, :], x1sbF[:, k * NTOK:(k + 1) * NTOK])
        ln_stats(x1sbF, NTOK, DT_, S1f(), S2f(), sqF)
        sp.v(lambda: nc.vector.tensor_scalar_mul(MUf(), S1f(), dinv))
        sp.v(lambda: nc.vector.tensor_mul(S1f(), MUf(), MUf()))
        sp.v(lambda: nc.vector.tensor_scalar_mul(S2f(), S2f(), dinv))
        sp.v(lambda: nc.vector.tensor_sub(S2f(), S2f(), S1f()))
        sp.s(lambda: nc.scalar.activation(out=RSTDf(), in_=S2f(), func=AF.Sqrt, bias=csb[0:1, 0:1]))
        sp.v(lambda: nc.vector.reciprocal(RSTDf(), RSTDf()))
        sp.v(lambda: nc.vector.tensor_mul(NMRf(), MUf(), RSTDf()))
        sp.v(lambda: nc.vector.tensor_scalar_mul(NMRf(), NMRf(), -1.0))
        for ch in range(NTOK // NCH):
            bcast_row(P1, RSTDf(), ch * NCH, NCH)
            bcast_row(P2, NMRf(), ch * NCH, NCH)
            for k in range(DT_):
                c0 = k * NTOK + ch * NCH
                sp.v(lambda c0=c0: nc.vector.tensor_mul(t1F[:, 0:NCH], x1sbF[:, c0:c0 + NCH], P1[:, 0:NCH]))
                sp.v(lambda c0=c0: nc.vector.tensor_add(t1F[:, 0:NCH], t1F[:, 0:NCH], P2[:, 0:NCH]))
                sp.st(HBxn1[k * 128:(k + 1) * 128, ch * NCH:(ch + 1) * NCH], t1F[:, 0:NCH])
                if ch == 0:
                    sp.st(KEYI[k * 128:(k + 1) * 128, 0:128], t1F[:, 0:128])
                else:
                    sp.st(KEYI[k * 128:(k + 1) * 128, 128:384], t1F[:, 256:512])
        sp.g(lambda: nc.gpsimd.collective_compute(
            "AllGather", ALU.bypass, replica_groups=G8, ins=[KEYI], outs=[KEYO]))

        AR.reset()
        NH1 = 16
        xasb = AR.al(DT_ * 128)
        xasb2 = AR.al(DT_ * 128)
        kvsb = AR.al(DT_ * 128)
        qsb1 = AR.al(DT_ * NCH)
        xnsb1 = AR.al(DT_ * NCH)
        avsb = AR.al(DT_ * NCH)
        wtile1 = AR.al(DT_ * 128)
        ktile = AR.al(128)
        vtt = AR.al(128)
        trig1 = AR.al(128)
        tmq1 = AR.al(128)
        tmq21 = AR.al(128)
        gsc1 = AR.al(128)
        expb1 = AR.al(NCH)
        cneg1 = AR.al(NCH)
        sq1 = AR.al(NCH)
        angs1 = AR.al(3 * 64)
        smv1 = AR.al(4 * DT_ + 8 + 12 + 12 + 1)
        vrow1 = AR.al(3 * NTOK + 64)
        kmr = AR.al(KPAD)
        CSg = lambda: vrow1[0:1, 0:NTOK]
        QNg = lambda: vrow1[0:1, NTOK:2 * NTOK]
        AUXg = lambda: vrow1[0:1, 2 * NTOK:3 * NTOK]
        KNMg = lambda: vrow1[0:1, 3 * NTOK:3 * NTOK + 64]
        OBQ, OBK, OBV, OBO = 0, DT_, 2 * DT_, 3 * DT_
        OQP, OKP, OKM = 4 * DT_, 4 * DT_ + 8, 4 * DT_ + 8 + 12
        OSM = 4 * DT_ + 8 + 12 + 12
        ifr1 = VR[:, 1024:1088]
        sp.ld(smv1[:, OBQ:OBQ + DT_], Vbq1[:])
        sp.ld(smv1[:, OBK:OBK + DT_], Vbk1[:])
        sp.ld(smv1[:, OBV:OBV + DT_], Vbv1[:])
        sp.ld(smv1[:, OBO:OBO + DT_], Vbo1[:])
        sp.ld(smv1[:, OQP:OQP + 8], Iqp[:])
        sp.ld(smv1[:, OKP:OKP + 12], Ikp[:])
        sp.ld(smv1[:, OKM:OKM + 12], Ikm[:])
        sp.ld(smv1[:, OSM:OSM + 1], Ismask[:])
        sp.ld(kmr[0:1, 0:KPAD], Ikmr[:])
        sp.g(lambda: nc.gpsimd.iota(ifriI[:], pattern=[[1, 64]], base=0, channel_multiplier=0))
        sp.v(lambda: nc.vector.tensor_copy(ifr1[:], ifriI[:]))
        sp.s(lambda: nc.scalar.activation(out=ifr1[:], in_=ifr1[:], func=AF.Exp, scale=csb[0:1, 4:5]))
        sp.v(lambda: nc.vector.memset(KNMg()[:, 0:64], 0.0))

        def rope_trig1(pos_col):
            sp.t(lambda: nc.tensor.matmul(P3[:, 0:64], ones1[:, 0:128], ifr1[:], start=True, stop=True))
            A_ = angs1[:, 0:64]; R_ = angs1[:, 64:128]; W_ = angs1[:, 128:192]
            sp.v(lambda pos_col=pos_col: nc.vector.tensor_scalar(A_, P3[:, 0:64], pos_col, None, ALU.mult))
            for (dst, phase) in ((64, 0.0), (0, np.pi / 2)):
                sp.v(lambda phase=phase: nc.vector.tensor_scalar(W_, A_, 1.0 / ROPE_C1, phase / ROPE_C1, ALU.mult, ALU.add))
                sp.v(lambda: nc.vector.tensor_copy(nintI[:], W_))
                sp.v(lambda: nc.vector.tensor_copy(W_, nintI[:]))
                sp.v(lambda phase=phase: nc.vector.tensor_scalar(R_, A_, 1.0, phase, ALU.mult, ALU.add))
                sp.v(lambda: nc.vector.scalar_tensor_tensor(R_, W_, -ROPE_C1, R_, ALU.mult, ALU.add))
                sp.v(lambda: nc.vector.scalar_tensor_tensor(R_, W_, -ROPE_C2, R_, ALU.mult, ALU.add))
                sp.s(lambda dst=dst: nc.scalar.activation(out=trig1[:, dst:dst + 64], in_=R_, func=AF.Sin))

        def rope_tile1(fm_ap):
            sp.t(lambda fm_ap=fm_ap: nc.tensor.transpose(P4[:, 0:128], fm_ap, idf[:]))
            sp.v(lambda: nc.vector.tensor_copy(tmq1[:], P4[:, 0:128]))
            C_ = trig1[:, 0:64]; Sn = trig1[:, 64:128]
            sp.v(lambda: nc.vector.tensor_mul(tmq21[:, 0:64], tmq1[:, 0:64], C_))
            sp.v(lambda: nc.vector.tensor_mul(gsc1[:, 0:64], tmq1[:, 64:128], Sn))
            sp.v(lambda: nc.vector.tensor_sub(tmq21[:, 0:64], tmq21[:, 0:64], gsc1[:, 0:64]))
            sp.v(lambda: nc.vector.tensor_mul(tmq21[:, 64:128], tmq1[:, 64:128], C_))
            sp.v(lambda: nc.vector.tensor_mul(gsc1[:, 0:64], tmq1[:, 0:64], Sn))
            sp.v(lambda: nc.vector.tensor_add(tmq21[:, 64:128], tmq21[:, 64:128], gsc1[:, 0:64]))
            sp.t(lambda: nc.tensor.transpose(P4[:, 0:128], tmq21[:], idf[:]))
            sp.v(lambda fm_ap=fm_ap: nc.vector.tensor_copy(fm_ap, P4[:, 0:128]))

        for kt in range(12):
            s0_ = kt // 3
            mblk = kt % 3
            for p0 in range(0, DT_, 8):
                sp.ld(xasb[:, p0 * 128:(p0 + 8) * 128],
                      KEYO[s0_ * D + p0 * 128:s0_ * D + (p0 + 8) * 128, mblk * 128:(mblk + 1) * 128]
                      .rearrange("(k p) j -> p k j", p=128))
                sp.ld(xasb2[:, p0 * 128:(p0 + 8) * 128],
                      KEYO[(4 + s0_) * D + p0 * 128:(4 + s0_) * D + (p0 + 8) * 128, mblk * 128:(mblk + 1) * 128]
                      .rearrange("(k p) j -> p k j", p=128))
            sp.v(lambda: nc.vector.tensor_sub(xasb2[:], xasb2[:], xasb[:]))
            sp.v(lambda: nc.vector.scalar_tensor_tensor(xasb[:], xasb2[:], smv1[:, OSM:OSM + 1], xasb[:], ALU.mult, ALU.add))
            rope_trig1(smv1[:, OKP + kt:OKP + kt + 1])
            for (Wt, bof, isk) in ((Wk1, OBK, True), (Wv1, OBV, False)):
                for m in range(DT_):
                    band_ld(sp, nc, wtile1, Wt, m, DT_, parts=2)
                    for k in range(DT_):
                        sp.t(lambda k=k, st=(k == 0), sp_=(k == DT_ - 1):
                             nc.tensor.matmul(P1[:, 0:128], wtile1[:, k * 128:(k + 1) * 128],
                                              xasb[:, k * 128:(k + 1) * 128], start=st, stop=sp_))
                    sp.v(lambda m=m, bof=bof: nc.vector.tensor_scalar_add(
                        kvsb[:, m * 128:(m + 1) * 128], P1[:, 0:128], smv1[:, bof + m:bof + m + 1]))
                if isk:
                    for h in range(NH1):
                        rope_tile1(kvsb[:, h * 128:(h + 1) * 128])
                        sp.s(lambda h=h: nc.scalar.activation(out=sq1[:, 0:128], in_=kvsb[:, h * 128:(h + 1) * 128], func=AF.Square))
                        sp.t(lambda: nc.tensor.matmul(PR2[0:1, 0:128], ones128[:, 0:1], sq1[:, 0:128], start=True, stop=True))
                        sp.v(lambda kt=kt: nc.vector.tensor_mul(sq1[0:1, 0:128], PR2[0:1, 0:128], kmr[0:1, kt * 128:(kt + 1) * 128]))
                        sp.v(lambda h=h: nc.vector.reduce_max(AUXg()[:, h:h + 1], sq1[0:1, 0:128], axis=AX.X))
                        sp.v(lambda h=h: nc.vector.tensor_max(KNMg()[:, h:h + 1], KNMg()[:, h:h + 1], AUXg()[:, h:h + 1]))
                    for k in range(DT_):
                        sp.st(Kscr[k * 128:(k + 1) * 128, kt * 128:(kt + 1) * 128], kvsb[:, k * 128:(k + 1) * 128])
                else:
                    for h in range(NH1):
                        sp.t(lambda h=h: nc.tensor.transpose(P4[:, 0:128], kvsb[:, h * 128:(h + 1) * 128], idf[:]))
                        sp.v(lambda: nc.vector.tensor_copy(vtt[:], P4[:, 0:128]))
                        sp.st(Vscr[kt * 128:(kt + 1) * 128, h * 128:(h + 1) * 128], vtt[:])
        sp.s(lambda: nc.scalar.activation(out=KNMg()[:, 0:16], in_=KNMg()[:, 0:16], func=AF.Sqrt))
        for ch in range(NTOK // NCH):
            t0 = ch * NCH
            for k in range(DT_):
                sp.ld(xnsb1[:, k * NCH:(k + 1) * NCH], HBxn1[k * 128:(k + 1) * 128, t0:t0 + NCH])
            for m in range(DT_):
                band_ld(sp, nc, wtile1, Wq1, m, DT_, parts=2)
                for k in range(DT_):
                    sp.t(lambda k=k, st=(k == 0), sp_=(k == DT_ - 1):
                         nc.tensor.matmul(P0[:, 0:NCH], wtile1[:, k * 128:(k + 1) * 128],
                                          xnsb1[:, k * NCH:(k + 1) * NCH], start=st, stop=sp_))
                sp.v(lambda m=m: nc.vector.tensor_scalar_add(qsb1[:, m * NCH:(m + 1) * NCH], P0[:, 0:NCH], smv1[:, OBQ + m:OBQ + m + 1]))
            for sti in range(NCH // 128):
                rope_trig1(smv1[:, OQP + ch * (NCH // 128) + sti:OQP + ch * (NCH // 128) + sti + 1])
                for h in range(NH1):
                    rope_tile1(qsb1[:, h * NCH + sti * 128:h * NCH + sti * 128 + 128])
            for h in range(NH1):
                sp.s(lambda h=h: nc.scalar.activation(out=sq1[:, 0:NCH], in_=qsb1[:, h * NCH:(h + 1) * NCH], func=AF.Square))
                sp.t(lambda: nc.tensor.matmul(PR2[0:1, 0:NCH], ones128[:, 0:1], sq1[:, 0:NCH], start=True, stop=True))
                sp.s(lambda: nc.scalar.activation(out=QNg()[:, 0:NCH], in_=PR2[0:1, 0:NCH], func=AF.Sqrt))
                sp.v(lambda h=h: nc.vector.tensor_scalar(QNg()[:, 0:NCH], QNg()[:, 0:NCH], KNMg()[0:1, h:h + 1], None, ALU.mult))
                sp.v(lambda: nc.vector.tensor_scalar_mul(QNg()[:, 0:NCH], QNg()[:, 0:NCH], -1.0 / np.sqrt(128.0)))
                sp.t(lambda: nc.tensor.matmul(P3[:, 0:NCH], ones1[:, 0:128], QNg()[:, 0:NCH], start=True, stop=True))
                sp.v(lambda: nc.vector.tensor_copy(cneg1[:, 0:NCH], P3[:, 0:NCH]))
                for kt in range(12):
                    sp.ld(ktile[:], Kscr[h * 128:(h + 1) * 128, kt * 128:(kt + 1) * 128])
                    sp.ld(vtt[:], Vscr[kt * 128:(kt + 1) * 128, h * 128:(h + 1) * 128])
                    sp.t(lambda h=h: nc.tensor.matmul(P1[:, 0:NCH], ktile[:], qsb1[:, h * NCH:(h + 1) * NCH], start=True, stop=True))
                    sp.v(lambda: nc.vector.scalar_tensor_tensor(expb1[:, 0:NCH], P1[:, 0:NCH], 1.0 / np.sqrt(128.0), cneg1[:, 0:NCH], ALU.mult, ALU.add))
                    sp.s(lambda: nc.scalar.activation(out=expb1[:, 0:NCH], in_=expb1[:, 0:NCH], func=AF.Exp))
                    sp.v(lambda kt=kt: nc.vector.tensor_scalar(expb1[:, 0:NCH], expb1[:, 0:NCH], smv1[:, OKM + kt:OKM + kt + 1], None, ALU.mult))
                    sp.t(lambda kt=kt: nc.tensor.matmul(PR2[0:1, 0:NCH], ones128[:, 0:1], expb1[:, 0:NCH], start=(kt == 0), stop=(kt == 11)))
                    sp.t(lambda kt=kt: nc.tensor.matmul(P2[:, 0:NCH], vtt[:], expb1[:, 0:NCH], start=(kt == 0), stop=(kt == 11)))
                sp.v(lambda: nc.vector.tensor_copy(CSg()[:, 0:NCH], PR2[0:1, 0:NCH]))
                sp.v(lambda: nc.vector.reciprocal(CSg()[:, 0:NCH], CSg()[:, 0:NCH]))
                sp.t(lambda: nc.tensor.matmul(P3[:, 0:NCH], ones1[:, 0:128], CSg()[:, 0:NCH], start=True, stop=True))
                sp.v(lambda: nc.vector.tensor_copy(cneg1[:, 0:NCH], P3[:, 0:NCH]))
                sp.v(lambda h=h: nc.vector.tensor_mul(avsb[:, h * NCH:(h + 1) * NCH], P2[:, 0:NCH], cneg1[:, 0:NCH]))
            for m in range(DT_):
                band_ld(sp, nc, wtile1, Wo1, m, DT_, parts=2)
                for k in range(DT_):
                    sp.t(lambda k=k, st=(k == 0), sp_=(k == DT_ - 1):
                         nc.tensor.matmul(P0[:, 0:NCH], wtile1[:, k * 128:(k + 1) * 128],
                                          avsb[:, k * NCH:(k + 1) * NCH], start=st, stop=sp_))
                sp.v(lambda m=m: nc.vector.tensor_scalar_add(expb1[:, 0:NCH], P0[:, 0:NCH], smv1[:, OBO + m:OBO + m + 1]))
                sp.st(HBs1[m * 128:(m + 1) * 128, t0:t0 + NCH], expb1[:, 0:NCH])

        AR.reset()
        TCG = 512
        xnsb2 = AR.al(DT_ * TCG)
        midb2 = AR.al(NMID * TCG)
        wtf = AR.al(DT_ * 128)
        gsc2 = AR.al(TCG)
        osc2 = AR.al(TCG)
        srow = VR[:, 0:128]

        def deq(W_, Sc_, m, k0, nk):
            sp.ld(srow, Sc_[0:1, m * 128:(m + 1) * 128])
            sp.t(lambda: nc.tensor.matmul(P3[:, 0:128], ones1[:, 0:128], srow, start=True, stop=True))
            hstep = min(nk, 8)
            for p0 in range(0, nk, hstep):
                n_ = min(hstep, nk - p0)
                sp.ld(wti[:, 0:n_ * 128],
                      W_[(k0 + p0) * 128:(k0 + p0 + n_) * 128, m * 128:(m + 1) * 128]
                      .rearrange("(k p) j -> p k j", p=128))
                sp.v(lambda n_=n_, p0=p0: nc.vector.tensor_copy(wtf[:, p0 * 128:(p0 + n_) * 128], wti[:, 0:n_ * 128]))
            sp.v(lambda nk=nk: nc.vector.tensor_scalar_add(wtf[:, 0:nk * 128], wtf[:, 0:nk * 128], -128.0))
            sp.v(lambda nk=nk: nc.vector.tensor_mul(
                wtf[:, 0:nk * 128].rearrange("p (k j) -> p k j", j=128),
                wtf[:, 0:nk * 128].rearrange("p (k j) -> p k j", j=128),
                P3[:, 0:128].unsqueeze(1).broadcast_to([128, nk, 128])))

        for ch in range(NTOK // TCG):
            t0 = ch * TCG
            for k in range(DT_):
                sp.ld(xnsb2[:, k * TCG:(k + 1) * TCG], HBxn1[k * 128:(k + 1) * 128, t0:t0 + TCG])
            for m in range(NMID):
                deq(Wqg, Sg, m, 0, DT_)
                for k in range(DT_):
                    sp.t(lambda k=k, st=(k == 0), sp_=(k == DT_ - 1):
                         nc.tensor.matmul(P0[:, 0:TCG], wtf[:, k * 128:(k + 1) * 128],
                                          xnsb2[:, k * TCG:(k + 1) * TCG], start=st, stop=sp_))
                sp.s(lambda: nc.scalar.activation(out=gsc2[:], in_=P0[:, 0:TCG], func=AF.Silu))
                deq(Wqu, Su, m, 0, DT_)
                for k in range(DT_):
                    sp.t(lambda k=k, st=(k == 0), sp_=(k == DT_ - 1):
                         nc.tensor.matmul(P0[:, 0:TCG], wtf[:, k * 128:(k + 1) * 128],
                                          xnsb2[:, k * TCG:(k + 1) * TCG], start=st, stop=sp_))
                sp.v(lambda m=m: nc.vector.tensor_mul(midb2[:, m * TCG:(m + 1) * TCG], gsc2[:], P0[:, 0:TCG]))
            for mo in range(DT_):
                for qtr in range(4):
                    k0 = qtr * 11
                    deq(Wqd, Sd, mo, k0, 11)
                    for k in range(11):
                        kk = k0 + k
                        sp.t(lambda k=k, kk=kk, st=(kk == 0), sp_=(kk == NMID - 1):
                             nc.tensor.matmul(P0[:, 0:TCG], wtf[:, k * 128:(k + 1) * 128],
                                              midb2[:, kk * TCG:(kk + 1) * TCG], start=st, stop=sp_))
                sp.s(lambda: nc.scalar.activation(out=gsc2[:], in_=P0[:, 0:TCG], func=AF.Silu))
                sp.ld(osc2[:], HBs1[mo * 128:(mo + 1) * 128, t0:t0 + TCG])
                sp.v(lambda: nc.vector.tensor_mul(gsc2[:], gsc2[:], osc2[:]))
                sp.ld(osc2[:], HBx1[mo * 128:(mo + 1) * 128, t0:t0 + TCG])
                sp.v(lambda: nc.vector.tensor_add(gsc2[:], gsc2[:], osc2[:]))
                sp.st(Oout[mo * 128:(mo + 1) * 128, t0:t0 + TCG], gsc2[:])

        sp.build(sem)
    return nc



def tok_rows(c):
    a, b = c % 4, c // 4
    rows = []
    for n in range(8):
        t0 = 128 * (a + 4 * n)
        rows.append(np.arange(b * T + t0, b * T + t0 + 128))
    return np.concatenate(rows)


def rope_perm():
    p = np.zeros(D, np.int64)
    for h in range(16):
        p[h * 128:h * 128 + 64] = h * 128 + np.arange(0, 128, 2)
        p[h * 128 + 64:(h + 1) * 128] = h * 128 + np.arange(1, 128, 2)
    return p


def prep_inputs(inputs):
    x = np.asarray(inputs["L0_attn_hidden_states"], np.float32).reshape(B * T, D)
    w_in = fm(inputs["L0_attn_in_proj_w"])
    w_dt = fm(inputs["L0_attn_dt_proj_w"])
    b_dt = colv(inputs["L0_attn_dt_proj_b"])
    g1 = colv(inputs["L0_attn_norm1_w"])
    wr = colv(inputs["L0_ffn_w_router"])
    alog = np.asarray(inputs["L0_attn_A_log"], np.float32)
    dpar = np.asarray(inputs["L0_attn_D_param"], np.float32)
    w_out = fm(inputs["L0_attn_out_proj_w"])
    w_g = fm(inputs["L0_attn_gate_w"])
    w_u = fm(inputs["L0_attn_up_w"])
    w_d = fm(inputs["L0_attn_down_w"])
    g2 = colv(inputs["L0_attn_norm2_w"])
    tt = np.arange(T)
    pos_all = np.stack([tt, tt], 0).astype(np.float32)
    own = (tt // 128) % 4
    lrow = 128 * ((tt // 128) // 4) + tt % 128
    grow_all = np.stack([own * 1024 + lrow, (4 + own) * 1024 + lrow], 0).astype(np.float32)
    def tm64(arr2):
        return np.ascontiguousarray(arr2.reshape(2, 32, 128).transpose(2, 0, 1).reshape(128, 64))
    postm = tm64(pos_all)
    growtm = tm64(grow_all)
    iotar = np.arange(NTOK, dtype=np.float32).reshape(1, NTOK)
    wqE = np.ascontiguousarray(np.asarray(inputs["L0_ffn_wq"], np.float32))
    wkE = np.ascontiguousarray(np.asarray(inputs["L0_ffn_wk"], np.float32))
    wvE = np.ascontiguousarray(np.asarray(inputs["L0_ffn_wv"], np.float32))
    woE = np.ascontiguousarray(np.asarray(inputs["L0_ffn_wo"], np.float32))
    wgE = np.ascontiguousarray(np.asarray(inputs["L0_ffn_w_gate"], np.float32))
    wuE = np.ascontiguousarray(np.asarray(inputs["L0_ffn_w_up"], np.float32))
    wdE = np.ascontiguousarray(np.asarray(inputs["L0_ffn_w_down"], np.float32))
    ln1 = colv(inputs["L0_ffn_ln1_w"]); ln2 = colv(inputs["L0_ffn_ln2_w"])
    bqE = colv(inputs["L0_ffn_bq"]); bkE = colv(inputs["L0_ffn_bk"], 8); bvE = colv(inputs["L0_ffn_bv"], 8)
    perm = rope_perm()
    wqt = fm(inputs["L1_attn_Wq"])[:, perm]
    wkt = fm(inputs["L1_attn_Wk"])[:, perm]
    wvt = fm(inputs["L1_attn_Wv"])
    wot = fm(inputs["L1_attn_Wo"])
    bq1 = colv(np.asarray(inputs["L1_attn_bq"], np.float32)[perm])
    bk1 = colv(np.asarray(inputs["L1_attn_bk"], np.float32)[perm])
    bv1 = colv(inputs["L1_attn_bv"])
    bo1 = colv(inputs["L1_attn_bo"])
    kpos = np.zeros(KPAD, np.float32)
    kmask = np.zeros(KPAD, np.float32)
    for ap_ in range(4):
        for mblk, gb in enumerate((ap_, 24 + ap_, 28 + ap_)):
            j0 = 384 * ap_ + 128 * mblk
            kpos[j0:j0 + 128] = 128 * gb + np.arange(128)
            if ap_ == 3 and mblk == 0:
                kmask[j0:j0 + 26] = 1.0
            else:
                kmask[j0:j0 + 128] = 1.0
    kposc = np.ascontiguousarray(kpos.reshape(12, 128).T)
    kmaskc = np.ascontiguousarray(kmask.reshape(12, 128).T)
    kmaskr = np.ascontiguousarray(kmask.reshape(1, KPAD))
    wqg = np.ascontiguousarray(np.asarray(inputs["L1_ffn_wq_gate"]).T.astype(np.int32))
    wqu = np.ascontiguousarray(np.asarray(inputs["L1_ffn_wq_up"]).T.astype(np.int32))
    wqd = np.ascontiguousarray(np.asarray(inputs["L1_ffn_wq_down"]).T.astype(np.int32))
    sg = np.asarray(inputs["L1_ffn_s_gate"], np.float32).reshape(1, 5632)
    su = np.asarray(inputs["L1_ffn_s_up"], np.float32).reshape(1, 5632)
    sd = np.asarray(inputs["L1_ffn_s_down"], np.float32).reshape(1, D)

    maps = []
    for c in range(N_CORES):
        a, b = c % 4, c // 4
        rows = tok_rows(c)
        alog_sl = np.ascontiguousarray(
            alog[256 * c:256 * c + 256].reshape(2, 128, 32).transpose(1, 0, 2).reshape(128, 64))
        dp_sl = colv(dpar[256 * c:256 * c + 256], 2)
        cofs = np.full((128, 1), -1024.0 * c, np.float32)
        onehot = np.zeros((1, 8), np.float32); onehot[0, c] = 1.0
        myslot = (128 * c + np.arange(128)).astype(np.int32).reshape(128, 1)
        myrows = (b * 32 + a + 4 * np.arange(8)).astype(np.int32).reshape(8, 1)
        q0 = 128 * c
        kl = np.arange(128)[:, None]
        kt_ql = np.arange(8 * 128)[None, :]
        kglob = (kt_ql // 128) * 128 + kl
        qglob = q0 + (kt_ql % 128)
        cm = (kglob <= qglob).astype(np.float32)
        qpos = np.ascontiguousarray(
            (128.0 * (a + 4 * np.arange(8))[None, :] + np.arange(128)[:, None]).astype(np.float32))
        smask = np.full((128, 1), float(b), np.float32)
        maps.append({
            "x": fm(x[rows]), "w_in": w_in, "w_dt": w_dt, "b_dt": b_dt, "g1": g1, "wr": wr,
            "alog_sl": alog_sl, "dp_sl": dp_sl,
            "w_out": w_out, "w_gate": w_g, "w_up": w_u, "w_down": w_d, "g2": g2,
            "postm": postm, "growtm": growtm, "iotar": iotar, "cofs": cofs,
            "onehot": onehot, "myslot": myslot, "myrows": myrows,
            "wq": wqE, "wk": wkE, "wv": wvE, "wo": woE, "wg": wgE, "wu": wuE, "wd": wdE,
            "ln1": ln1, "ln2": ln2, "bqE": bqE, "bkE": bkE, "bvE": bvE, "cmask": cm,
            "wqt": wqt, "wkt": wkt, "wvt": wvt, "wot": wot,
            "bq1": bq1, "bk1": bk1, "bv1": bv1, "bo1": bo1,
            "qpos": qpos, "kpos": kposc, "kmask": kmaskc, "kmaskr": kmaskr, "smask": smask,
            "wqg": wqg, "wqu": wqu, "wqd": wqd, "sg": sg, "su": su, "sd": sd,
        })
    return maps


def kernel(**inputs):
    nc = build_fused()
    maps = prep_inputs(inputs)
    res = run_bass_kernel_spmd(nc, maps, core_ids=list(range(N_CORES))).results
    out = np.empty((B * T, D), np.float32)
    for c in range(N_CORES):
        o = np.ascontiguousarray(res[c]["out"].T)
        out[tok_rows(c)] = o
    return np.ascontiguousarray(out.reshape(B, T, D))


# version 05
# speedup vs baseline: 1.0501x; 1.0501x over previous
import numpy as np
from contextlib import ExitStack
import concourse.bass as bass
import concourse.mybir as mybir
from concourse.bass import IndirectOffsetOnAxis
from concourse.bass_utils import run_bass_kernel_spmd
from concourse.aot_env import allow_ici_all_to_all

F32 = mybir.dt.float32
I32 = mybir.dt.int32
AF = mybir.ActivationFunctionType
ALU = mybir.AluOpType
AX = mybir.AxisListType

B, T, D = 2, 4096, 2048
N_CORES = 8
NTOK = 1024
DT_ = 16
NCH = 512
PROJ = 5248
LN_EPS = 1e-5
L0_EPS = 1e-5
MOD_EPS = 1e-6
TCB = 256
TCC = 256
NMID = 44
KPAD = 1536
NS = 32
ROPE_C1 = float(np.float32(2 * np.pi))
ROPE_C2 = float(np.float64(2 * np.pi) - np.float32(2 * np.pi))

G8 = [list(range(8))]


def fm(a2d):
    return np.ascontiguousarray(np.asarray(a2d).T.astype(np.float32, copy=False))


def colv(vec, nt=None):
    v = np.asarray(vec).astype(np.float32, copy=False)
    n = v.shape[0] // 128 if nt is None else nt
    return np.ascontiguousarray(v.reshape(n, 128).T)


class SP:

    def __init__(self, nc):
        self.nc = nc
        self.ops = []

    def ld(self, out, in_):
        self.ops.append(("sync", (lambda o=out, i=in_: self.nc.sync.dma_start(out=o, in_=i)), 16))

    def mark(self):
        return len(self.ops)

    def ld_early(self, out, in_, wait_mark):
        self.ops.append(("sync", (lambda o=out, i=in_: self.nc.sync.dma_start(out=o, in_=i)),
                         0, ("early", wait_mark)))
        self.n_early = getattr(self, "n_early", 0) + 1
        return self.n_early

    def wait2(self, eng_name, n_loads):
        self.ops.append((eng_name, None, 0, ("wait2", 16 * n_loads)))

    def st(self, out, in_):
        self.ops.append(("gpsimd", (lambda o=out, i=in_: self.nc.gpsimd.dma_start(out=o, in_=i)), 16))

    def v(self, fn):
        self.ops.append(("vector", fn, 1))

    def s(self, fn):
        self.ops.append(("scalar", fn, 1))

    def t(self, fn):
        self.ops.append(("tensor", fn, 1))

    def g(self, fn, inc=1):
        self.ops.append(("gpsimd", fn, inc))

    def build(self, sem, sem2=None):
        ops = [(o + (None,) if len(o) == 3 else o) for o in self.ops]
        pre = []
        tot = 0
        for (_, _, inc, _) in ops:
            pre.append(tot)
            tot += inc
        tot2 = 16 * getattr(self, "n_early", 0)
        nc = self.nc
        with nc.Block() as block:
            def mk(name):
                def body(eng):
                    for i, (e, f, inc, mode) in enumerate(ops):
                        if e != name:
                            continue
                        if mode is None:
                            if pre[i] > 0:
                                eng.wait_ge(sem, pre[i])
                            f().then_inc(sem, inc)
                        elif mode[0] == "early":
                            wv = pre[mode[1]]
                            if wv > 0:
                                eng.wait_ge(sem, wv)
                            f().then_inc(sem2, 16)
                        else:
                            if mode[1] > 0:
                                eng.wait_ge(sem2, mode[1])
                    if name == "gpsimd":
                        eng.wait_ge(sem, tot)
                        if tot2 > 0:
                            eng.wait_ge(sem2, tot2)
                return body
            block.sync(mk("sync"))
            block.vector(mk("vector"))
            block.scalar(mk("scalar"))
            block.tensor(mk("tensor"))
            block.gpsimd(mk("gpsimd"))


class Arena:
    def __init__(self, ap, lo, hi):
        self.ap = ap
        self.lo = lo
        self.hi = hi
        self.off = lo

    def al(self, w):
        a = self.ap[:, self.off:self.off + w]
        self.off += w
        assert self.off <= self.hi, (self.off, self.hi)
        return a

    def reset(self):
        self.off = self.lo


def band_ld(sp, nc, wtile, W, m, nk, parts=2):
    step = nk // parts
    for p0 in range(0, nk, step):
        sp.ld(wtile[:, p0 * 128:(p0 + step) * 128],
              W[p0 * 128:(p0 + step) * 128, m * 128:(m + 1) * 128]
              .rearrange("(k p) j -> p k j", p=128))


def band_ld_early(sp, nc, wtile, W, m, nk, wait_mark, parts=2):
    step = nk // parts
    n = 0
    for p0 in range(0, nk, step):
        n = sp.ld_early(wtile[:, p0 * 128:(p0 + step) * 128],
                        W[p0 * 128:(p0 + step) * 128, m * 128:(m + 1) * 128]
                        .rearrange("(k p) j -> p k j", p=128), wait_mark)
    return n


def build_fused():
    nc = bass.Bass("TRN2", target_bir_lowering=False, debug=False)
    Xin = nc.dram_tensor("x", [D, NTOK], F32, kind="ExternalInput").ap()
    Win = nc.dram_tensor("w_in", [D, PROJ], F32, kind="ExternalInput").ap()
    Wdt = nc.dram_tensor("w_dt", [128, D], F32, kind="ExternalInput").ap()
    Bdt = nc.dram_tensor("b_dt", [128, DT_], F32, kind="ExternalInput").ap()
    G1v = nc.dram_tensor("g1", [128, DT_], F32, kind="ExternalInput").ap()
    WR = nc.dram_tensor("wr", [128, DT_], F32, kind="ExternalInput").ap()
    Ial = nc.dram_tensor("alog_sl", [128, 64], F32, kind="ExternalInput").ap()
    Idp = nc.dram_tensor("dp_sl", [128, 2], F32, kind="ExternalInput").ap()
    Wo0 = nc.dram_tensor("w_out", [D, D], F32, kind="ExternalInput").ap()
    Wg0 = nc.dram_tensor("w_gate", [D, 5632], F32, kind="ExternalInput").ap()
    Wu0 = nc.dram_tensor("w_up", [D, 5632], F32, kind="ExternalInput").ap()
    Wd0 = nc.dram_tensor("w_down", [5632, D], F32, kind="ExternalInput").ap()
    G2v = nc.dram_tensor("g2", [128, DT_], F32, kind="ExternalInput").ap()
    Ipostm = nc.dram_tensor("postm", [128, 64], F32, kind="ExternalInput").ap()
    Igrowtm = nc.dram_tensor("growtm", [128, 64], F32, kind="ExternalInput").ap()
    Iiota = nc.dram_tensor("iotar", [1, NTOK], F32, kind="ExternalInput").ap()
    Icofs = nc.dram_tensor("cofs", [128, 1], F32, kind="ExternalInput").ap()
    Ioneh = nc.dram_tensor("onehot", [1, 8], F32, kind="ExternalInput").ap()
    Imyslot = nc.dram_tensor("myslot", [128, 1], I32, kind="ExternalInput").ap()
    Imyrows = nc.dram_tensor("myrows", [8, 1], I32, kind="ExternalInput").ap()
    WqE = nc.dram_tensor("wq", [D, D], F32, kind="ExternalInput").ap()
    WkE = nc.dram_tensor("wk", [D, 1024], F32, kind="ExternalInput").ap()
    WvE = nc.dram_tensor("wv", [D, 1024], F32, kind="ExternalInput").ap()
    WoE = nc.dram_tensor("wo", [D, D], F32, kind="ExternalInput").ap()
    WgE = nc.dram_tensor("wg", [D, 8192], F32, kind="ExternalInput").ap()
    WuE = nc.dram_tensor("wu", [D, 8192], F32, kind="ExternalInput").ap()
    WdE = nc.dram_tensor("wd", [8192, D], F32, kind="ExternalInput").ap()
    Vln1 = nc.dram_tensor("ln1", [128, DT_], F32, kind="ExternalInput").ap()
    Vln2 = nc.dram_tensor("ln2", [128, DT_], F32, kind="ExternalInput").ap()
    VbqE = nc.dram_tensor("bqE", [128, DT_], F32, kind="ExternalInput").ap()
    VbkE = nc.dram_tensor("bkE", [128, 8], F32, kind="ExternalInput").ap()
    VbvE = nc.dram_tensor("bvE", [128, 8], F32, kind="ExternalInput").ap()
    Icm = nc.dram_tensor("cmask", [128, 8 * 128], F32, kind="ExternalInput").ap()
    Wq1 = nc.dram_tensor("wqt", [D, D], F32, kind="ExternalInput").ap()
    Wk1 = nc.dram_tensor("wkt", [D, D], F32, kind="ExternalInput").ap()
    Wv1 = nc.dram_tensor("wvt", [D, D], F32, kind="ExternalInput").ap()
    Wo1 = nc.dram_tensor("wot", [D, D], F32, kind="ExternalInput").ap()
    Vbq1 = nc.dram_tensor("bq1", [128, DT_], F32, kind="ExternalInput").ap()
    Vbk1 = nc.dram_tensor("bk1", [128, DT_], F32, kind="ExternalInput").ap()
    Vbv1 = nc.dram_tensor("bv1", [128, DT_], F32, kind="ExternalInput").ap()
    Vbo1 = nc.dram_tensor("bo1", [128, DT_], F32, kind="ExternalInput").ap()
    Iqp = nc.dram_tensor("qpos", [128, 8], F32, kind="ExternalInput").ap()
    Ikp = nc.dram_tensor("kpos", [128, 12], F32, kind="ExternalInput").ap()
    Ikm = nc.dram_tensor("kmask", [128, 12], F32, kind="ExternalInput").ap()
    Ikmr = nc.dram_tensor("kmaskr", [1, KPAD], F32, kind="ExternalInput").ap()
    Ismask = nc.dram_tensor("smask", [128, 1], F32, kind="ExternalInput").ap()
    Wqg = nc.dram_tensor("wqg", [D, 5632], I32, kind="ExternalInput").ap()
    Wqu = nc.dram_tensor("wqu", [D, 5632], I32, kind="ExternalInput").ap()
    Wqd = nc.dram_tensor("wqd", [5632, D], I32, kind="ExternalInput").ap()
    Sg = nc.dram_tensor("sg", [1, 5632], F32, kind="ExternalInput").ap()
    Su = nc.dram_tensor("su", [1, 5632], F32, kind="ExternalInput").ap()
    Sd = nc.dram_tensor("sd", [1, D], F32, kind="ExternalInput").ap()
    Oout = nc.dram_tensor("out", [D, NTOK], F32, kind="ExternalOutput").ap()
    HBxn0 = nc.dram_tensor("hb_xn0", [D, NTOK], F32).ap()
    HBgb = nc.dram_tensor("hb_gb", [D, NTOK], F32).ap()
    HBz = nc.dram_tensor("hb_z", [D, NTOK], F32).ap()
    XN0TM = nc.dram_tensor("xn0_tm", [1025, D], F32).ap()
    A2AI = nc.dram_tensor("a2a_in", [8 * 640, NTOK], F32).ap()
    A2AO = nc.dram_tensor("a2a_out", [8 * 640, NTOK], F32).ap()
    SCI = nc.dram_tensor("sc_in", [1, NTOK], F32).ap()
    SCO = nc.dram_tensor("sc_out", [8, NTOK], F32).ap()
    YI = nc.dram_tensor("y_in", [8 * 256, NTOK], F32).ap()
    YO = nc.dram_tensor("y_out", [8 * 256, NTOK], F32).ap()
    CANS = nc.dram_tensor("canon_slot", [2, T], F32).ap()
    CANSC = nc.dram_tensor("canon_sc", [2, T], F32).ap()
    BYSLOT = nc.dram_tensor("byslot", [4, NTOK], F32).ap()
    SELP = nc.dram_tensor("selpart", [NTOK, D], F32).ap()
    SELTM = nc.dram_tensor("sel_tm", [NTOK, D], F32).ap()
    GATI = nc.dram_tensor("gat_in", [128, D], F32).ap()
    GATO = nc.dram_tensor("gat_out", [NTOK, D], F32).ap()
    HBs0 = nc.dram_tensor("hb_s0", [D, NTOK], F32).ap()
    KEYI = nc.dram_tensor("keys_in", [D, 384], F32).ap()
    KEYO = nc.dram_tensor("keys_out", [8 * D, 384], F32).ap()
    HBx1 = nc.dram_tensor("hb_x1", [D, NTOK], F32).ap()
    HBxn1 = nc.dram_tensor("hb_xn1", [D, NTOK], F32).ap()
    HBs1 = nc.dram_tensor("hb_s1", [D, NTOK], F32).ap()
    Kscr = nc.dram_tensor("kscr", [D, KPAD], F32).ap()
    Vscr = nc.dram_tensor("vscr", [KPAD, D], F32).ap()

    sp = SP(nc)
    with ExitStack() as es:
        E = es.enter_context
        AR_t = E(nc.sbuf_tensor("AR", [128, 49152], F32))
        VR = E(nc.sbuf_tensor("VR", [1, 1352], F32))
        MR = E(nc.sbuf_tensor("MR", [8, 1152], F32))
        IR = E(nc.sbuf_tensor("IR", [128, 1152], I32))
        P0 = E(nc.psum_tensor("P0", [128, 512], F32))
        P1 = E(nc.psum_tensor("P1", [128, 512], F32))
        P2 = E(nc.psum_tensor("P2", [128, 512], F32))
        P3 = E(nc.psum_tensor("P3", [128, 512], F32))
        P4 = E(nc.psum_tensor("P4", [128, 512], F32))
        PR = E(nc.psum_tensor("PR", [1, 512], F32))
        PR2 = E(nc.psum_tensor("PR2", [1, 512], F32))
        sem = E(nc.semaphore("gs"))
        sem2 = E(nc.semaphore("g2s"))

        AR = Arena(AR_t, 0, 48600)
        PERS = Arena(AR_t, 48600, 49152)
        idf = PERS.al(128)
        ones128 = PERS.al(1)
        csb = PERS.al(8)
        posk8 = PERS.al(8)
        grow8 = PERS.al(8)
        sco8 = PERS.al(8)
        slotf = PERS.al(8)
        posq = PERS.al(1)
        t8a = PERS.al(8)
        t8b = PERS.al(8)
        t8c = PERS.al(8)
        ones1 = VR[:, 1096:1224]
        scqrow = VR[:, 1224:1352]
        wti = IR[:, 0:1024]
        pii = IR[:, 1024:1025]

        fiiI = IR[:, 0:128]
        sp.g(lambda: nc.gpsimd.iota(pii[:], pattern=[[0, 1]], base=0, channel_multiplier=1))
        sp.g(lambda: nc.gpsimd.iota(fiiI[:], pattern=[[1, 128]], base=0, channel_multiplier=0))
        tmpf = AR.al(128)
        sp.v(lambda: nc.vector.tensor_copy(tmpf[:, 0:1], pii[:]))
        sp.v(lambda: nc.vector.tensor_copy(idf[:], fiiI[:]))
        sp.v(lambda: nc.vector.tensor_scalar(idf[:], idf[:], tmpf[:, 0:1], None, ALU.is_equal))
        sp.v(lambda: nc.vector.memset(ones128[:], 1.0))
        sp.v(lambda: nc.vector.memset(ones1[:], 1.0))
        sp.v(lambda: nc.vector.memset(csb[:, 0:1], LN_EPS))
        sp.v(lambda: nc.vector.memset(csb[:, 1:2], 1.0))
        sp.v(lambda: nc.vector.memset(csb[:, 2:3], -1.0))
        sp.v(lambda: nc.vector.memset(csb[:, 3:4], MOD_EPS))
        sp.v(lambda: nc.vector.memset(csb[:, 4:5], -np.log(10000.0) / 64.0))
        AR.reset()

        selidxI = IR[:, 0:8]
        gidxI = IR[:, 8:16]
        myslotI = IR[:, 16:17]
        myrowsI = IR[0:8, 17:18]
        nintI = IR[:, 18:82]
        ifriI = IR[0:1, 82:146]

        def ln_stats(src_sb, ntok, nt, s1, s2, sq, base=0):
            nch = max(1, ntok // NCH)
            w = min(ntok, NCH)
            for ch in range(nch):
                for k in range(nt):
                    c0 = base + k * ntok + ch * w
                    sp.t(lambda a=PR, c=src_sb, c0=c0, st=(k == 0), sp_=(k == nt - 1), w=w:
                         nc.tensor.matmul(a[:, 0:w], ones128[:, 0:1], c[:, c0:c0 + w], start=st, stop=sp_))
                sp.v(lambda a=s1, ch=ch, w=w: nc.vector.tensor_copy(a[:, ch * w:(ch + 1) * w], PR[0:1, 0:w]))
                for k in range(nt):
                    c0 = base + k * ntok + ch * w
                    sp.s(lambda a=sq, b=src_sb, c0=c0, w=w: nc.scalar.activation(out=a[:, 0:w], in_=b[:, c0:c0 + w], func=AF.Square))
                    sp.t(lambda a=PR, b=sq, st=(k == 0), sp_=(k == nt - 1), w=w:
                         nc.tensor.matmul(a[:, 0:w], ones128[:, 0:1], b[:, 0:w], start=st, stop=sp_))
                sp.v(lambda a=s2, ch=ch, w=w: nc.vector.tensor_copy(a[:, ch * w:(ch + 1) * w], PR[0:1, 0:w]))

        def ln_stats2(src_sb, ntok, nt, s1, s2, sq, base=0, stride=None):
            if stride is None:
                stride = ntok
            for k in range(nt):
                c0 = base + k * stride
                sp.t(lambda c0=c0, st=(k == 0), sp_=(k == nt - 1):
                     nc.tensor.matmul(PR[0:1, 0:ntok], ones128[:, 0:1], src_sb[:, c0:c0 + ntok], start=st, stop=sp_))
            sp.v(lambda: nc.vector.tensor_copy(s1[:, 0:ntok], PR[0:1, 0:ntok]))
            for k in range(nt):
                c0 = base + k * stride
                sp.s(lambda c0=c0: nc.scalar.activation(out=sq[:, 0:ntok], in_=src_sb[:, c0:c0 + ntok], func=AF.Square))
                sp.t(lambda st=(k == 0), sp_=(k == nt - 1):
                     nc.tensor.matmul(PR[0:1, 0:ntok], ones128[:, 0:1], sq[:, 0:ntok], start=st, stop=sp_))
            sp.v(lambda: nc.vector.tensor_copy(s2[:, 0:ntok], PR[0:1, 0:ntok]))

        def rstd_from_s2(s2, rstd, dinv, eps_ap, ntok):
            sp.v(lambda: nc.vector.tensor_scalar_mul(s2[:, 0:ntok], s2[:, 0:ntok], dinv))
            sp.s(lambda: nc.scalar.activation(out=rstd[:, 0:ntok], in_=s2[:, 0:ntok], func=AF.Sqrt, bias=eps_ap))
            sp.v(lambda: nc.vector.reciprocal(rstd[:, 0:ntok], rstd[:, 0:ntok]))

        def bcast_row(ps, vec, c0, n):
            sp.t(lambda ps=ps, vec=vec, c0=c0, n=n: nc.tensor.matmul(
                ps[:, 0:n], ones1[:, 0:128], vec[:, c0:c0 + n], start=True, stop=True))

        xsb = AR.al(DT_ * NTOK)
        xnsb = AR.al(DT_ * NTOK)
        r1sb = xsb
        drsb = AR.al(NTOK)
        dtwsb = AR.al(D)
        wtileA = AR.al(DT_ * 128)
        tmb = AR.al(D)
        sq_ = AR.al(NCH)
        outsb = AR.al(NCH)
        t1_ = AR.al(NCH)
        t2_ = AR.al(NCH)
        smallv = AR.al(3 * DT_)
        vrowA = AR.al(6 * NTOK)
        S1 = lambda: vrowA[0:1, 0:NTOK]
        S2 = lambda: vrowA[0:1, NTOK:2 * NTOK]
        MU = lambda: vrowA[0:1, 2 * NTOK:3 * NTOK]
        RSTD = lambda: vrowA[0:1, 3 * NTOK:4 * NTOK]
        NMR = lambda: vrowA[0:1, 4 * NTOK:5 * NTOK]
        SCr = lambda: vrowA[0:1, 5 * NTOK:6 * NTOK]

        for k in range(DT_):
            sp.ld(xsb[:, k * NTOK:(k + 1) * NTOK], Xin[k * 128:(k + 1) * 128, :])
        sp.ld(dtwsb[:], Wdt[:])
        sp.ld(smallv[:, 0:DT_], Bdt[:])
        sp.ld(smallv[:, DT_:2 * DT_], G1v[:])
        sp.ld(smallv[:, 2 * DT_:3 * DT_], WR[:])

        ln_stats(xsb, NTOK, DT_, S1(), S2(), sq_)
        dinv = 1.0 / D
        sp.v(lambda: nc.vector.tensor_scalar_mul(MU(), S1(), dinv))
        sp.v(lambda: nc.vector.tensor_mul(S1(), MU(), MU()))
        sp.v(lambda: nc.vector.tensor_scalar_mul(S2(), S2(), dinv))
        sp.v(lambda: nc.vector.tensor_sub(S2(), S2(), S1()))
        sp.s(lambda: nc.scalar.activation(out=RSTD(), in_=S2(), func=AF.Sqrt, bias=csb[0:1, 0:1]))
        sp.v(lambda: nc.vector.reciprocal(RSTD(), RSTD()))
        sp.v(lambda: nc.vector.tensor_mul(NMR(), MU(), RSTD()))
        sp.v(lambda: nc.vector.tensor_scalar_mul(NMR(), NMR(), -1.0))
        for ch in range(NTOK // NCH):
            bcast_row(P1, RSTD(), ch * NCH, NCH)
            bcast_row(P2, NMR(), ch * NCH, NCH)
            for k in range(DT_):
                c0 = k * NTOK + ch * NCH
                sp.v(lambda c0=c0: nc.vector.tensor_mul(t1_[:, 0:NCH], xsb[:, c0:c0 + NCH], P1[:, 0:NCH]))
                sp.v(lambda c0=c0: nc.vector.tensor_add(xnsb[:, c0:c0 + NCH], t1_[:, 0:NCH], P2[:, 0:NCH]))
        ln_stats(xnsb, NTOK, DT_, S1(), S2(), sq_)
        rstd_from_s2(S2(), RSTD(), dinv, csb[0:1, 0:1], NTOK)
        for ch in range(NTOK // NCH):
            bcast_row(P1, RSTD(), ch * NCH, NCH)
            for k in range(DT_):
                c0 = k * NTOK + ch * NCH
                sp.v(lambda c0=c0, k=k: nc.vector.scalar_tensor_tensor(
                    r1sb[:, c0:c0 + NCH], xnsb[:, c0:c0 + NCH], smallv[:, DT_ + k:DT_ + k + 1],
                    P1[:, 0:NCH], ALU.mult, ALU.mult))
        for ch in range(NTOK // NCH):
            for k in range(DT_):
                c0 = k * NTOK + ch * NCH
                sp.t(lambda c0=c0, k=k, st=(k == 0), sp_=(k == DT_ - 1):
                     nc.tensor.matmul(PR[0:1, 0:NCH], smallv[:, 2 * DT_ + k:2 * DT_ + k + 1],
                                      xnsb[:, c0:c0 + NCH], start=st, stop=sp_))
            sp.v(lambda ch=ch: nc.vector.tensor_copy(SCr()[:, ch * NCH:(ch + 1) * NCH], PR[0:1, 0:NCH]))
        sp.st(SCI[:], SCr())
        for k in range(DT_):
            sp.st(HBxn0[k * 128:(k + 1) * 128, :], xnsb[:, k * NTOK:(k + 1) * NTOK])
        for k in range(DT_):
            for ch in range(NTOK // NCH):
                c0 = k * NTOK + ch * NCH
                sp.s(lambda c0=c0: nc.scalar.activation(out=outsb[:, 0:NCH], in_=xnsb[:, c0:c0 + NCH], func=AF.Silu))
                sp.st(HBgb[k * 128:(k + 1) * 128, ch * NCH:(ch + 1) * NCH], outsb[:, 0:NCH])
        for lk in range(8):
            for ft in range(DT_):
                sp.t(lambda lk=lk, ft=ft: nc.tensor.transpose(
                    P4[:, 0:128], xnsb[:, ft * NTOK + lk * 128:ft * NTOK + lk * 128 + 128], idf[:]))
                sp.v(lambda ft=ft: nc.vector.tensor_copy(tmb[:, ft * 128:(ft + 1) * 128], P4[:, 0:128]))
            sp.st(XN0TM[lk * 128:(lk + 1) * 128, :], tmb[:])
        sp.v(lambda: nc.vector.memset(tmb[0:1, :], 0.0))
        sp.st(XN0TM[1024:1025, :], tmb[0:1, 0:D])
        for m in range(PROJ // 128):
            for k in range(DT_):
                sp.ld(wtileA[:, k * 128:(k + 1) * 128], Win[k * 128:(k + 1) * 128, m * 128:(m + 1) * 128])
            for ch in range(NTOK // NCH):
                for k in range(DT_):
                    c0 = k * NTOK + ch * NCH
                    sp.t(lambda c0=c0, k=k, st=(k == 0), sp_=(k == DT_ - 1):
                         nc.tensor.matmul(P0[:, 0:NCH], wtileA[:, k * 128:(k + 1) * 128],
                                          r1sb[:, c0:c0 + NCH], start=st, stop=sp_))
                if m == 40:
                    sp.v(lambda ch=ch: nc.vector.tensor_copy(drsb[:, ch * NCH:(ch + 1) * NCH], P0[:, 0:NCH]))
                else:
                    sp.v(lambda: nc.vector.tensor_copy(outsb[:, 0:NCH], P0[:, 0:NCH]))
                    cs = slice(ch * NCH, (ch + 1) * NCH)
                    if m < 16:
                        sp.st(HBz[m * 128:(m + 1) * 128, cs], outsb[:, 0:NCH])
                    elif m < 20:
                        e0 = 2 * (m - 16)
                        sp.st(A2AI[e0 * 640 + 512:e0 * 640 + 576, cs], outsb[0:64, 0:NCH])
                        sp.st(A2AI[(e0 + 1) * 640 + 512:(e0 + 1) * 640 + 576, cs], outsb[64:128, 0:NCH])
                    elif m < 24:
                        e0 = 2 * (m - 20)
                        sp.st(A2AI[e0 * 640 + 576:e0 * 640 + 640, cs], outsb[0:64, 0:NCH])
                        sp.st(A2AI[(e0 + 1) * 640 + 576:(e0 + 1) * 640 + 640, cs], outsb[64:128, 0:NCH])
                    else:
                        e0 = (m - 24) // 2
                        ro = 256 + 128 * ((m - 24) % 2)
                        sp.st(A2AI[e0 * 640 + ro:e0 * 640 + ro + 128, cs], outsb[:, 0:NCH])
        for m2 in range(DT_):
            e0 = m2 // 2
            ro = 128 * (m2 % 2)
            for ch in range(NTOK // NCH):
                sp.t(lambda m2=m2, ch=ch: nc.tensor.matmul(
                    P0[:, 0:NCH], dtwsb[:, m2 * 128:(m2 + 1) * 128],
                    drsb[:, ch * NCH:(ch + 1) * NCH], start=True, stop=True))
                sp.v(lambda m2=m2: nc.vector.tensor_scalar_add(outsb[:, 0:NCH], P0[:, 0:NCH], smallv[:, m2:m2 + 1]))
                sp.s(lambda: nc.scalar.activation(out=t1_[:, 0:NCH], in_=outsb[:, 0:NCH], func=AF.Abs))
                sp.s(lambda: nc.scalar.activation(out=t1_[:, 0:NCH], in_=t1_[:, 0:NCH], func=AF.Exp, scale=csb[:, 2:3]))
                sp.s(lambda: nc.scalar.activation(out=t1_[:, 0:NCH], in_=t1_[:, 0:NCH], func=AF.Ln, bias=csb[:, 1:2]))
                sp.s(lambda: nc.scalar.activation(out=t2_[:, 0:NCH], in_=outsb[:, 0:NCH], func=AF.Relu))
                sp.v(lambda: nc.vector.tensor_add(t2_[:, 0:NCH], t2_[:, 0:NCH], t1_[:, 0:NCH]))
                sp.st(A2AI[e0 * 640 + ro:e0 * 640 + ro + 128, ch * NCH:(ch + 1) * NCH], t2_[:, 0:NCH])

        sp.g(lambda: nc.gpsimd.collective_compute(
            "AllGather", ALU.bypass, replica_groups=G8, ins=[SCI], outs=[SCO]))
        def _a2a1():
            with allow_ici_all_to_all():
                return nc.gpsimd.collective_compute(
                    "AllToAll", ALU.bypass, replica_groups=G8, ins=[A2AI], outs=[A2AO])
        sp.g(_a2a1)

        AR.reset()
        mda = AR.al(NS * TCB)
        ub = AR.al(NS * TCB)
        hbt = AR.al(NS * TCB)
        cr = AR.al(NS * TCB)
        dtc = AR.al(TCB)
        xrc = AR.al(TCB)
        usb2 = AR.al(TCB)
        ysb2 = AR.al(TCB)
        asb = AR.al(64)
        dpsb = AR.al(2)
        stt = AR.al(NS)
        sp.ld(asb[:], Ial[:])
        sp.ld(dpsb[:], Idp[:])
        sp.s(lambda: nc.scalar.activation(out=asb[:], in_=asb[:], func=AF.Exp))
        sp.v(lambda: nc.vector.tensor_scalar_mul(asb[:], asb[:], -1.0))
        for b_ in range(2):
            for h in range(2):
                A_t = asb[:, h * NS:(h + 1) * NS]
                sp.v(lambda: nc.vector.memset(stt[:], 0.0))
                for cch in range(T // TCB):
                    rs_ = []
                    for q2 in range(2):
                        gg = 2 * cch + q2
                        r = 4 * b_ + gg % 4
                        n = gg // 4
                        rs_.append((r, n))
                        sp.ld(dtc[:, q2 * 128:(q2 + 1) * 128],
                              A2AO[r * 640 + 128 * h:r * 640 + 128 * h + 128, 128 * n:128 * n + 128])
                        sp.ld(xrc[:, q2 * 128:(q2 + 1) * 128],
                              A2AO[r * 640 + 512 + 32 * h:r * 640 + 512 + 32 * h + 32, 128 * n:128 * n + 128]
                              .unsqueeze(0).broadcast_to([4, 32, 128]))
                    sp.v(lambda: nc.vector.tensor_mul(usb2[:], dtc[:], xrc[:]))
                    sp.v(lambda A_t=A_t: nc.vector.tensor_mul(
                        mda[:].rearrange("p (n t) -> p n t", n=NS),
                        dtc[:].unsqueeze(1).broadcast_to([128, NS, TCB]),
                        A_t.unsqueeze(2).broadcast_to([128, NS, TCB])))
                    sp.s(lambda: nc.scalar.activation(out=mda[:], in_=mda[:], func=AF.Exp))
                    for q2, (r, n) in enumerate(rs_):
                        sp.ld(ub[:].rearrange("p (n t) -> p n t", n=NS)[:, :, q2 * 128:(q2 + 1) * 128],
                              A2AO[r * 640 + 576 + 32 * h:r * 640 + 576 + 32 * h + 32, 128 * n:128 * n + 128]
                              .unsqueeze(0).broadcast_to([128, 32, 128]))
                    sp.v(lambda: nc.vector.tensor_mul(
                        ub[:].rearrange("p (n t) -> p n t", n=NS),
                        usb2[:].unsqueeze(1).broadcast_to([128, NS, TCB]),
                        ub[:].rearrange("p (n t) -> p n t", n=NS)))
                    for n_ in range(NS):
                        sp.v(lambda n_=n_: nc.vector.tensor_tensor_scan(
                            hbt[:, n_ * TCB:(n_ + 1) * TCB], mda[:, n_ * TCB:(n_ + 1) * TCB],
                            ub[:, n_ * TCB:(n_ + 1) * TCB], stt[:, n_:n_ + 1], ALU.mult, ALU.add))
                    sp.v(lambda: nc.vector.tensor_copy(stt[:], hbt[:, TCB - 1:NS * TCB:TCB]))
                    for gq in range(4):
                        for q2, (r, n) in enumerate(rs_):
                            sp.ld(cr[gq * 32:(gq + 1) * 32, 0:NS * TCB]
                                  .rearrange("p (n t) -> p n t", n=NS)[:, :, q2 * 128:(q2 + 1) * 128],
                                  A2AO[r * 640 + 256 + 32 * (4 * h + gq):r * 640 + 256 + 32 * (4 * h + gq) + 32,
                                       128 * n:128 * n + 128]
                                  .unsqueeze(0).broadcast_to([32, 32, 128]))
                    sp.v(lambda: nc.vector.tensor_mul(cr[:], hbt[:], cr[:]))
                    sp.v(lambda: nc.vector.reduce_sum(
                        ysb2[:, 0:TCB], cr[:].rearrange("p (n t) -> p t n", n=NS), axis=AX.X))
                    sp.v(lambda h=h: nc.vector.scalar_tensor_tensor(
                        ysb2[:], xrc[:], dpsb[:, h:h + 1], ysb2[:], ALU.mult, ALU.add))
                    for q2, (r, n) in enumerate(rs_):
                        sp.st(YI[r * 256 + 128 * h:r * 256 + 128 * h + 128, 128 * n:128 * n + 128],
                              ysb2[:, q2 * 128:(q2 + 1) * 128])
        def _a2a2():
            with allow_ici_all_to_all():
                return nc.gpsimd.collective_compute(
                    "AllToAll", ALU.bypass, replica_groups=G8, ins=[YI], outs=[YO])
        sp.g(_a2a2)

        AR.reset()
        ybuf = AR.al(DT_ * TCC)
        zbuf = AR.al(DT_ * TCC)
        h1buf = AR.al(DT_ * TCC)
        midbuf = AR.al(NMID * TCC)
        wtC = [AR.al(NMID * 128), AR.al(NMID * 128)]
        xnsc16 = AR.al(DT_ * TCC)
        gscC = AR.al(TCC)
        oscC = AR.al(TCC)
        sqC = AR.al(TCC)
        g2sb = AR.al(DT_)
        vrowC = AR.al(3 * TCC)
        S1c = lambda: vrowC[0:1, 0:TCC]
        S2c = lambda: vrowC[0:1, TCC:2 * TCC]
        RSTDc = lambda: vrowC[0:1, 2 * TCC:3 * TCC]
        sp.ld(g2sb[:], G2v[:])
        cstart = sp.mark()
        done_hist = [cstart, cstart]
        bandi = [0]

        def cband(W, m, nk, parts):
            buf = wtC[bandi[0] % 2]
            nl = band_ld_early(sp, nc, buf, W, m, nk, done_hist[0], parts=parts)
            bandi[0] += 1
            return buf, nl

        def cband_done():
            done_hist[0] = done_hist[1]
            done_hist[1] = sp.mark()

        for ch in range(NTOK // TCC):
            t0 = ch * TCC
            for k in range(DT_):
                sp.ld(ybuf[:, k * TCC:(k + 1) * TCC],
                      YO[(k // 2) * 256 + 128 * (k % 2):(k // 2) * 256 + 128 * (k % 2) + 128, t0:t0 + TCC])
                sp.ld(zbuf[:, k * TCC:(k + 1) * TCC], HBz[k * 128:(k + 1) * 128, t0:t0 + TCC])
                sp.ld(xnsc16[:, k * TCC:(k + 1) * TCC], HBxn0[k * 128:(k + 1) * 128, t0:t0 + TCC])
            for k in range(DT_):
                c0 = k * TCC
                sp.s(lambda c0=c0: nc.scalar.activation(out=zbuf[:, c0:c0 + TCC], in_=zbuf[:, c0:c0 + TCC], func=AF.Silu))
                sp.v(lambda c0=c0: nc.vector.tensor_mul(ybuf[:, c0:c0 + TCC], ybuf[:, c0:c0 + TCC], zbuf[:, c0:c0 + TCC]))
            for m in range(DT_):
                buf, nl = cband(Wo0, m, DT_, 2)
                sp.wait2("tensor", nl)
                for k in range(DT_):
                    sp.t(lambda buf=buf, k=k, st=(k == 0), sp_=(k == DT_ - 1):
                         nc.tensor.matmul(P0[:, 0:TCC], buf[:, k * 128:(k + 1) * 128],
                                          ybuf[:, k * TCC:(k + 1) * TCC], start=st, stop=sp_))
                cband_done()
                sp.v(lambda m=m: nc.vector.tensor_add(h1buf[:, m * TCC:(m + 1) * TCC], P0[:, 0:TCC],
                                                      xnsc16[:, m * TCC:(m + 1) * TCC]))
            ln_stats2(h1buf, TCC, DT_, S1c(), S2c(), sqC)
            rstd_from_s2(S2c(), RSTDc(), 1.0 / D, csb[0:1, 0:1], TCC)
            sp.t(lambda: nc.tensor.matmul(P1[:, 0:TCC], ones1[:, 0:128], RSTDc()[:, 0:TCC], start=True, stop=True))
            for k in range(DT_):
                c0 = k * TCC
                sp.v(lambda c0=c0, k=k: nc.vector.scalar_tensor_tensor(
                    zbuf[:, c0:c0 + TCC], h1buf[:, c0:c0 + TCC], g2sb[:, k:k + 1],
                    P1[:, 0:TCC], ALU.mult, ALU.mult))
            for m in range(NMID):
                buf, nl = cband(Wg0, m, DT_, 2)
                sp.wait2("tensor", nl)
                for k in range(DT_):
                    sp.t(lambda buf=buf, k=k, st=(k == 0), sp_=(k == DT_ - 1):
                         nc.tensor.matmul(P0[:, 0:TCC], buf[:, k * 128:(k + 1) * 128],
                                          zbuf[:, k * TCC:(k + 1) * TCC], start=st, stop=sp_))
                cband_done()
                sp.s(lambda: nc.scalar.activation(out=gscC[:], in_=P0[:, 0:TCC], func=AF.Silu))
                buf, nl = cband(Wu0, m, DT_, 2)
                sp.wait2("tensor", nl)
                for k in range(DT_):
                    sp.t(lambda buf=buf, k=k, st=(k == 0), sp_=(k == DT_ - 1):
                         nc.tensor.matmul(P0[:, 0:TCC], buf[:, k * 128:(k + 1) * 128],
                                          zbuf[:, k * TCC:(k + 1) * TCC], start=st, stop=sp_))
                cband_done()
                sp.v(lambda m=m: nc.vector.tensor_mul(midbuf[:, m * TCC:(m + 1) * TCC], gscC[:], P0[:, 0:TCC]))
            for mo in range(DT_):
                buf, nl = cband(Wd0, mo, NMID, 4)
                sp.wait2("tensor", nl)
                for k in range(NMID):
                    sp.t(lambda buf=buf, k=k, st=(k == 0), sp_=(k == NMID - 1):
                         nc.tensor.matmul(P0[:, 0:TCC], buf[:, k * 128:(k + 1) * 128],
                                          midbuf[:, k * TCC:(k + 1) * TCC], start=st, stop=sp_))
                cband_done()
                sp.v(lambda mo=mo: nc.vector.tensor_add(oscC[:], P0[:, 0:TCC], h1buf[:, mo * TCC:(mo + 1) * TCC]))
                sp.st(HBs0[mo * 128:(mo + 1) * 128, t0:t0 + TCC], oscC[:])

        AR.reset()
        sD = AR.al(64)
        geD = AR.al(64)
        sc1D = AR.al(64)
        sc2D = AR.al(64)
        rsD = AR.al(2)
        ones64f = AR.al(64)
        ltri = AR.al(128)
        vvD = AR.al(32)
        BIG = 1e30
        LOd = vvD[0:1, 0:2]; HId = vvD[0:1, 2:4]; MIDd = vvD[0:1, 4:6]; CNTd = vvD[0:1, 6:8]
        OKd = vvD[0:1, 8:10]; OKCd = vvD[0:1, 10:12]; TMPd = vvD[0:1, 12:14]; VVd = vvD[0:1, 14:16]
        NEEDd = vvD[0:1, 16:18]; C512d = vvD[0:1, 18:20]
        tp = MR[0:2, 0:128]
        for n_ in range(8):
            for b_ in range(2):
                for a_ in range(4):
                    sp.ld(sD[16 * n_ + 4 * a_:16 * n_ + 4 * a_ + 4, 32 * b_:32 * b_ + 32],
                          SCO[4 * b_ + a_:4 * b_ + a_ + 1, 128 * n_:128 * n_ + 128]
                          .rearrange("x (q s) -> (x q) s", q=4))
        sp.v(lambda: nc.vector.memset(ones64f[:], 1.0))
        sp.v(lambda: nc.vector.memset(vvD[:], 0.0))
        sp.v(lambda: nc.vector.memset(C512d, 512.0))
        sp.v(lambda: nc.vector.tensor_copy(ltri[:], fiiI[:]))
        sp.v(lambda: nc.vector.tensor_copy(geD[:, 0:1], pii[:]))
        sp.v(lambda: nc.vector.tensor_scalar(ltri[:], ltri[:], geD[:, 0:1], None, ALU.is_gt))
        s3 = sD[:].rearrange("p (b j) -> p b j", b=2)
        ge3 = geD[:].rearrange("p (b j) -> p b j", b=2)

        def cross_min(dst, src3, op):
            sp.v(lambda src3=src3, op=op: nc.vector.tensor_reduce(rsD[:], src3, axis=AX.X, op=op))
            sp.t(lambda: nc.tensor.transpose(P2[0:2, 0:128], rsD[:], idf[:]))
            sp.v(lambda: nc.vector.tensor_copy(tp[:], P2[0:2, 0:128]))
            sp.v(lambda op=op: nc.vector.tensor_reduce(tp[0:2, 0:1], tp[0:2, :], axis=AX.X, op=op))
            sp.t(lambda: nc.tensor.transpose(P2[0:1, 0:2], tp[0:2, 0:1], idf[0:2, 0:2]))
            sp.v(lambda d=dst: nc.vector.tensor_copy(d, P2[0:1, 0:2]))

        def count_ge(dst_cnt, thr):
            sp.t(lambda thr=thr: nc.tensor.matmul(P0[:, 0:2], ones1[:, 0:128], thr, start=True, stop=True))
            sp.v(lambda: nc.vector.tensor_tensor(ge3, s3, P0[:, 0:2].unsqueeze(2).broadcast_to([128, 2, 32]), ALU.is_ge))
            sp.v(lambda: nc.vector.reduce_sum(rsD[:], ge3, axis=AX.X))
            sp.t(lambda: nc.tensor.matmul(PR[0:1, 0:2], ones128[:, 0:1], rsD[:], start=True, stop=True))
            sp.v(lambda d=dst_cnt: nc.vector.tensor_copy(d, PR[0:1, 0:2]))

        cross_min(LOd, s3, ALU.min)
        cross_min(HId, s3, ALU.max)
        sp.v(lambda: nc.vector.tensor_scalar_add(LOd, LOd, -1.0))
        sp.v(lambda: nc.vector.tensor_scalar_add(HId, HId, 1.0))
        for it in range(44):
            sp.v(lambda: nc.vector.tensor_add(MIDd, LOd, HId))
            sp.v(lambda: nc.vector.tensor_scalar_mul(MIDd, MIDd, 0.5))
            count_ge(CNTd, MIDd)
            sp.v(lambda: nc.vector.tensor_tensor(OKd, CNTd, C512d, ALU.is_ge))
            sp.v(lambda: nc.vector.tensor_scalar(OKCd, OKd, -1.0, 1.0, ALU.mult, ALU.add))
            sp.v(lambda: nc.vector.tensor_sub(TMPd, MIDd, LOd))
            sp.v(lambda: nc.vector.tensor_mul(TMPd, TMPd, OKd))
            sp.v(lambda: nc.vector.tensor_add(LOd, LOd, TMPd))
            sp.v(lambda: nc.vector.tensor_sub(TMPd, MIDd, HId))
            sp.v(lambda: nc.vector.tensor_mul(TMPd, TMPd, OKCd))
            sp.v(lambda: nc.vector.tensor_add(HId, HId, TMPd))
        count_ge(CNTd, LOd)
        sp.v(lambda: nc.vector.tensor_mul(sc1D[:], sD[:], geD[:]))
        sp.v(lambda: nc.vector.tensor_scalar(sc2D[:], geD[:], -BIG, BIG, ALU.mult, ALU.add))
        sp.v(lambda: nc.vector.tensor_add(sc1D[:], sc1D[:], sc2D[:]))
        cross_min(VVd, sc1D[:].rearrange("p (b j) -> p b j", b=2), ALU.min)
        sp.t(lambda: nc.tensor.matmul(P0[:, 0:2], ones1[:, 0:128], VVd, start=True, stop=True))
        sp.v(lambda: nc.vector.tensor_tensor(sc1D[:].rearrange("p (b j) -> p b j", b=2), s3,
                                             P0[:, 0:2].unsqueeze(2).broadcast_to([128, 2, 32]), ALU.is_gt))
        sp.v(lambda: nc.vector.tensor_tensor(sc2D[:].rearrange("p (b j) -> p b j", b=2), s3,
                                             P0[:, 0:2].unsqueeze(2).broadcast_to([128, 2, 32]), ALU.is_equal))
        sp.v(lambda: nc.vector.reduce_sum(rsD[:], sc1D[:].rearrange("p (b j) -> p b j", b=2), axis=AX.X))
        sp.t(lambda: nc.tensor.matmul(PR[0:1, 0:2], ones128[:, 0:1], rsD[:], start=True, stop=True))
        sp.v(lambda: nc.vector.tensor_copy(NEEDd, PR[0:1, 0:2]))
        sp.v(lambda: nc.vector.tensor_sub(NEEDd, C512d, NEEDd))

        def pos_cumsum(src, dst):
            for b_ in range(2):
                sp.v(lambda b_=b_: nc.vector.tensor_tensor_scan(
                    dst[:, b_ * 32:(b_ + 1) * 32], ones64f[:, 0:32], src[:, b_ * 32:(b_ + 1) * 32],
                    0.0, ALU.mult, ALU.add))
            sp.v(lambda dst=dst: nc.vector.tensor_copy(rsD[:], dst[:, 31:64:32]))
            sp.t(lambda: nc.tensor.matmul(P3[:, 0:2], ltri[:], rsD[:], start=True, stop=True))
            sp.v(lambda dst=dst: nc.vector.tensor_tensor(dst[:].rearrange("p (b j) -> p b j", b=2),
                                                         dst[:].rearrange("p (b j) -> p b j", b=2),
                                                         P3[:, 0:2].unsqueeze(2).broadcast_to([128, 2, 32]), ALU.add))

        pos_cumsum(sc2D, geD)
        sp.t(lambda: nc.tensor.matmul(P0[:, 0:2], ones1[:, 0:128], NEEDd, start=True, stop=True))
        sp.v(lambda: nc.vector.tensor_tensor(ge3, ge3, P0[:, 0:2].unsqueeze(2).broadcast_to([128, 2, 32]), ALU.is_le))
        sp.v(lambda: nc.vector.tensor_mul(geD[:], geD[:], sc2D[:]))
        sp.v(lambda: nc.vector.tensor_add(sc1D[:], sc1D[:], geD[:]))
        pos_cumsum(sc1D, sc2D)
        sp.v(lambda: nc.vector.tensor_scalar_add(sc2D[:, 0:32], sc2D[:, 0:32], -1.0))
        sp.v(lambda: nc.vector.tensor_scalar_add(sc2D[:, 32:64], sc2D[:, 32:64], 511.0))
        sp.v(lambda: nc.vector.tensor_scalar_add(geD[:], sc2D[:], 1.0))
        sp.v(lambda: nc.vector.tensor_mul(geD[:], geD[:], sc1D[:]))
        sp.v(lambda: nc.vector.tensor_scalar_add(geD[:], geD[:], -1.0))
        for b_ in range(2):
            sp.st(CANS[b_:b_ + 1, :].rearrange("x (p s) -> (x p) s", p=128), geD[:, 32 * b_:32 * b_ + 32])
            sp.st(CANSC[b_:b_ + 1, :].rearrange("x (p s) -> (x p) s", p=128), sD[:, 32 * b_:32 * b_ + 32])

        AR.reset()
        iotasb = AR.al(NTOK)
        Mt = AR.al(NTOK)
        sc2tm = AR.al(64)
        sctm = AR.al(64)
        postm = AR.al(64)
        growtm = AR.al(64)
        vt = AR.al(4)
        tpad = AR.al(128)
        vr1 = VR[:, 0:NTOK]
        sp.ld(vr1, Iiota[:])
        for sh in range(2):
            sp.t(lambda sh=sh: nc.tensor.matmul(P0[:, 0:512], ones1[:, 0:128], vr1[:, sh * 512:(sh + 1) * 512], start=True, stop=True))
            sp.v(lambda sh=sh: nc.vector.tensor_copy(iotasb[:, sh * 512:(sh + 1) * 512], P0[:, 0:512]))
        nat64 = AR.al(128)
        sp.ld(nat64[0:64, 0:128], CANS[:, :].rearrange("b (r k) -> (b r) k", k=128))
        sp.t(lambda: nc.tensor.transpose(P2[:, 0:64], nat64[0:64, 0:128], idf[0:64, 0:64]))
        sp.v(lambda: nc.vector.tensor_copy(sc2tm[:], P2[:, 0:64]))
        sp.ld(nat64[0:64, 0:128], CANSC[:, :].rearrange("b (r k) -> (b r) k", k=128))
        sp.t(lambda: nc.tensor.transpose(P2[:, 0:64], nat64[0:64, 0:128], idf[0:64, 0:64]))
        sp.v(lambda: nc.vector.tensor_copy(sctm[:], P2[:, 0:64]))
        sp.ld(postm[:], Ipostm[:])
        sp.ld(growtm[:], Igrowtm[:])
        sp.v(lambda: nc.vector.memset(vt[:, 3:4], 0.0))
        for c64 in range(64):
            sp.v(lambda c64=c64: nc.vector.tensor_scalar(Mt[:], iotasb[:], sc2tm[:, c64:c64 + 1], None, ALU.is_equal))
            sp.v(lambda c64=c64: nc.vector.tensor_copy(vt[:, 0:1], postm[:, c64:c64 + 1]))
            sp.v(lambda c64=c64: nc.vector.tensor_copy(vt[:, 1:2], growtm[:, c64:c64 + 1]))
            sp.v(lambda c64=c64: nc.vector.tensor_copy(vt[:, 2:3], sctm[:, c64:c64 + 1]))
            sp.t(lambda c64=c64: nc.tensor.matmul(P3[0:4, 0:512], vt[:, 0:4], Mt[:, 0:512],
                                                  start=(c64 == 0), stop=(c64 == 63)))
            sp.t(lambda c64=c64: nc.tensor.matmul(P4[0:4, 0:512], vt[:, 0:4], Mt[:, 512:1024],
                                                  start=(c64 == 0), stop=(c64 == 63)))
        sp.v(lambda: nc.vector.tensor_copy(MR[0:4, 0:512], P3[0:4, 0:512]))
        sp.v(lambda: nc.vector.tensor_copy(MR[0:4, 512:1024], P4[0:4, 0:512]))
        sp.st(BYSLOT[0:4, :], MR[0:4, 0:NTOK])
        for (dst8, row) in ((posk8, 0), (grow8, 1), (sco8, 2)):
            sp.ld(nat64[0:8, 0:128], BYSLOT[row:row + 1, :].rearrange("x (k p) -> (x k) p", p=128))
            sp.t(lambda: nc.tensor.transpose(P2[:, 0:8], nat64[0:8, 0:128], idf[0:8, 0:8]))
            sp.v(lambda dst8=dst8: nc.vector.tensor_copy(dst8[:], P2[:, 0:8]))
        vr8 = VR[:, 1088:1096]
        sp.ld(vr8, Ioneh[:])
        sp.t(lambda: nc.tensor.matmul(P2[:, 0:8], ones1[:, 0:128], vr8, start=True, stop=True))
        sp.v(lambda: nc.vector.tensor_mul(t8a[:], posk8[:], P2[:, 0:8]))
        sp.v(lambda: nc.vector.reduce_sum(posq[:], t8a[:], axis=AX.X))
        sp.v(lambda: nc.vector.tensor_mul(t8a[:], sco8[:], P2[:, 0:8]))
        sp.v(lambda: nc.vector.reduce_sum(t8b[:, 0:1], t8a[:], axis=AX.X))
        sp.t(lambda: nc.tensor.transpose(P2[0:1, 0:128], t8b[:, 0:1], idf[:]))
        sp.v(lambda: nc.vector.tensor_copy(scqrow[:], P2[0:1, 0:128]))
        cofs_sb = PERS.al(1)
        sp.ld(cofs_sb[:], Icofs[:])
        sp.v(lambda: nc.vector.tensor_scalar(t8a[:], grow8[:], cofs_sb[:, 0:1], None, ALU.add))
        sp.v(lambda: nc.vector.tensor_scalar(t8b[:], t8a[:], 0.0, None, ALU.is_ge))
        sp.v(lambda: nc.vector.tensor_scalar(t8c[:], t8a[:], 1024.0, None, ALU.is_lt))
        sp.v(lambda: nc.vector.tensor_mul(t8b[:], t8b[:], t8c[:]))
        sp.v(lambda: nc.vector.tensor_scalar_add(t8a[:], t8a[:], -1024.0))
        sp.v(lambda: nc.vector.tensor_mul(t8a[:], t8a[:], t8b[:]))
        sp.v(lambda: nc.vector.tensor_scalar_add(t8a[:], t8a[:], 1024.0))
        sp.v(lambda: nc.vector.tensor_copy(selidxI[:], t8a[:]))
        sp.ld(myslotI[:], Imyslot[:])
        sp.ld(myrowsI[:], Imyrows[:])
        slotrow = MR[0:8, 1024:1152]
        mselrow = MR[0:1, 0:1024]
        sp.g(lambda: nc.gpsimd.indirect_dma_start(
            out=slotrow, out_offset=None,
            in_=CANS[:, :].rearrange("b (r k) -> (b r) k", k=128),
            in_offset=IndirectOffsetOnAxis(ap=myrowsI, axis=0)), inc=16)
        sp.v(lambda: nc.vector.memset(tpad[:], 0.0))
        sp.v(lambda: nc.vector.tensor_copy(tpad[0:8, 0:128], slotrow))
        sp.t(lambda: nc.tensor.transpose(P2[:, 0:128], tpad[:], idf[:]))
        sp.v(lambda: nc.vector.tensor_copy(slotf[:], P2[:, 0:8]))
        sp.v(lambda: nc.vector.tensor_scalar(t8a[:], slotf[:], 0.0, None, ALU.is_ge))
        for lk in range(8):
            sp.t(lambda lk=lk: nc.tensor.transpose(P2[0:1, 0:128], t8a[:, lk:lk + 1], idf[:]))
            sp.v(lambda lk=lk: nc.vector.tensor_copy(mselrow[:, lk * 128:(lk + 1) * 128], P2[0:1, 0:128]))
        sp.s(lambda: nc.scalar.activation(out=slotf[:], in_=slotf[:], func=AF.Relu))
        sp.v(lambda: nc.vector.tensor_copy(gidxI[:], slotf[:]))
        g128 = AR.al(D)
        for k8 in range(8):
            sp.g(lambda k8=k8: nc.gpsimd.indirect_dma_start(
                out=g128[:], out_offset=None, in_=XN0TM[0:1, :],
                in_offset=IndirectOffsetOnAxis(ap=selidxI[:, k8:k8 + 1], axis=0)), inc=16)
            sp.st(SELP[k8 * 128:(k8 + 1) * 128, :], g128[:])
        sp.g(lambda: nc.gpsimd.collective_compute(
            "AllReduce", ALU.add, replica_groups=G8, ins=[SELP], outs=[SELTM]))

        AR.reset()
        S = 1024
        NH, NKV = 16, 8
        bufA = AR.al(DT_ * S)
        bufH = AR.al(DT_ * S)
        qsb = AR.al(DT_ * 128)
        selmy = AR.al(DT_ * 128)
        x1sbE = AR.al(DT_ * 128)
        avfm = AR.al(DT_ * 128)
        h2sb = qsb
        wtileE = AR.al(16 * 128)
        ttile = x1sbE
        gat_tm = avfm
        cmsb = bufH[:, 8192:9216]
        trig = AR.al(2 * 128)
        tmq = AR.al(128)
        tmq2 = AR.al(128)
        gscE = AR.al(128)
        expb = AR.al(128)
        sqE = AR.al(512)
        angs = AR.al(3 * 64)
        smv = AR.al(5 * DT_)
        vecsE = AR.al(3 * S + 384)
        RSTDe = lambda: vecsE[0:1, 0:S]
        S1e = lambda: vecsE[0:1, S:2 * S]
        S2e = lambda: vecsE[0:1, 2 * S:3 * S]
        AUXe = lambda: vecsE[0:1, 3 * S:3 * S + 384]
        ifr = VR[:, 1024:1088]

        for sk in range(8):
            sp.ld(ttile[:], SELTM[sk * 128:(sk + 1) * 128, :])
            for ft in range(DT_):
                sp.t(lambda ft=ft: nc.tensor.transpose(P4[:, 0:128], ttile[:, ft * 128:(ft + 1) * 128], idf[:]))
                sp.v(lambda ft=ft, sk=sk: nc.vector.tensor_copy(
                    bufA[:, ft * S + sk * 128:ft * S + sk * 128 + 128], P4[:, 0:128]))
        sp.g(lambda: nc.gpsimd.indirect_dma_start(
            out=ttile[:], out_offset=None, in_=SELTM[0:1, :],
            in_offset=IndirectOffsetOnAxis(ap=myslotI, axis=0)), inc=16)
        for ft in range(DT_):
            sp.t(lambda ft=ft: nc.tensor.transpose(P4[:, 0:128], ttile[:, ft * 128:(ft + 1) * 128], idf[:]))
            sp.v(lambda ft=ft: nc.vector.tensor_copy(selmy[:, ft * 128:(ft + 1) * 128], P4[:, 0:128]))
        sp.ld(smv[:, 0:DT_], Vln1[:])
        sp.ld(smv[:, DT_:2 * DT_], Vln2[:])
        sp.ld(smv[:, 2 * DT_:3 * DT_], VbqE[:])
        sp.ld(smv[:, 3 * DT_:3 * DT_ + 8], VbkE[:])
        sp.ld(smv[:, 4 * DT_:4 * DT_ + 8], VbvE[:])
        sp.g(lambda: nc.gpsimd.iota(ifriI[:], pattern=[[1, 64]], base=0, channel_multiplier=0))
        sp.v(lambda: nc.vector.tensor_copy(ifr[:], ifriI[:]))
        sp.s(lambda: nc.scalar.activation(out=ifr[:], in_=ifr[:], func=AF.Exp, scale=csb[0:1, 4:5]))

        def rope_trig(pos_col, dstc):
            sp.t(lambda: nc.tensor.matmul(P3[:, 0:64], ones1[:, 0:128], ifr[:], start=True, stop=True))
            A_ = angs[:, 0:64]; R_ = angs[:, 64:128]; W_ = angs[:, 128:192]
            sp.v(lambda pos_col=pos_col: nc.vector.tensor_scalar(A_, P3[:, 0:64], pos_col, None, ALU.mult))
            for (dst, phase) in ((dstc + 64, 0.0), (dstc, np.pi / 2)):
                sp.v(lambda phase=phase: nc.vector.tensor_scalar(W_, A_, 1.0 / ROPE_C1, phase / ROPE_C1, ALU.mult, ALU.add))
                sp.v(lambda: nc.vector.tensor_copy(nintI[:], W_))
                sp.v(lambda: nc.vector.tensor_copy(W_, nintI[:]))
                sp.v(lambda phase=phase: nc.vector.tensor_scalar(R_, A_, 1.0, phase, ALU.mult, ALU.add))
                sp.v(lambda: nc.vector.scalar_tensor_tensor(R_, W_, -ROPE_C1, R_, ALU.mult, ALU.add))
                sp.v(lambda: nc.vector.scalar_tensor_tensor(R_, W_, -ROPE_C2, R_, ALU.mult, ALU.add))
                sp.s(lambda dst=dst: nc.scalar.activation(out=trig[:, dst:dst + 64], in_=R_, func=AF.Sin))

        def rope_tile(fm_ap, trigc):
            sp.t(lambda fm_ap=fm_ap: nc.tensor.transpose(P4[:, 0:128], fm_ap, idf[:]))
            sp.v(lambda: nc.vector.tensor_copy(tmq[:], P4[:, 0:128]))
            C_ = trig[:, trigc:trigc + 64]; Sn = trig[:, trigc + 64:trigc + 128]
            sp.v(lambda: nc.vector.tensor_mul(tmq2[:, 0:64], tmq[:, 0:64], C_))
            sp.v(lambda: nc.vector.tensor_mul(gscE[:, 0:64], tmq[:, 64:128], Sn))
            sp.v(lambda: nc.vector.tensor_sub(tmq2[:, 0:64], tmq2[:, 0:64], gscE[:, 0:64]))
            sp.v(lambda: nc.vector.tensor_mul(tmq2[:, 64:128], tmq[:, 64:128], C_))
            sp.v(lambda: nc.vector.tensor_mul(gscE[:, 0:64], tmq[:, 0:64], Sn))
            sp.v(lambda: nc.vector.tensor_add(tmq2[:, 64:128], tmq2[:, 64:128], gscE[:, 0:64]))
            sp.t(lambda: nc.tensor.transpose(P4[:, 0:128], tmq2[:], idf[:]))
            sp.v(lambda fm_ap=fm_ap: nc.vector.tensor_copy(fm_ap, P4[:, 0:128]))

        for half in range(2):
            ln_stats2(bufA, 512, DT_, S1e()[:, half * 512:(half + 1) * 512],
                      S2e()[:, half * 512:(half + 1) * 512], sqE, base=half * 512, stride=S)
        sp.v(lambda: nc.vector.tensor_scalar_mul(S2e(), S2e(), 1.0 / D))
        sp.s(lambda: nc.scalar.activation(out=RSTDe(), in_=S2e(), func=AF.Sqrt, bias=csb[0:1, 3:4]))
        sp.v(lambda: nc.vector.reciprocal(RSTDe(), RSTDe()))
        for half in range(2):
            sp.t(lambda half=half: nc.tensor.matmul(P0[:, 0:512], ones1[:, 0:128],
                                                    RSTDe()[:, half * 512:(half + 1) * 512], start=True, stop=True))
            for k in range(DT_):
                c0 = k * S + half * 512
                sp.v(lambda c0=c0, k=k: nc.vector.scalar_tensor_tensor(
                    bufH[:, c0:c0 + 512], bufA[:, c0:c0 + 512], smv[:, k:k + 1], P0[:, 0:512], ALU.mult, ALU.mult))
        ln_stats2(selmy, 128, DT_, S1e()[:, 0:128], S2e()[:, 0:128], sqE)
        sp.v(lambda: nc.vector.tensor_scalar_mul(S2e()[:, 0:128], S2e()[:, 0:128], 1.0 / D))
        sp.s(lambda: nc.scalar.activation(out=AUXe()[:, 0:128], in_=S2e()[:, 0:128], func=AF.Sqrt, bias=csb[0:1, 3:4]))
        sp.v(lambda: nc.vector.reciprocal(AUXe()[:, 0:128], AUXe()[:, 0:128]))
        sp.t(lambda: nc.tensor.matmul(P3[:, 0:128], ones1[:, 0:128], AUXe()[:, 0:128], start=True, stop=True))
        for k in range(DT_):
            sp.v(lambda k=k: nc.vector.scalar_tensor_tensor(
                x1sbE[:, k * 128:(k + 1) * 128], selmy[:, k * 128:(k + 1) * 128], smv[:, k:k + 1],
                P3[:, 0:128], ALU.mult, ALU.mult))
        for m in range(DT_):
            band_ld(sp, nc, wtileE, WqE, m, DT_, parts=2)
            for k in range(DT_):
                sp.t(lambda k=k, st=(k == 0), sp_=(k == DT_ - 1):
                     nc.tensor.matmul(P1[:, 0:128], wtileE[:, k * 128:(k + 1) * 128],
                                      x1sbE[:, k * 128:(k + 1) * 128], start=st, stop=sp_))
            sp.v(lambda m=m: nc.vector.tensor_scalar_add(qsb[:, m * 128:(m + 1) * 128], P1[:, 0:128], smv[:, 2 * DT_ + m:2 * DT_ + m + 1]))
        for (W_, bcol, off) in ((WkE, 3 * DT_, 0), (WvE, 4 * DT_, 8 * S)):
            for m in range(8):
                band_ld(sp, nc, wtileE, W_, m, DT_, parts=2)
                for half in range(2):
                    for k in range(DT_):
                        sp.t(lambda k=k, half=half, st=(k == 0), sp_=(k == DT_ - 1):
                             nc.tensor.matmul(P0[:, 0:512], wtileE[:, k * 128:(k + 1) * 128],
                                              bufH[:, k * S + half * 512:k * S + half * 512 + 512], start=st, stop=sp_))
                    sp.v(lambda m=m, half=half, bcol=bcol, off=off: nc.vector.tensor_scalar_add(
                        bufA[:, off + m * S + half * 512:off + m * S + half * 512 + 512], P0[:, 0:512],
                        smv[:, bcol + m:bcol + m + 1]))
        rope_trig(posq[:, 0:1], 128)
        for h in range(NH):
            rope_tile(qsb[:, h * 128:(h + 1) * 128], 128)
        for st_ in range(8):
            rope_trig(posk8[:, st_:st_ + 1], 0)
            for kh in range(NKV):
                rope_tile(bufA[:, kh * S + st_ * 128:kh * S + st_ * 128 + 128], 0)
        for kh in range(NKV):
            for st_ in range(8):
                sp.t(lambda kh=kh, st_=st_: nc.tensor.transpose(
                    P4[:, 0:128], bufA[:, 8 * S + kh * S + st_ * 128:8 * S + kh * S + st_ * 128 + 128], idf[:]))
                sp.v(lambda kh=kh, st_=st_: nc.vector.tensor_copy(
                    bufH[:, (kh * 8 + st_) * 128:(kh * 8 + st_ + 1) * 128], P4[:, 0:128]))
        sp.ld(cmsb[:], Icm[:])
        for kh in range(NKV):
            for st_ in range(8):
                sp.s(lambda kh=kh, st_=st_: nc.scalar.activation(
                    out=sqE[:, 0:128], in_=bufA[:, kh * S + st_ * 128:kh * S + st_ * 128 + 128], func=AF.Square))
                sp.t(lambda: nc.tensor.matmul(PR2[0:1, 0:128], ones128[:, 0:1], sqE[:, 0:128], start=True, stop=True))
                sp.v(lambda st_=st_: nc.vector.tensor_copy(S1e()[:, st_ * 128:(st_ + 1) * 128], PR2[0:1, 0:128]))
            sp.v(lambda kh=kh: nc.vector.reduce_max(AUXe()[:, 256 + kh:256 + kh + 1], S1e()[:, 0:S], axis=AX.X))
        for h in range(NH):
            kh = h // 2
            sp.s(lambda h=h: nc.scalar.activation(out=sqE[:, 0:128], in_=qsb[:, h * 128:(h + 1) * 128], func=AF.Square))
            sp.t(lambda: nc.tensor.matmul(PR2[0:1, 0:128], ones128[:, 0:1], sqE[:, 0:128], start=True, stop=True))
            sp.s(lambda: nc.scalar.activation(out=S1e()[:, 0:128], in_=PR2[0:1, 0:128], func=AF.Sqrt))
            sp.s(lambda kh=kh: nc.scalar.activation(out=AUXe()[:, 300:301], in_=AUXe()[:, 256 + kh:257 + kh], func=AF.Sqrt))
            sp.v(lambda: nc.vector.tensor_scalar(S1e()[:, 0:128], S1e()[:, 0:128], AUXe()[0:1, 300:301], None, ALU.mult))
            sp.v(lambda: nc.vector.tensor_scalar_mul(S1e()[:, 0:128], S1e()[:, 0:128], -1.0 / np.sqrt(128.0)))
            sp.t(lambda: nc.tensor.matmul(P3[:, 0:128], ones1[:, 0:128], S1e()[:, 0:128], start=True, stop=True))
            sp.v(lambda: nc.vector.tensor_copy(tmq[:], P3[:, 0:128]))
            for kt in range(8):
                sp.t(lambda kh=kh, kt=kt, h=h: nc.tensor.matmul(
                    P1[:, 0:128], bufA[:, kh * S + kt * 128:kh * S + kt * 128 + 128],
                    qsb[:, h * 128:(h + 1) * 128], start=True, stop=True))
                sp.v(lambda: nc.vector.scalar_tensor_tensor(expb[:], P1[:, 0:128], 1.0 / np.sqrt(128.0), tmq[:], ALU.mult, ALU.add))
                sp.s(lambda: nc.scalar.activation(out=expb[:], in_=expb[:], func=AF.Exp))
                sp.v(lambda kt=kt: nc.vector.tensor_mul(expb[:], expb[:], cmsb[:, kt * 128:(kt + 1) * 128]))
                sp.t(lambda kt=kt: nc.tensor.matmul(PR2[0:1, 0:128], ones128[:, 0:1], expb[:], start=(kt == 0), stop=(kt == 7)))
                sp.t(lambda kh=kh, kt=kt: nc.tensor.matmul(
                    P2[:, 0:128], bufH[:, (kh * 8 + kt) * 128:(kh * 8 + kt + 1) * 128], expb[:], start=(kt == 0), stop=(kt == 7)))
            sp.v(lambda: nc.vector.tensor_copy(S1e()[:, 0:128], PR2[0:1, 0:128]))
            sp.v(lambda: nc.vector.reciprocal(S1e()[:, 0:128], S1e()[:, 0:128]))
            sp.t(lambda: nc.tensor.matmul(P3[:, 0:128], ones1[:, 0:128], S1e()[:, 0:128], start=True, stop=True))
            sp.v(lambda: nc.vector.tensor_copy(tmq[:], P3[:, 0:128]))
            sp.v(lambda h=h: nc.vector.tensor_mul(avfm[:, h * 128:(h + 1) * 128], P2[:, 0:128], tmq[:]))
        for m in range(DT_):
            band_ld(sp, nc, wtileE, WoE, m, DT_, parts=2)
            for k in range(DT_):
                sp.t(lambda k=k, st=(k == 0), sp_=(k == DT_ - 1):
                     nc.tensor.matmul(P1[:, 0:128], wtileE[:, k * 128:(k + 1) * 128],
                                      avfm[:, k * 128:(k + 1) * 128], start=st, stop=sp_))
            sp.v(lambda m=m: nc.vector.tensor_add(x1sbE[:, m * 128:(m + 1) * 128], P1[:, 0:128], selmy[:, m * 128:(m + 1) * 128]))
        ln_stats2(x1sbE, 128, DT_, S1e()[:, 0:128], S2e()[:, 0:128], sqE)
        sp.v(lambda: nc.vector.tensor_scalar_mul(S2e()[:, 0:128], S2e()[:, 0:128], 1.0 / D))
        sp.s(lambda: nc.scalar.activation(out=AUXe()[:, 0:128], in_=S2e()[:, 0:128], func=AF.Sqrt, bias=csb[0:1, 3:4]))
        sp.v(lambda: nc.vector.reciprocal(AUXe()[:, 0:128], AUXe()[:, 0:128]))
        sp.t(lambda: nc.tensor.matmul(P3[:, 0:128], ones1[:, 0:128], AUXe()[:, 0:128], start=True, stop=True))
        for k in range(DT_):
            sp.v(lambda k=k: nc.vector.scalar_tensor_tensor(
                h2sb[:, k * 128:(k + 1) * 128], x1sbE[:, k * 128:(k + 1) * 128], smv[:, DT_ + k:DT_ + k + 1],
                P3[:, 0:128], ALU.mult, ALU.mult))
        MIDOFF = 64 * 128
        for m in range(64):
            band_ld(sp, nc, wtileE, WgE, m, DT_, parts=2)
            for k in range(DT_):
                sp.t(lambda k=k, st=(k == 0), sp_=(k == DT_ - 1):
                     nc.tensor.matmul(P1[:, 0:128], wtileE[:, k * 128:(k + 1) * 128],
                                      h2sb[:, k * 128:(k + 1) * 128], start=st, stop=sp_))
            sp.s(lambda: nc.scalar.activation(out=gscE[:], in_=P1[:, 0:128], func=AF.Silu))
            band_ld(sp, nc, wtileE, WuE, m, DT_, parts=2)
            for k in range(DT_):
                sp.t(lambda k=k, st=(k == 0), sp_=(k == DT_ - 1):
                     nc.tensor.matmul(P1[:, 0:128], wtileE[:, k * 128:(k + 1) * 128],
                                      h2sb[:, k * 128:(k + 1) * 128], start=st, stop=sp_))
            sp.v(lambda m=m: nc.vector.tensor_mul(bufH[:, MIDOFF + m * 128:MIDOFF + (m + 1) * 128], gscE[:], P1[:, 0:128]))
        sp.s(lambda: nc.scalar.activation(out=S1e()[:, 0:128], in_=scqrow[:], func=AF.Sigmoid))
        sp.t(lambda: nc.tensor.matmul(P3[:, 0:128], ones1[:, 0:128], S1e()[:, 0:128], start=True, stop=True))
        for mo in range(DT_):
            for hf in range(4):
                for p0 in range(0, 16, 16):
                    sp.ld(wtileE[:, 0:16 * 128],
                          WdE[hf * 16 * 128:(hf + 1) * 16 * 128, mo * 128:(mo + 1) * 128]
                          .rearrange("(k p) j -> p k j", p=128))
                for k in range(16):
                    kk = hf * 16 + k
                    sp.t(lambda k=k, kk=kk, st=(kk == 0), sp_=(kk == 63):
                         nc.tensor.matmul(P1[:, 0:128], wtileE[:, k * 128:(k + 1) * 128],
                                          bufH[:, MIDOFF + kk * 128:MIDOFF + (kk + 1) * 128], start=st, stop=sp_))
            sp.v(lambda mo=mo: nc.vector.tensor_add(gscE[:], P1[:, 0:128], x1sbE[:, mo * 128:(mo + 1) * 128]))
            sp.v(lambda mo=mo: nc.vector.tensor_sub(gscE[:], gscE[:], selmy[:, mo * 128:(mo + 1) * 128]))
            sp.v(lambda: nc.vector.tensor_mul(gscE[:], gscE[:], P3[:, 0:128]))
            sp.v(lambda mo=mo: nc.vector.tensor_add(gscE[:], gscE[:], selmy[:, mo * 128:(mo + 1) * 128]))
            sp.s(lambda: nc.scalar.activation(out=gscE[:], in_=gscE[:], func=AF.Silu))
            sp.t(lambda: nc.tensor.transpose(P4[:, 0:128], gscE[:], idf[:]))
            sp.v(lambda mo=mo: nc.vector.tensor_copy(gat_tm[:, mo * 128:(mo + 1) * 128], P4[:, 0:128]))
        sp.st(GATI[:, :], gat_tm[:])
        sp.g(lambda: nc.gpsimd.collective_compute(
            "AllGather", ALU.bypass, replica_groups=G8, ins=[GATI], outs=[GATO]))

        AR.reset()
        x1sbF = AR.al(DT_ * NTOK)
        gself = AR.al(DT_ * NTOK)
        gtm = AR.al(D)
        tsbF = AR.al(NTOK)
        sqF = AR.al(NCH)
        t1F = AR.al(NCH)
        vrowF = AR.al(5 * NTOK)
        S1f = lambda: vrowF[0:1, 0:NTOK]
        S2f = lambda: vrowF[0:1, NTOK:2 * NTOK]
        MUf = lambda: vrowF[0:1, 2 * NTOK:3 * NTOK]
        RSTDf = lambda: vrowF[0:1, 3 * NTOK:4 * NTOK]
        NMRf = lambda: vrowF[0:1, 4 * NTOK:5 * NTOK]
        for lk in range(8):
            sp.g(lambda lk=lk: nc.gpsimd.indirect_dma_start(
                out=gtm[:], out_offset=None, in_=GATO[0:1, :],
                in_offset=IndirectOffsetOnAxis(ap=gidxI[:, lk:lk + 1], axis=0)), inc=16)
            for ft in range(DT_):
                sp.t(lambda ft=ft: nc.tensor.transpose(P4[:, 0:128], gtm[:, ft * 128:(ft + 1) * 128], idf[:]))
                sp.v(lambda ft=ft, lk=lk: nc.vector.tensor_copy(
                    gself[:, ft * NTOK + lk * 128:ft * NTOK + lk * 128 + 128], P4[:, 0:128]))
        mselrow = MR[0:1, 0:1024]
        for lk in range(8):
            bcast_row(P3, mselrow, lk * 128, 128)
            for k in range(DT_):
                c0 = k * NTOK + lk * 128
                sp.ld(tsbF[:, 0:128], HBgb[k * 128:(k + 1) * 128, lk * 128:lk * 128 + 128])
                sp.v(lambda c0=c0: nc.vector.tensor_sub(t1F[:, 0:128], gself[:, c0:c0 + 128], tsbF[:, 0:128]))
                sp.v(lambda: nc.vector.tensor_mul(t1F[:, 0:128], t1F[:, 0:128], P3[:, 0:128]))
                sp.v(lambda: nc.vector.tensor_add(t1F[:, 0:128], t1F[:, 0:128], tsbF[:, 0:128]))
                sp.ld(tsbF[:, 128:256], HBs0[k * 128:(k + 1) * 128, lk * 128:lk * 128 + 128])
                sp.v(lambda: nc.vector.tensor_mul(t1F[:, 0:128], t1F[:, 0:128], tsbF[:, 128:256]))
                sp.ld(tsbF[:, 256:384], Xin[k * 128:(k + 1) * 128, lk * 128:lk * 128 + 128])
                sp.v(lambda c0=c0: nc.vector.tensor_add(x1sbF[:, c0:c0 + 128], t1F[:, 0:128], tsbF[:, 256:384]))
        for k in range(DT_):
            sp.st(HBx1[k * 128:(k + 1) * 128, :], x1sbF[:, k * NTOK:(k + 1) * NTOK])
        ln_stats(x1sbF, NTOK, DT_, S1f(), S2f(), sqF)
        sp.v(lambda: nc.vector.tensor_scalar_mul(MUf(), S1f(), dinv))
        sp.v(lambda: nc.vector.tensor_mul(S1f(), MUf(), MUf()))
        sp.v(lambda: nc.vector.tensor_scalar_mul(S2f(), S2f(), dinv))
        sp.v(lambda: nc.vector.tensor_sub(S2f(), S2f(), S1f()))
        sp.s(lambda: nc.scalar.activation(out=RSTDf(), in_=S2f(), func=AF.Sqrt, bias=csb[0:1, 0:1]))
        sp.v(lambda: nc.vector.reciprocal(RSTDf(), RSTDf()))
        sp.v(lambda: nc.vector.tensor_mul(NMRf(), MUf(), RSTDf()))
        sp.v(lambda: nc.vector.tensor_scalar_mul(NMRf(), NMRf(), -1.0))
        for ch in range(NTOK // NCH):
            bcast_row(P1, RSTDf(), ch * NCH, NCH)
            bcast_row(P2, NMRf(), ch * NCH, NCH)
            for k in range(DT_):
                c0 = k * NTOK + ch * NCH
                sp.v(lambda c0=c0: nc.vector.tensor_mul(t1F[:, 0:NCH], x1sbF[:, c0:c0 + NCH], P1[:, 0:NCH]))
                sp.v(lambda c0=c0: nc.vector.tensor_add(t1F[:, 0:NCH], t1F[:, 0:NCH], P2[:, 0:NCH]))
                sp.st(HBxn1[k * 128:(k + 1) * 128, ch * NCH:(ch + 1) * NCH], t1F[:, 0:NCH])
                if ch == 0:
                    sp.st(KEYI[k * 128:(k + 1) * 128, 0:128], t1F[:, 0:128])
                else:
                    sp.st(KEYI[k * 128:(k + 1) * 128, 128:384], t1F[:, 256:512])
        sp.g(lambda: nc.gpsimd.collective_compute(
            "AllGather", ALU.bypass, replica_groups=G8, ins=[KEYI], outs=[KEYO]))

        AR.reset()
        NH1 = 16
        xasb = AR.al(DT_ * 128)
        xasb2 = AR.al(DT_ * 128)
        kvsb = AR.al(DT_ * 128)
        qsb1 = AR.al(DT_ * NCH)
        xnsb1 = AR.al(DT_ * NCH)
        avsb = AR.al(DT_ * NCH)
        wtile1 = AR.al(DT_ * 128)
        ktile = AR.al(128)
        vtt = AR.al(128)
        trig1 = AR.al(128)
        tmq1 = AR.al(128)
        tmq21 = AR.al(128)
        gsc1 = AR.al(128)
        expb1 = AR.al(NCH)
        cneg1 = AR.al(NCH)
        sq1 = AR.al(NCH)
        angs1 = AR.al(3 * 64)
        smv1 = AR.al(4 * DT_ + 8 + 12 + 12 + 1)
        vrow1 = AR.al(3 * NTOK + 64)
        kmr = AR.al(KPAD)
        CSg = lambda: vrow1[0:1, 0:NTOK]
        QNg = lambda: vrow1[0:1, NTOK:2 * NTOK]
        AUXg = lambda: vrow1[0:1, 2 * NTOK:3 * NTOK]
        KNMg = lambda: vrow1[0:1, 3 * NTOK:3 * NTOK + 64]
        OBQ, OBK, OBV, OBO = 0, DT_, 2 * DT_, 3 * DT_
        OQP, OKP, OKM = 4 * DT_, 4 * DT_ + 8, 4 * DT_ + 8 + 12
        OSM = 4 * DT_ + 8 + 12 + 12
        ifr1 = VR[:, 1024:1088]
        sp.ld(smv1[:, OBQ:OBQ + DT_], Vbq1[:])
        sp.ld(smv1[:, OBK:OBK + DT_], Vbk1[:])
        sp.ld(smv1[:, OBV:OBV + DT_], Vbv1[:])
        sp.ld(smv1[:, OBO:OBO + DT_], Vbo1[:])
        sp.ld(smv1[:, OQP:OQP + 8], Iqp[:])
        sp.ld(smv1[:, OKP:OKP + 12], Ikp[:])
        sp.ld(smv1[:, OKM:OKM + 12], Ikm[:])
        sp.ld(smv1[:, OSM:OSM + 1], Ismask[:])
        sp.ld(kmr[0:1, 0:KPAD], Ikmr[:])
        sp.g(lambda: nc.gpsimd.iota(ifriI[:], pattern=[[1, 64]], base=0, channel_multiplier=0))
        sp.v(lambda: nc.vector.tensor_copy(ifr1[:], ifriI[:]))
        sp.s(lambda: nc.scalar.activation(out=ifr1[:], in_=ifr1[:], func=AF.Exp, scale=csb[0:1, 4:5]))
        sp.v(lambda: nc.vector.memset(KNMg()[:, 0:64], 0.0))

        def rope_trig1(pos_col):
            sp.t(lambda: nc.tensor.matmul(P3[:, 0:64], ones1[:, 0:128], ifr1[:], start=True, stop=True))
            A_ = angs1[:, 0:64]; R_ = angs1[:, 64:128]; W_ = angs1[:, 128:192]
            sp.v(lambda pos_col=pos_col: nc.vector.tensor_scalar(A_, P3[:, 0:64], pos_col, None, ALU.mult))
            for (dst, phase) in ((64, 0.0), (0, np.pi / 2)):
                sp.v(lambda phase=phase: nc.vector.tensor_scalar(W_, A_, 1.0 / ROPE_C1, phase / ROPE_C1, ALU.mult, ALU.add))
                sp.v(lambda: nc.vector.tensor_copy(nintI[:], W_))
                sp.v(lambda: nc.vector.tensor_copy(W_, nintI[:]))
                sp.v(lambda phase=phase: nc.vector.tensor_scalar(R_, A_, 1.0, phase, ALU.mult, ALU.add))
                sp.v(lambda: nc.vector.scalar_tensor_tensor(R_, W_, -ROPE_C1, R_, ALU.mult, ALU.add))
                sp.v(lambda: nc.vector.scalar_tensor_tensor(R_, W_, -ROPE_C2, R_, ALU.mult, ALU.add))
                sp.s(lambda dst=dst: nc.scalar.activation(out=trig1[:, dst:dst + 64], in_=R_, func=AF.Sin))

        def rope_tile1(fm_ap):
            sp.t(lambda fm_ap=fm_ap: nc.tensor.transpose(P4[:, 0:128], fm_ap, idf[:]))
            sp.v(lambda: nc.vector.tensor_copy(tmq1[:], P4[:, 0:128]))
            C_ = trig1[:, 0:64]; Sn = trig1[:, 64:128]
            sp.v(lambda: nc.vector.tensor_mul(tmq21[:, 0:64], tmq1[:, 0:64], C_))
            sp.v(lambda: nc.vector.tensor_mul(gsc1[:, 0:64], tmq1[:, 64:128], Sn))
            sp.v(lambda: nc.vector.tensor_sub(tmq21[:, 0:64], tmq21[:, 0:64], gsc1[:, 0:64]))
            sp.v(lambda: nc.vector.tensor_mul(tmq21[:, 64:128], tmq1[:, 64:128], C_))
            sp.v(lambda: nc.vector.tensor_mul(gsc1[:, 0:64], tmq1[:, 0:64], Sn))
            sp.v(lambda: nc.vector.tensor_add(tmq21[:, 64:128], tmq21[:, 64:128], gsc1[:, 0:64]))
            sp.t(lambda: nc.tensor.transpose(P4[:, 0:128], tmq21[:], idf[:]))
            sp.v(lambda fm_ap=fm_ap: nc.vector.tensor_copy(fm_ap, P4[:, 0:128]))

        for kt in range(12):
            s0_ = kt // 3
            mblk = kt % 3
            for p0 in range(0, DT_, 8):
                sp.ld(xasb[:, p0 * 128:(p0 + 8) * 128],
                      KEYO[s0_ * D + p0 * 128:s0_ * D + (p0 + 8) * 128, mblk * 128:(mblk + 1) * 128]
                      .rearrange("(k p) j -> p k j", p=128))
                sp.ld(xasb2[:, p0 * 128:(p0 + 8) * 128],
                      KEYO[(4 + s0_) * D + p0 * 128:(4 + s0_) * D + (p0 + 8) * 128, mblk * 128:(mblk + 1) * 128]
                      .rearrange("(k p) j -> p k j", p=128))
            sp.v(lambda: nc.vector.tensor_sub(xasb2[:], xasb2[:], xasb[:]))
            sp.v(lambda: nc.vector.scalar_tensor_tensor(xasb[:], xasb2[:], smv1[:, OSM:OSM + 1], xasb[:], ALU.mult, ALU.add))
            rope_trig1(smv1[:, OKP + kt:OKP + kt + 1])
            for (Wt, bof, isk) in ((Wk1, OBK, True), (Wv1, OBV, False)):
                for m in range(DT_):
                    band_ld(sp, nc, wtile1, Wt, m, DT_, parts=2)
                    for k in range(DT_):
                        sp.t(lambda k=k, st=(k == 0), sp_=(k == DT_ - 1):
                             nc.tensor.matmul(P1[:, 0:128], wtile1[:, k * 128:(k + 1) * 128],
                                              xasb[:, k * 128:(k + 1) * 128], start=st, stop=sp_))
                    sp.v(lambda m=m, bof=bof: nc.vector.tensor_scalar_add(
                        kvsb[:, m * 128:(m + 1) * 128], P1[:, 0:128], smv1[:, bof + m:bof + m + 1]))
                if isk:
                    for h in range(NH1):
                        rope_tile1(kvsb[:, h * 128:(h + 1) * 128])
                        sp.s(lambda h=h: nc.scalar.activation(out=sq1[:, 0:128], in_=kvsb[:, h * 128:(h + 1) * 128], func=AF.Square))
                        sp.t(lambda: nc.tensor.matmul(PR2[0:1, 0:128], ones128[:, 0:1], sq1[:, 0:128], start=True, stop=True))
                        sp.v(lambda kt=kt: nc.vector.tensor_mul(sq1[0:1, 0:128], PR2[0:1, 0:128], kmr[0:1, kt * 128:(kt + 1) * 128]))
                        sp.v(lambda h=h: nc.vector.reduce_max(AUXg()[:, h:h + 1], sq1[0:1, 0:128], axis=AX.X))
                        sp.v(lambda h=h: nc.vector.tensor_max(KNMg()[:, h:h + 1], KNMg()[:, h:h + 1], AUXg()[:, h:h + 1]))
                    for k in range(DT_):
                        sp.st(Kscr[k * 128:(k + 1) * 128, kt * 128:(kt + 1) * 128], kvsb[:, k * 128:(k + 1) * 128])
                else:
                    for h in range(NH1):
                        sp.t(lambda h=h: nc.tensor.transpose(P4[:, 0:128], kvsb[:, h * 128:(h + 1) * 128], idf[:]))
                        sp.v(lambda: nc.vector.tensor_copy(vtt[:], P4[:, 0:128]))
                        sp.st(Vscr[kt * 128:(kt + 1) * 128, h * 128:(h + 1) * 128], vtt[:])
        sp.s(lambda: nc.scalar.activation(out=KNMg()[:, 0:16], in_=KNMg()[:, 0:16], func=AF.Sqrt))
        for ch in range(NTOK // NCH):
            t0 = ch * NCH
            for k in range(DT_):
                sp.ld(xnsb1[:, k * NCH:(k + 1) * NCH], HBxn1[k * 128:(k + 1) * 128, t0:t0 + NCH])
            for m in range(DT_):
                band_ld(sp, nc, wtile1, Wq1, m, DT_, parts=2)
                for k in range(DT_):
                    sp.t(lambda k=k, st=(k == 0), sp_=(k == DT_ - 1):
                         nc.tensor.matmul(P0[:, 0:NCH], wtile1[:, k * 128:(k + 1) * 128],
                                          xnsb1[:, k * NCH:(k + 1) * NCH], start=st, stop=sp_))
                sp.v(lambda m=m: nc.vector.tensor_scalar_add(qsb1[:, m * NCH:(m + 1) * NCH], P0[:, 0:NCH], smv1[:, OBQ + m:OBQ + m + 1]))
            for sti in range(NCH // 128):
                rope_trig1(smv1[:, OQP + ch * (NCH // 128) + sti:OQP + ch * (NCH // 128) + sti + 1])
                for h in range(NH1):
                    rope_tile1(qsb1[:, h * NCH + sti * 128:h * NCH + sti * 128 + 128])
            for h in range(NH1):
                sp.s(lambda h=h: nc.scalar.activation(out=sq1[:, 0:NCH], in_=qsb1[:, h * NCH:(h + 1) * NCH], func=AF.Square))
                sp.t(lambda: nc.tensor.matmul(PR2[0:1, 0:NCH], ones128[:, 0:1], sq1[:, 0:NCH], start=True, stop=True))
                sp.s(lambda: nc.scalar.activation(out=QNg()[:, 0:NCH], in_=PR2[0:1, 0:NCH], func=AF.Sqrt))
                sp.v(lambda h=h: nc.vector.tensor_scalar(QNg()[:, 0:NCH], QNg()[:, 0:NCH], KNMg()[0:1, h:h + 1], None, ALU.mult))
                sp.v(lambda: nc.vector.tensor_scalar_mul(QNg()[:, 0:NCH], QNg()[:, 0:NCH], -1.0 / np.sqrt(128.0)))
                sp.t(lambda: nc.tensor.matmul(P3[:, 0:NCH], ones1[:, 0:128], QNg()[:, 0:NCH], start=True, stop=True))
                sp.v(lambda: nc.vector.tensor_copy(cneg1[:, 0:NCH], P3[:, 0:NCH]))
                for kt in range(12):
                    sp.ld(ktile[:], Kscr[h * 128:(h + 1) * 128, kt * 128:(kt + 1) * 128])
                    sp.ld(vtt[:], Vscr[kt * 128:(kt + 1) * 128, h * 128:(h + 1) * 128])
                    sp.t(lambda h=h: nc.tensor.matmul(P1[:, 0:NCH], ktile[:], qsb1[:, h * NCH:(h + 1) * NCH], start=True, stop=True))
                    sp.v(lambda: nc.vector.scalar_tensor_tensor(expb1[:, 0:NCH], P1[:, 0:NCH], 1.0 / np.sqrt(128.0), cneg1[:, 0:NCH], ALU.mult, ALU.add))
                    sp.s(lambda: nc.scalar.activation(out=expb1[:, 0:NCH], in_=expb1[:, 0:NCH], func=AF.Exp))
                    sp.v(lambda kt=kt: nc.vector.tensor_scalar(expb1[:, 0:NCH], expb1[:, 0:NCH], smv1[:, OKM + kt:OKM + kt + 1], None, ALU.mult))
                    sp.t(lambda kt=kt: nc.tensor.matmul(PR2[0:1, 0:NCH], ones128[:, 0:1], expb1[:, 0:NCH], start=(kt == 0), stop=(kt == 11)))
                    sp.t(lambda kt=kt: nc.tensor.matmul(P2[:, 0:NCH], vtt[:], expb1[:, 0:NCH], start=(kt == 0), stop=(kt == 11)))
                sp.v(lambda: nc.vector.tensor_copy(CSg()[:, 0:NCH], PR2[0:1, 0:NCH]))
                sp.v(lambda: nc.vector.reciprocal(CSg()[:, 0:NCH], CSg()[:, 0:NCH]))
                sp.t(lambda: nc.tensor.matmul(P3[:, 0:NCH], ones1[:, 0:128], CSg()[:, 0:NCH], start=True, stop=True))
                sp.v(lambda: nc.vector.tensor_copy(cneg1[:, 0:NCH], P3[:, 0:NCH]))
                sp.v(lambda h=h: nc.vector.tensor_mul(avsb[:, h * NCH:(h + 1) * NCH], P2[:, 0:NCH], cneg1[:, 0:NCH]))
            for m in range(DT_):
                band_ld(sp, nc, wtile1, Wo1, m, DT_, parts=2)
                for k in range(DT_):
                    sp.t(lambda k=k, st=(k == 0), sp_=(k == DT_ - 1):
                         nc.tensor.matmul(P0[:, 0:NCH], wtile1[:, k * 128:(k + 1) * 128],
                                          avsb[:, k * NCH:(k + 1) * NCH], start=st, stop=sp_))
                sp.v(lambda m=m: nc.vector.tensor_scalar_add(expb1[:, 0:NCH], P0[:, 0:NCH], smv1[:, OBO + m:OBO + m + 1]))
                sp.st(HBs1[m * 128:(m + 1) * 128, t0:t0 + NCH], expb1[:, 0:NCH])

        AR.reset()
        TCG = 512
        xnsb2 = AR.al(DT_ * TCG)
        midb2 = AR.al(NMID * TCG)
        wtf = AR.al(DT_ * 128)
        gsc2 = AR.al(TCG)
        osc2 = AR.al(TCG)
        srow = VR[:, 0:128]

        def deq(W_, Sc_, m, k0, nk):
            sp.ld(srow, Sc_[0:1, m * 128:(m + 1) * 128])
            sp.t(lambda: nc.tensor.matmul(P3[:, 0:128], ones1[:, 0:128], srow, start=True, stop=True))
            hstep = min(nk, 8)
            for p0 in range(0, nk, hstep):
                n_ = min(hstep, nk - p0)
                sp.ld(wti[:, 0:n_ * 128],
                      W_[(k0 + p0) * 128:(k0 + p0 + n_) * 128, m * 128:(m + 1) * 128]
                      .rearrange("(k p) j -> p k j", p=128))
                sp.v(lambda n_=n_, p0=p0: nc.vector.tensor_copy(wtf[:, p0 * 128:(p0 + n_) * 128], wti[:, 0:n_ * 128]))
            sp.v(lambda nk=nk: nc.vector.tensor_scalar_add(wtf[:, 0:nk * 128], wtf[:, 0:nk * 128], -128.0))
            sp.v(lambda nk=nk: nc.vector.tensor_mul(
                wtf[:, 0:nk * 128].rearrange("p (k j) -> p k j", j=128),
                wtf[:, 0:nk * 128].rearrange("p (k j) -> p k j", j=128),
                P3[:, 0:128].unsqueeze(1).broadcast_to([128, nk, 128])))

        for ch in range(NTOK // TCG):
            t0 = ch * TCG
            for k in range(DT_):
                sp.ld(xnsb2[:, k * TCG:(k + 1) * TCG], HBxn1[k * 128:(k + 1) * 128, t0:t0 + TCG])
            for m in range(NMID):
                deq(Wqg, Sg, m, 0, DT_)
                for k in range(DT_):
                    sp.t(lambda k=k, st=(k == 0), sp_=(k == DT_ - 1):
                         nc.tensor.matmul(P0[:, 0:TCG], wtf[:, k * 128:(k + 1) * 128],
                                          xnsb2[:, k * TCG:(k + 1) * TCG], start=st, stop=sp_))
                sp.s(lambda: nc.scalar.activation(out=gsc2[:], in_=P0[:, 0:TCG], func=AF.Silu))
                deq(Wqu, Su, m, 0, DT_)
                for k in range(DT_):
                    sp.t(lambda k=k, st=(k == 0), sp_=(k == DT_ - 1):
                         nc.tensor.matmul(P0[:, 0:TCG], wtf[:, k * 128:(k + 1) * 128],
                                          xnsb2[:, k * TCG:(k + 1) * TCG], start=st, stop=sp_))
                sp.v(lambda m=m: nc.vector.tensor_mul(midb2[:, m * TCG:(m + 1) * TCG], gsc2[:], P0[:, 0:TCG]))
            for mo in range(DT_):
                for qtr in range(4):
                    k0 = qtr * 11
                    deq(Wqd, Sd, mo, k0, 11)
                    for k in range(11):
                        kk = k0 + k
                        sp.t(lambda k=k, kk=kk, st=(kk == 0), sp_=(kk == NMID - 1):
                             nc.tensor.matmul(P0[:, 0:TCG], wtf[:, k * 128:(k + 1) * 128],
                                              midb2[:, kk * TCG:(kk + 1) * TCG], start=st, stop=sp_))
                sp.s(lambda: nc.scalar.activation(out=gsc2[:], in_=P0[:, 0:TCG], func=AF.Silu))
                sp.ld(osc2[:], HBs1[mo * 128:(mo + 1) * 128, t0:t0 + TCG])
                sp.v(lambda: nc.vector.tensor_mul(gsc2[:], gsc2[:], osc2[:]))
                sp.ld(osc2[:], HBx1[mo * 128:(mo + 1) * 128, t0:t0 + TCG])
                sp.v(lambda: nc.vector.tensor_add(gsc2[:], gsc2[:], osc2[:]))
                sp.st(Oout[mo * 128:(mo + 1) * 128, t0:t0 + TCG], gsc2[:])

        sp.build(sem, sem2)
    return nc



def tok_rows(c):
    a, b = c % 4, c // 4
    rows = []
    for n in range(8):
        t0 = 128 * (a + 4 * n)
        rows.append(np.arange(b * T + t0, b * T + t0 + 128))
    return np.concatenate(rows)


def rope_perm():
    p = np.zeros(D, np.int64)
    for h in range(16):
        p[h * 128:h * 128 + 64] = h * 128 + np.arange(0, 128, 2)
        p[h * 128 + 64:(h + 1) * 128] = h * 128 + np.arange(1, 128, 2)
    return p


def prep_inputs(inputs):
    x = np.asarray(inputs["L0_attn_hidden_states"], np.float32).reshape(B * T, D)
    w_in = fm(inputs["L0_attn_in_proj_w"])
    w_dt = fm(inputs["L0_attn_dt_proj_w"])
    b_dt = colv(inputs["L0_attn_dt_proj_b"])
    g1 = colv(inputs["L0_attn_norm1_w"])
    wr = colv(inputs["L0_ffn_w_router"])
    alog = np.asarray(inputs["L0_attn_A_log"], np.float32)
    dpar = np.asarray(inputs["L0_attn_D_param"], np.float32)
    w_out = fm(inputs["L0_attn_out_proj_w"])
    w_g = fm(inputs["L0_attn_gate_w"])
    w_u = fm(inputs["L0_attn_up_w"])
    w_d = fm(inputs["L0_attn_down_w"])
    g2 = colv(inputs["L0_attn_norm2_w"])
    tt = np.arange(T)
    pos_all = np.stack([tt, tt], 0).astype(np.float32)
    own = (tt // 128) % 4
    lrow = 128 * ((tt // 128) // 4) + tt % 128
    grow_all = np.stack([own * 1024 + lrow, (4 + own) * 1024 + lrow], 0).astype(np.float32)
    def tm64(arr2):
        return np.ascontiguousarray(arr2.reshape(2, 32, 128).transpose(2, 0, 1).reshape(128, 64))
    postm = tm64(pos_all)
    growtm = tm64(grow_all)
    iotar = np.arange(NTOK, dtype=np.float32).reshape(1, NTOK)
    wqE = np.ascontiguousarray(np.asarray(inputs["L0_ffn_wq"], np.float32))
    wkE = np.ascontiguousarray(np.asarray(inputs["L0_ffn_wk"], np.float32))
    wvE = np.ascontiguousarray(np.asarray(inputs["L0_ffn_wv"], np.float32))
    woE = np.ascontiguousarray(np.asarray(inputs["L0_ffn_wo"], np.float32))
    wgE = np.ascontiguousarray(np.asarray(inputs["L0_ffn_w_gate"], np.float32))
    wuE = np.ascontiguousarray(np.asarray(inputs["L0_ffn_w_up"], np.float32))
    wdE = np.ascontiguousarray(np.asarray(inputs["L0_ffn_w_down"], np.float32))
    ln1 = colv(inputs["L0_ffn_ln1_w"]); ln2 = colv(inputs["L0_ffn_ln2_w"])
    bqE = colv(inputs["L0_ffn_bq"]); bkE = colv(inputs["L0_ffn_bk"], 8); bvE = colv(inputs["L0_ffn_bv"], 8)
    perm = rope_perm()
    wqt = fm(inputs["L1_attn_Wq"])[:, perm]
    wkt = fm(inputs["L1_attn_Wk"])[:, perm]
    wvt = fm(inputs["L1_attn_Wv"])
    wot = fm(inputs["L1_attn_Wo"])
    bq1 = colv(np.asarray(inputs["L1_attn_bq"], np.float32)[perm])
    bk1 = colv(np.asarray(inputs["L1_attn_bk"], np.float32)[perm])
    bv1 = colv(inputs["L1_attn_bv"])
    bo1 = colv(inputs["L1_attn_bo"])
    kpos = np.zeros(KPAD, np.float32)
    kmask = np.zeros(KPAD, np.float32)
    for ap_ in range(4):
        for mblk, gb in enumerate((ap_, 24 + ap_, 28 + ap_)):
            j0 = 384 * ap_ + 128 * mblk
            kpos[j0:j0 + 128] = 128 * gb + np.arange(128)
            if ap_ == 3 and mblk == 0:
                kmask[j0:j0 + 26] = 1.0
            else:
                kmask[j0:j0 + 128] = 1.0
    kposc = np.ascontiguousarray(kpos.reshape(12, 128).T)
    kmaskc = np.ascontiguousarray(kmask.reshape(12, 128).T)
    kmaskr = np.ascontiguousarray(kmask.reshape(1, KPAD))
    wqg = np.ascontiguousarray(np.asarray(inputs["L1_ffn_wq_gate"]).T.astype(np.int32))
    wqu = np.ascontiguousarray(np.asarray(inputs["L1_ffn_wq_up"]).T.astype(np.int32))
    wqd = np.ascontiguousarray(np.asarray(inputs["L1_ffn_wq_down"]).T.astype(np.int32))
    sg = np.asarray(inputs["L1_ffn_s_gate"], np.float32).reshape(1, 5632)
    su = np.asarray(inputs["L1_ffn_s_up"], np.float32).reshape(1, 5632)
    sd = np.asarray(inputs["L1_ffn_s_down"], np.float32).reshape(1, D)

    maps = []
    for c in range(N_CORES):
        a, b = c % 4, c // 4
        rows = tok_rows(c)
        alog_sl = np.ascontiguousarray(
            alog[256 * c:256 * c + 256].reshape(2, 128, 32).transpose(1, 0, 2).reshape(128, 64))
        dp_sl = colv(dpar[256 * c:256 * c + 256], 2)
        cofs = np.full((128, 1), -1024.0 * c, np.float32)
        onehot = np.zeros((1, 8), np.float32); onehot[0, c] = 1.0
        myslot = (128 * c + np.arange(128)).astype(np.int32).reshape(128, 1)
        myrows = (b * 32 + a + 4 * np.arange(8)).astype(np.int32).reshape(8, 1)
        q0 = 128 * c
        kl = np.arange(128)[:, None]
        kt_ql = np.arange(8 * 128)[None, :]
        kglob = (kt_ql // 128) * 128 + kl
        qglob = q0 + (kt_ql % 128)
        cm = (kglob <= qglob).astype(np.float32)
        qpos = np.ascontiguousarray(
            (128.0 * (a + 4 * np.arange(8))[None, :] + np.arange(128)[:, None]).astype(np.float32))
        smask = np.full((128, 1), float(b), np.float32)
        maps.append({
            "x": fm(x[rows]), "w_in": w_in, "w_dt": w_dt, "b_dt": b_dt, "g1": g1, "wr": wr,
            "alog_sl": alog_sl, "dp_sl": dp_sl,
            "w_out": w_out, "w_gate": w_g, "w_up": w_u, "w_down": w_d, "g2": g2,
            "postm": postm, "growtm": growtm, "iotar": iotar, "cofs": cofs,
            "onehot": onehot, "myslot": myslot, "myrows": myrows,
            "wq": wqE, "wk": wkE, "wv": wvE, "wo": woE, "wg": wgE, "wu": wuE, "wd": wdE,
            "ln1": ln1, "ln2": ln2, "bqE": bqE, "bkE": bkE, "bvE": bvE, "cmask": cm,
            "wqt": wqt, "wkt": wkt, "wvt": wvt, "wot": wot,
            "bq1": bq1, "bk1": bk1, "bv1": bv1, "bo1": bo1,
            "qpos": qpos, "kpos": kposc, "kmask": kmaskc, "kmaskr": kmaskr, "smask": smask,
            "wqg": wqg, "wqu": wqu, "wqd": wqd, "sg": sg, "su": su, "sd": sd,
        })
    return maps


def kernel(**inputs):
    nc = build_fused()
    maps = prep_inputs(inputs)
    res = run_bass_kernel_spmd(nc, maps, core_ids=list(range(N_CORES))).results
    out = np.empty((B * T, D), np.float32)
    for c in range(N_CORES):
        o = np.ascontiguousarray(res[c]["out"].T)
        out[tok_rows(c)] = o
    return np.ascontiguousarray(out.reshape(B, T, D))
